# version 26
# baseline (speedup 1.0000x reference)
import numpy as np
from contextlib import ExitStack

import concourse.bass as bass
import concourse.mybir as mybir
from concourse.bass_utils import run_bass_kernel_spmd

F32 = mybir.dt.float32
F32R = mybir.dt.float32r
BF16 = mybir.dt.bfloat16
ALU = mybir.AluOpType
AF = mybir.ActivationFunctionType
AX = mybir.AxisListType

D = 1024
BLK = 128
CH = 256
EPS = 1e-5
NEG = -1e30
HALO = 256


class Sem:
    def __init__(self, h):
        self.h = h
        self.n = 0

    def bump(self, k=1):
        self.n += k
        return self.n


PF_ON = True


def build_program(T_real=4096, ST_real=512, trunc=None):
    assert T_real % ST_real == 0 and ST_real % CH == 0
    n_st = T_real // ST_real
    n_scr = HALO + T_real
    STW = ST_real + HALO

    nc = bass.Bass("TRN2", target_bir_lowering=False)

    din = lambda name, shape, dt=F32: nc.dram_tensor(name, shape, dt, kind="ExternalInput").ap()
    x_main = din("x_main", [T_real, D])
    x_halo = din("x_halo", [HALO + 1, D])
    xrow1 = din("xrow1", [1, D])
    brow_a = din("brow_a", [1, D])
    wt_d = {k: din("wt_" + k, [8, 128, 1024]) for k in ["r", "kf", "vf", "q", "k", "v", "o"]}
    wta_d = din("wta", [8, 128, 1024])
    mufm_r_d = din("mufm_r", [128, 8])
    mufm_k_d = din("mufm_k", [128, 8])
    bfm_d = {k: din("bfm_" + k, [128, 8]) for k in ["q", "k", "v", "o"]}
    ident_d = din("ident", [128, 128])
    mask_d = {k: din("mask_" + k, [128, 256], BF16) for k in ["first", "band", "blk2"]}
    gate_d = din("gate", [128, 8])
    cos_d = din("cos_t", [128, n_scr])
    sin_d = din("sin_t", [128, n_scr])

    y_out = nc.dram_tensor("y", [T_real, D], F32, kind="ExternalOutput").ap()
    wt_hl = nc.dram_tensor("wt_hl", [7, 8, 128, 2048], F32R, kind="Internal").ap()
    x2s = nc.dram_tensor("x2s", [n_scr, D], F32, kind="Internal").ap()
    x3s = nc.dram_tensor("x3s", [n_scr, D], F32, kind="Internal").ap()

    TE, VE, SE, GE, SY = nc.tensor, nc.vector, nc.scalar, nc.gpsimd, nc.sync

    with ExitStack() as ectx:
        sb = lambda name, shape, dt=F32: ectx.enter_context(nc.sbuf_tensor(name, shape, dt))
        psal = lambda name, shape: ectx.enter_context(nc.psum_tensor(name, shape, F32))

        AR0 = sb("AR0", [128, 8 * STW])
        AR2 = sb("AR2", [128, 8 * STW])
        ARL = sb("ARL", [128, 8 * (STW + 128)])
        SPH = sb("SPH", [128, 8 * STW], F32R)
        SPL = sb("SPL", [128, 8 * STW], F32R)
        wsl = sb("wsl", [128, 2 * 1024])
        wsl2 = sb("wsl2", [128, 3 * 2048], F32R)
        xt = [sb("xt0", [128, 2048]), sb("xt1", [128, 2048])]
        st_ = [sb("st0", [128, 2048]), sb("st1", [128, 2048])]
        ctm = sb("ctm", [128, 1024])
        vcarry = sb("vcarry", [128, 1024])
        kcarry = sb("kcarry", [128, 1024])
        y1b = sb("y1b", [128, 1024])
        masks = {k: sb("m_" + k, [128, 256], BF16) for k in ["first", "band", "blk2"]}
        ident = sb("identt", [128, 128])
        gate = sb("gatet", [128, 8])
        mur = sb("mur", [128, 8])
        muk = sb("muk", [128, 8])
        bias_fm = {k: sb("b_" + k, [128, 8]) for k in ["q", "k", "v", "o"]}
        lncol2 = sb("lncol2", [128, 8])
        lncol4 = sb("lncol4", [128, 8])
        l1fm = sb("l1fm", [128, 8])
        stats = sb("statst", [128, 48])
        mv = sb("mvt", [128, 8])
        rstd = sb("rstdt", [128, 4])
        nbias = sb("nbiast", [128, 4])
        nmax = sb("nmaxt", [128, 16])
        rsum = sb("rsumt", [128, 16])
        recip = sb("recipt", [128, 16])
        rowa = ctm[0:1, 0:1024]
        rowb = AR0[0:1, 0:1024]
        rowc = AR2[0:1, 0:1024]
        rowd = st_[0][0:1, 0:1024]
        one1 = sb("one1", [1, 128])
        cosT = wsl[:, 0:STW]
        sinT = wsl[:, STW:2 * STW]
        stat1 = sb("stat1", [1, 12])
        mv1 = sb("mv1", [1, 2])
        rst1 = sb("rst1", [1, 2])

        psA = psal("psA", [128, 1024])
        psB = psal("psB", [128, 1024])
        psC = psal("psC", [128, 1024])
        pb = [psA[:, 0:512], psA[:, 512:1024], psB[:, 0:512], psB[:, 512:1024],
              psC[:, 0:512], psC[:, 512:1024]]
        pc = psal("pc", [128, 1024])
        ps_of = {0: psA, 1: psA, 2: psB, 3: psB, 4: psC, 5: psC}

        GD = Sem(ectx.enter_context(nc.semaphore("GD")))
        SV = Sem(ectx.enter_context(nc.semaphore("SV")))
        SS = Sem(ectx.enter_context(nc.semaphore("SS")))
        TT = Sem(ectx.enter_context(nc.semaphore("TT")))
        SG = Sem(ectx.enter_context(nc.semaphore("SG")))
        SM = Sem(ectx.enter_context(nc.semaphore("SM")))
        W0 = Sem(ectx.enter_context(nc.semaphore("W0")))
        W1 = Sem(ectx.enter_context(nc.semaphore("W1")))
        X0 = Sem(ectx.enter_context(nc.semaphore("X0")))
        X1 = Sem(ectx.enter_context(nc.semaphore("X1")))
        TB = Sem(ectx.enter_context(nc.semaphore("TB")))
        WSEM = [W0, W1]
        XSEM = [X0, X1]
        GDS = [Sem(ectx.enter_context(nc.semaphore(f"GDS{i}"))) for i in range(4)]
        WSEM2 = [Sem(ectx.enter_context(nc.semaphore(f"WS2{i}"))) for i in range(3)]

        v8 = lambda buf, w: buf[:].rearrange("p (f c) -> p f c", f=8)
        a0, a2 = v8(AR0, STW), v8(AR2, STW)
        aL = v8(ARL, STW + 128)
        sph = v8(SPH, STW)
        spl = v8(SPL, STW)
        sphf = SPH[:].bitcast(F32).rearrange("p (f c) -> p f c", f=8)
        vcv = v8(vcarry, 128)
        kcv = v8(kcarry, 128)
        wslot = [wsl[:, 0:1024], wsl[:, 1024:2048]]
        wslot2 = [wsl2[:, 0:2048], wsl2[:, 2048:4096], wsl2[:, 4096:6144]]
        WIDX = {"r": 0, "kf": 1, "vf": 2, "q": 3, "k": 4, "v": 5, "o": 6}

        guards = {}

        def g_wait(eng, *keys):
            for key in keys:
                for s, n in guards.pop(key, []):
                    eng.wait_ge(s.h, n)

        def g_set(key, sem, n=None):
            guards.setdefault(key, []).append((sem, n if n is not None else sem.n))

        def sy_dma(out, in_, sem):
            SY.dma_start(out=out, in_=in_).then_inc(sem.h, 16)
            return sem.bump(16)

        def ge_dma(out, in_):
            GE.dma_start(out=out, in_=in_).then_inc(GD.h, 16)
            return GD.bump(16)

        def ge_fence():
            GE.wait_ge(GD.h, GD.n)
            GE.sem_inc(SG.h, 1)
            return SG.bump()

        def bc(ap, reps):
            return ap.unsqueeze(1).broadcast_to([ap.shape[0], reps, ap.shape[1]])

        def scr_rows(t, r0):
            return t[r0:r0 + 256, :].rearrange("(b p) d -> p b d", p=128)

        def tile_v(tile):
            return tile[:].rearrange("p (b d) -> p b d", b=2)

        sy_dma(rowa, xrow1, SM)
        wta_bufs = [wslot[0][:], wslot[1][:],
                    st_[1][:, 0:1024], st_[1][:, 1024:2048], st_[0][:, 1024:2048],
                    AR0[:, 1024:2048], AR0[:, 2048:3072], AR2[:, 1024:2048]]
        wta_sems = [W0, W1, X0, X1, TB, WSEM2[0], WSEM2[1], WSEM2[2]]
        wrd = {kk: sy_dma(wta_bufs[kk], wta_d[kk], wta_sems[kk]) for kk in range(2)}
        for dst, src in [(masks["first"][:], mask_d["first"]), (masks["band"][:], mask_d["band"]),
                         (masks["blk2"][:], mask_d["blk2"]), (ident[:], ident_d),
                         (gate[:], gate_d), (mur[:], mufm_r_d), (muk[:], mufm_k_d),
                         (bias_fm["q"][:], bfm_d["q"]), (bias_fm["k"][:], bfm_d["k"]),
                         (bias_fm["v"][:], bfm_d["v"]), (bias_fm["o"][:], bfm_d["o"]),
                         (rowb, brow_a), (rowc, x_halo[0:1, :])]:
            sy_dma(dst, src, SM)
        small_done = SM.n

        GE.memset(kcarry[:], 0.0)
        GE.memset(vcarry[:], 0.0)
        GE.memset(one1[:], 1.0).then_inc(SG.h, 1)
        SG.bump()

        def ln_row(src_row, dst_row):
            VE.wait_ge(SS.h, SS.n)
            VE.bn_stats(stat1[:, 0:6], src_row[0:1, 0:512])
            VE.bn_stats(stat1[:, 6:12], src_row[0:1, 512:1024])
            VE.drain()
            VE.bn_aggr(mv1[:], stat1[:])
            VE.drain()
            VE.tensor_scalar(rst1[:, 0:1], mv1[:, 1:2], EPS, None, ALU.add)
            VE.drain().then_inc(SV.h, 1)
            SV.bump()
            SE.wait_ge(SV.h, SV.n)
            SE.activation(rst1[:, 0:1], rst1[:, 0:1], AF.Ln)
            SE.drain()
            SE.activation(rst1[:, 0:1], rst1[:, 0:1], AF.Exp, scale=-0.5).then_inc(SS.h, 1)
            SS.bump()
            VE.wait_ge(SS.h, SS.n)
            VE.scalar_tensor_tensor(rst1[:, 1:2], mv1[:, 0:1], -1.0, rst1[:, 0:1],
                                    ALU.mult, ALU.mult).then_inc(SV.h, 1)
            SV.bump()
            SE.wait_ge(SV.h, SV.n)
            SE.activation(dst_row[0:1, :], src_row[0:1, :], AF.Identity,
                          bias=rst1[:, 1:2], scale=rst1[:, 0:1]).then_inc(SS.h, 1)
            SS.bump()

        def row_to_fm(src_row, dst_col):
            g_wait(TE, "pb5")
            TE.wait_ge(SS.h, SS.n)
            TE.wait_ge(SG.h, SG.n)
            last = None
            for fc in range(8):
                last = TE.transpose(pb[5][:, fc:fc + 1],
                                    src_row[0:1, fc * 128:(fc + 1) * 128], one1[0:1, 0:1])
            last.then_inc(TT.h, 1)
            TT.bump()
            VE.wait_ge(TT.h, TT.n)
            VE.tensor_copy(dst_col, pb[5][:, 0:8]).then_inc(SV.h, 1)
            SV.bump()
            g_set("pb5", SV)

        VE.wait_ge(SM.h, small_done)
        ln_row(rowa, rowd)
        row_to_fm(rowd, l1fm[:])
        for kk in range(2, 8):
            wrd[kk] = sy_dma(wta_bufs[kk], wta_d[kk], wta_sems[kk])
        for kk in range(8):
            TE.wait_ge(wta_sems[kk].h, wrd[kk])
            if kk == 0:
                TE.wait_ge(SV.h, SV.n)
                g_wait(TE, "pb0", "pb1")
            last = None
            for dh in range(2):
                last = TE.matmul(pb[dh][0:1, 0:512], l1fm[:, kk:kk + 1],
                                 wta_bufs[kk][:, dh * 512:(dh + 1) * 512],
                                 start=(kk == 0), stop=False)
            last.then_inc(TT.h, 1)
            TT.bump()
        last = None
        for dh in range(2):
            last = TE.matmul(pb[dh][0:1, 0:512], one1[0:1, 0:1],
                             rowb[0:1, dh * 512:(dh + 1) * 512], start=False, stop=True)
        last.then_inc(TT.h, 1)
        TT.bump()
        g_set("wsl0", TT)
        g_set("wsl1", TT)
        g_set("st1", TT)
        g_set("st0", TT)
        VE.wait_ge(TT.h, TT.n)
        VE.tensor_copy(rowd[0:1, 0:512], pb[0][0:1, 0:512])
        VE.tensor_copy(rowd[0:1, 512:1024], pb[1][0:1, 0:512])
        VE.drain().then_inc(SV.h, 1)
        SV.bump()
        g_set("pb0", SV)
        g_set("pb1", SV)
        g_wait(TE, "pc")
        TE.wait_ge(SV.h, SV.n)
        last = None
        for dh in range(2):
            last = TE.matmul(pc[:, dh * 512:(dh + 1) * 512], one1[0:1, :],
                             rowd[0:1, dh * 512:(dh + 1) * 512], start=True, stop=True)
        last.then_inc(TT.h, 1)
        TT.bump()
        VE.wait_ge(TT.h, TT.n)
        VE.tensor_copy(y1b[:], pc[:]).then_inc(SV.h, 1)
        SV.bump()
        g_set("pc", SV)
        y1b_ready = SV.n

        VE.tensor_tensor(rowa, rowc, rowd, ALU.add)
        VE.drain().then_inc(SV.h, 1)
        SV.bump()
        g_set("st0", SV)
        ln_row(rowa, rowc)
        row_to_fm(rowc, lncol2[:])
        g_set("AR2", TT)
        g_set("AR0", TT)

        wt_ready = {}
        _stgs = [(st_[0], "st0"), (st_[1], "st1"), (xt[0], "xt0"), (xt[1], "xt1")]
        _store_cnt = []
        _prep_it = [0]
        _prep_done = {}

        def prep_iter(wkey, m):
            wi = WIDX[wkey]
            it = _prep_it[0]
            _prep_it[0] += 1
            slot = it % 2
            g_wait(SY, f"wsl{slot}")
            wr = sy_dma(wslot[slot][:], wt_d[wkey][m], WSEM[slot])
            stg, stgk = _stgs[it % 4]
            g_wait(SE, stgk)
            SE.wait_ge(WSEM[slot].h, wr)
            SE.drain()
            SE.activation(stg[:, 0:1024].bitcast(F32R), wslot[slot][:],
                          AF.Identity).then_inc(SS.h, 1)
            SS.bump()
            VE.wait_ge(SS.h, SS.n)
            VE.tensor_tensor(stg[:, 1024:2048].bitcast(F32R), wslot[slot][:],
                             stg[:, 0:1024], ALU.subtract)
            VE.drain().then_inc(SV.h, 1)
            SV.bump()
            g_set(f"wsl{slot}", SV)
            GE.wait_ge(SV.h, SV.n)
            gsem = GDS[it % 4]
            GE.dma_start(out=wt_hl[wi, m], in_=stg[:].bitcast(F32R)).then_inc(gsem.h, 16)
            gsem.bump(16)
            _store_cnt.append((stgk, gsem, gsem.n))
            if len(_store_cnt) >= 4:
                pk, ps, pn = _store_cnt[-4]
                GE.wait_ge(ps.h, pn)
                GE.sem_inc(SG.h, 1)
                g_set(pk, SG, SG.bump())
            _prep_done[wkey] = _prep_done.get(wkey, 0) + 1

        def prep_fin(wkey):
            for gs in GDS:
                GE.wait_ge(gs.h, gs.n)
            GE.sem_inc(SG.h, 1)
            f = SG.bump()
            for pk, _, _ in _store_cnt[-3:]:
                g_set(pk, SG, f)
            wt_ready[wkey] = f

        def prep_weight(wkey):
            if wkey in wt_ready:
                return
            for m in range(_prep_done.get(wkey, 0), 8):
                prep_iter(wkey, m)
            prep_fin(wkey)

        prep_weight("r")

        ln_lock = [SS.n, SS.n]

        def ln_chunk(xtile, lnt, par):
            so = par * 24
            mo = par * 4
            VE.wait_ge(SS.h, ln_lock[par])
            for b2 in range(2):
                VE.bn_stats(stats[:, so + b2 * 12:so + b2 * 12 + 6],
                            xtile[:, b2 * 1024:b2 * 1024 + 512])
                VE.bn_stats(stats[:, so + b2 * 12 + 6:so + b2 * 12 + 12],
                            xtile[:, b2 * 1024 + 512:(b2 + 1) * 1024])
            VE.drain()
            for b2 in range(2):
                VE.bn_aggr(mv[:, mo + b2 * 2:mo + (b2 + 1) * 2],
                           stats[:, so + b2 * 12:so + (b2 + 1) * 12])
            VE.drain()
            mvv = mv[:, mo:mo + 4].rearrange("p (b t) -> p b t", t=2)
            VE.tensor_scalar(rstd[:, 2 * par:2 * par + 2], mvv[:, :, 1], EPS, None,
                             ALU.add)
            VE.drain().then_inc(SV.h, 1)
            SV.bump()
            SE.wait_ge(SV.h, SV.n)
            SE.activation(rstd[:, 2 * par:2 * par + 2], rstd[:, 2 * par:2 * par + 2],
                          AF.Ln)
            SE.drain()
            SE.activation(rstd[:, 2 * par:2 * par + 2], rstd[:, 2 * par:2 * par + 2],
                          AF.Exp, scale=-0.5).then_inc(SS.h, 1)
            SS.bump()
            VE.wait_ge(SS.h, SS.n)
            VE.scalar_tensor_tensor(nbias[:, 2 * par:2 * par + 2], mvv[:, :, 0], -1.0,
                                    rstd[:, 2 * par:2 * par + 2],
                                    ALU.mult, ALU.mult).then_inc(SV.h, 1)
            SV.bump()
            SE.wait_ge(SV.h, SV.n)
            last = None
            for b2 in range(2):
                last = SE.activation(lnt[:, b2 * 1024:(b2 + 1) * 1024],
                                     xtile[:, b2 * 1024:(b2 + 1) * 1024], AF.Identity,
                                     bias=nbias[:, 2 * par + b2:2 * par + b2 + 1],
                                     scale=rstd[:, 2 * par + b2:2 * par + b2 + 1])
            last.then_inc(SS.h, 1)
            SS.bump()
            ln_lock[par] = SS.n
            return SS.n

        xp_par = [0]

        def xpose_tile(src_tile, src_off, n_tok, dst_view, dst_c0, rsem, rn, stride=None):
            out_n = 0
            for b2 in range(n_tok // 128):
                for half in range(2):
                    par = xp_par[0]
                    xp_par[0] ^= 1
                    bank = pb[2 + par]
                    g_wait(TE, f"pb{2 + par}")
                    TE.wait_ge(rsem.h, rn)
                    last = None
                    for j in range(4):
                        fc = half * 4 + j
                        o = src_off + b2 * 1024 + fc * 128
                        last = TE.transpose(bank[:, j * 128:(j + 1) * 128],
                                            src_tile[:, o:o + 128], ident[:])
                    last.then_inc(TT.h, 1)
                    TT.bump()
                    VE.wait_ge(TT.h, TT.n)
                    pbv = bank[:].rearrange("p (f c) -> p f c", f=4)
                    VE.tensor_copy(dst_view[:, half * 4:half * 4 + 4,
                                            dst_c0 + b2 * 128:dst_c0 + (b2 + 1) * 128],
                                   pbv).then_inc(SV.h, 1)
                    SV.bump()
                    g_set(f"pb{2 + par}", SV)
                    out_n = SV.n
            return out_n

        pair_busy = [0]

        def xpose_split_tile(src_tile, dst_c0, rsem, rn):
            out_n = 0
            for half in range(2):
                par = xp_par[0]
                xp_par[0] ^= 1
                bank = pb[2 + par]
                g_wait(TE, f"pb{2 + par}")
                TE.wait_ge(rsem.h, rn)
                last = None
                for j in range(4):
                    fc = half * 4 + j
                    last = TE.transpose(bank[:, j * 128:(j + 1) * 128],
                                        src_tile[:, fc * 128:(fc + 1) * 128], ident[:])
                last.then_inc(TT.h, 1)
                TT.bump()
                VE.wait_ge(TT.h, TT.n)
                VE.wait_ge(TT.h, pair_busy[0])
                pbv = bank[:].rearrange("p (f c) -> p f c", f=4)
                VE.tensor_copy(sph[:, half * 4:half * 4 + 4, dst_c0:dst_c0 + 128], pbv)
                VE.drain()
                VE.tensor_tensor(spl[:, half * 4:half * 4 + 4, dst_c0:dst_c0 + 128], pbv,
                                 sphf[:, half * 4:half * 4 + 4, dst_c0:dst_c0 + 128],
                                 ALU.subtract)
                VE.drain().then_inc(SV.h, 1)
                SV.bump()
                g_set(f"pb{2 + par}", SV)
                out_n = SV.n
            return out_n

        def xpose_tile_pair(lnt, dst_c0, rsem, rn):
            out_n = 0
            for b2 in range(2):
                for half in range(2):
                    par = xp_par[0]
                    xp_par[0] ^= 1
                    bank = pb[2 + par]
                    g_wait(TE, f"pb{2 + par}")
                    TE.wait_ge(rsem.h, rn)
                    last = None
                    for j in range(4):
                        fc = half * 4 + j
                        last = TE.transpose(bank[:, j * 128:(j + 1) * 128],
                                            lnt[:, b2 * 1024 + fc * 128:
                                                b2 * 1024 + (fc + 1) * 128], ident[:])
                    last.then_inc(TT.h, 1)
                    TT.bump()
                    VE.wait_ge(TT.h, TT.n)
                    VE.wait_ge(TT.h, pair_busy[0])
                    pbv = bank[:].rearrange("p (f c) -> p f c", f=4)
                    d0 = dst_c0 + b2 * 128
                    VE.tensor_copy(sph[:, half * 4:half * 4 + 4, d0:d0 + 128], pbv)
                    VE.drain()
                    VE.tensor_tensor(spl[:, half * 4:half * 4 + 4, d0:d0 + 128], pbv,
                                     sphf[:, half * 4:half * 4 + 4, d0:d0 + 128],
                                     ALU.subtract)
                    VE.drain().then_inc(SV.h, 1)
                    SV.bump()
                    g_set(f"pb{2 + par}", SV)
                    out_n = SV.n
            return out_n

        def split(src_view, nrows, c_start, pre, per_chunk=None):
            out = {}
            first = True
            for c0, ln in ragged(nrows, c_start):
                for cc in range(c0, c0 + ln, CH):
                    if first:
                        SE.wait_ge(TT.h, pair_busy[0])
                    if per_chunk is not None:
                        if cc in per_chunk:
                            s, n = per_chunk[cc]
                            SE.wait_ge(s.h, n)
                    else:
                        for s, n in pre:
                            SE.wait_ge(s.h, n)
                    SE.drain()
                    SE.activation(sph[:, :, cc:cc + CH], src_view[:, :, cc:cc + CH],
                                  AF.Identity).then_inc(SS.h, 1)
                    SS.bump()
                    VE.wait_ge(SS.h, SS.n)
                    if first:
                        VE.wait_ge(TT.h, pair_busy[0])
                        first = False
                    VE.tensor_tensor(spl[:, :, cc:cc + CH], src_view[:, :, cc:cc + CH],
                                     sphf[:, :, cc:cc + CH], ALU.subtract)
                    VE.drain().then_inc(SV.h, 1)
                    SV.bump()
                out[c0] = SV.n
            return out

        def ragged(nrows, c_start):
            out = []
            c0 = c_start * CH
            while c0 < nrows:
                ln = min(512, nrows - c0)
                out.append((c0, ln))
                c0 += ln
            return out

        pre_wrd = {}

        def sweep_preload(wkey):
            if wkey in pre_wrd or wkey not in wt_ready:
                return
            wi = WIDX[wkey]
            d = {}
            for m in (0, 1, 2):
                g_wait(SY, f"ws2_{m}")
                SY.wait_ge(SG.h, wt_ready[wkey])
                d[m] = sy_dma(wslot2[m][:], wt_hl[wi, m], WSEM2[m])
            pre_wrd[wkey] = d

        def sweep3(wkey, nrows, c_start, evac, pre, pf=()):
            prep_weight(wkey)
            if not PF_ON:
                pf = ()
            pending = [(w, m) for w in pf if w not in wt_ready
                       for m in range(_prep_done.get(w, 0), 8)]
            wi = WIDX[wkey]
            chunks = ragged(nrows, c_start)
            if wkey in pre_wrd:
                wrd = pre_wrd.pop(wkey)
            else:
                wrd = {}
                for m in (0, 1, 2):
                    g_wait(SY, f"ws2_{m}")
                    SY.wait_ge(SG.h, wt_ready[wkey])
                    wrd[m] = sy_dma(wslot2[m][:], wt_hl[wi, m], WSEM2[m])
            first = True
            for m in range(8):
                TE.wait_ge(WSEM2[m % 3].h, wrd[m])
                if first:
                    if not isinstance(pre, dict):
                        for s, n in pre:
                            TE.wait_ge(s.h, n)
                    first = False
                for ci, (c0, ln) in enumerate(chunks):
                    par = (m * len(chunks) + ci) % 2
                    g_wait(TE, f"pb{par}")
                    if m == 0 and isinstance(pre, dict):
                        TE.wait_ge(SV.h, pre[c0])
                    last = None
                    nmm = 0
                    for k in range(8):
                        for wo, mvv in ((0, sph), (0, spl), (1024, sph)):
                            last = TE.matmul(pb[par][:, 0:ln],
                                             wslot2[m % 3][:, wo + k * 128:wo + (k + 1) * 128],
                                             mvv[:, k, c0:c0 + ln],
                                             start=(nmm == 0), stop=(nmm == 23))
                            nmm += 1
                    last.then_inc(TT.h, 1)
                    TT.bump()
                    if ci == len(chunks) - 1:
                        g_set(f"ws2_{m % 3}", TT)
                    evac(m, c0, ln, par, TT.n)
                for _ in range(2):
                    if pending:
                        w_, m_ = pending.pop(0)
                        prep_iter(w_, m_)
                        if m_ == 7:
                            prep_fin(w_)
                if m + 3 < 8:
                    g_wait(SY, f"ws2_{m % 3}")
                    wrd[m + 3] = sy_dma(wslot2[m % 3][:], wt_hl[wi, m + 3], WSEM2[m % 3])
            while pending:
                w_, m_ = pending.pop(0)
                prep_iter(w_, m_)
                if m_ == 7:
                    prep_fin(w_)
            pair_busy[0] = TT.n
            return

        def evac_s(dst_view, func, bias_t=None, off=0):
            def f(m, c0, ln, par, ttn):
                SE.wait_ge(TT.h, ttn)
                kw = {}
                if bias_t is not None:
                    kw = dict(bias=bias_t[:, m:m + 1], scale=1.0)
                SE.activation(dst_view[:, m, off + c0:off + c0 + ln],
                              pb[par][:, 0:ln], func, **kw).then_inc(SS.h, 1)
                SS.bump()
                g_set(f"pb{par}", SS)
            return f

        def evac_sigmoid(dst_view):
            def f(m, c0, ln, par, ttn):
                SE.wait_ge(TT.h, ttn)
                SE.activation(dst_view[:, m, c0:c0 + ln], pb[par][:, 0:ln],
                              AF.Tanh, scale=0.5).then_inc(SS.h, 1)
                SS.bump()
                g_set(f"pb{par}", SS)
                VE.wait_ge(SS.h, SS.n)
                VE.tensor_scalar(dst_view[:, m, c0:c0 + ln], dst_view[:, m, c0:c0 + ln],
                                 0.5, 0.5, ALU.mult, ALU.add)
                VE.drain().then_inc(SV.h, 1)
                SV.bump()
            return f

        def evac_v_prod(dst_view, r_view):
            def f(m, c0, ln, par, ttn):
                VE.wait_ge(TT.h, ttn)
                VE.tensor_tensor(dst_view[:, m, c0:c0 + ln], pb[par][:, 0:ln],
                                 r_view[:, m, c0:c0 + ln], ALU.mult).then_inc(SV.h, 1)
                SV.bump()
                g_set(f"pb{par}", SV)
            return f

        def resid_store(src_raw, prod_sem, prod_n, src_fn, dst, dst_is_y,
                        sbase, chunks, add_y1b=False, resident=False):
            fences = {}
            for c in chunks:
                xti = xt[c % 2]
                if src_fn is not None:
                    g_wait(SY, f"xt{c % 2}")
                    ap, wn = src_fn(c)
                    if wn:
                        SY.wait_ge(SG.h, wn)
                    xrd = sy_dma(tile_v(xti), ap, XSEM[c % 2])
                else:
                    xrd = None
                if add_y1b:
                    if xrd is not None:
                        VE.wait_ge(XSEM[c % 2].h, xrd)
                    VE.wait_ge(SV.h, y1b_ready)
                    VE.tensor_tensor(tile_v(xti), tile_v(xti), bc(y1b[:], 2), ALU.add)
                    VE.drain()
                if dst_is_y:
                    g_wait(VE, f"st{c % 2}")
                    dtile = st_[c % 2]
                else:
                    dtile = xti
                for b2 in range(2):
                    for half in range(2):
                        par = xp_par[0]
                        xp_par[0] ^= 1
                        bank = pb[2 + par]
                        g_wait(TE, f"pb{2 + par}")
                        TE.wait_ge(prod_sem.h, prod_n)
                        last = None
                        for j in range(4):
                            fc = half * 4 + j
                            o = fc * STW + c * CH + b2 * 128
                            last = TE.transpose(bank[:, j * 128:(j + 1) * 128],
                                                src_raw[:, o:o + 128], ident[:])
                        last.then_inc(TT.h, 1)
                        TT.bump()
                        VE.wait_ge(TT.h, TT.n)
                        if xrd is not None:
                            VE.wait_ge(XSEM[c % 2].h, xrd)
                        VE.drain()
                        sl = xti[:, b2 * 1024 + half * 512:b2 * 1024 + (half + 1) * 512]
                        dl = dtile[:, b2 * 1024 + half * 512:b2 * 1024 + (half + 1) * 512]
                        VE.tensor_tensor(dl, sl, bank[:], ALU.add).then_inc(SV.h, 1)
                        SV.bump()
                        g_set(f"pb{2 + par}", SV)
                if dst_is_y:
                    g_set(f"xt{c % 2}", SV)
                    GE.wait_ge(SV.h, SV.n)
                    r0 = sbase + c * CH - HALO
                    ge_dma(y_out[r0:r0 + 256, :].rearrange("(b p) d -> p b d", p=128),
                           tile_v(dtile))
                    fences[c] = ge_fence()
                    g_set(f"st{c % 2}", SG, fences[c])
                elif resident:
                    fences[c] = SV.n
                else:
                    g_set(f"xt{c % 2}", SV)
                    GE.wait_ge(SV.h, SV.n)
                    ge_dma(scr_rows(dst, sbase + c * CH), tile_v(xti))
                    fences[c] = ge_fence()
                    g_set(f"xt{c % 2}", SG, fences[c])
            return fences

        x_written = {"x1": 0, "x2": 0, "x3": 0}
        sl_state = {"tabs": 0}

        def ffn_sublayer(st, sbase, nrows, which):
            is2 = which == "sl2"
            lnc = lncol2 if is2 else lncol4
            c_start = 1 if (st == 0 and not is2) else 0
            chunks = list(range(c_start, nrows // CH))

            sweep_preload("r")
            g_wait(VE, "ARL")
            VE.drain()
            VE.tensor_copy(aL[:, :, c_start * CH:c_start * CH + 1], lnc[:].unsqueeze(2))
            VE.drain().then_inc(SV.h, 1)
            SV.bump()

            lnfm_ready = {}
            mix_ready = {}
            res = nrows == ST_real
            for c in chunks:
                xti = xt[c % 2]
                if is2:
                    g_wait(SY, f"xt{c % 2}")
                    if st == 0 and c == 0:
                        src = x_halo[1:257, :].rearrange("(b p) d -> p b d", p=128)
                    else:
                        t0 = sbase + c * CH - HALO
                        src = x_main[t0:t0 + 256, :].rearrange("(b p) d -> p b d", p=128)
                    xrd = sy_dma(tile_v(xti), src, XSEM[c % 2])
                    VE.wait_ge(XSEM[c % 2].h, xrd)
                    VE.wait_ge(SV.h, y1b_ready)
                    VE.tensor_tensor(tile_v(xti), tile_v(xti), bc(y1b[:], 2), ALU.add)
                    VE.drain().then_inc(SV.h, 1)
                    SV.bump()
                elif res:
                    VE.drain()
                else:
                    g_wait(SY, f"xt{c % 2}")
                    x3f = x_written["x3"]
                    SY.wait_ge(SG.h, x3f[c] if isinstance(x3f, dict) else x3f)
                    xrd = sy_dma(tile_v(xti), scr_rows(x3s, sbase + c * CH), XSEM[c % 2])
                    VE.wait_ge(XSEM[c % 2].h, xrd)
                lnt = st_[c % 2]
                g_wait(SE, f"st{c % 2}")
                ss_apply = ln_chunk(xti, lnt, c % 2)
                g_set(f"xt{c % 2}", SS, ss_apply)
                lnfm_ready[c] = xpose_tile(lnt, 0, CH, aL, 1 + c * CH, SS, ss_apply)
                g_set(f"st{c % 2}", TT)
                if st == 0 and is2 and c == 1:
                    VE.tensor_tensor(aL[:, :, CH:CH + 1], aL[:, :, CH:CH + 1],
                                     gate[:].unsqueeze(2), ALU.mult)
                    VE.drain().then_inc(SV.h, 1)
                    SV.bump()
                dt_ = st_[c % 2]
                g_wait(VE, f"st{c % 2}")
                if c == chunks[0]:
                    g_wait(VE, "AR0", "AR2")
                dv = v8(dt_, 256)
                c0 = 1 + c * CH
                VE.drain()
                VE.tensor_tensor(dv, aL[:, :, c0:c0 + CH], aL[:, :, c0 - 1:c0 + CH - 1],
                                 ALU.subtract)
                VE.drain()
                for fc in range(8):
                    VE.scalar_tensor_tensor(a2[:, fc, c * CH:(c + 1) * CH], dv[:, fc, :],
                                            mur[:, fc:fc + 1], aL[:, fc, c0 - 1:c0 + CH - 1],
                                            ALU.mult, ALU.add)
                    VE.scalar_tensor_tensor(a0[:, fc, c * CH:(c + 1) * CH], dv[:, fc, :],
                                            muk[:, fc:fc + 1], aL[:, fc, c0 - 1:c0 + CH - 1],
                                            ALU.mult, ALU.add)
                VE.drain().then_inc(SV.h, 1)
                SV.bump()
                g_set(f"st{c % 2}", SV)
                mix_ready[c * CH] = (SV, SV.n)
            mix_done = SV.n
            VE.tensor_copy(lnc[:].unsqueeze(2), aL[:, :, nrows:nrows + 1])
            VE.drain().then_inc(SV.h, 1)
            SV.bump()
            g_set("ARL", SV)

            sxr = split(a2, nrows, c_start, None, per_chunk=mix_ready)
            g_wait(SE, "AR2")
            sweep3("r", nrows, c_start, evac_sigmoid(a2), sxr, pf=("kf", "vf"))
            sxk = split(a0, nrows, c_start, [(SV, mix_done)])
            g_wait(SE, "AR0")
            sweep3("kf", nrows, c_start, evac_s(a0, AF.Relu), sxk, pf=("q", "k"))
            VE.wait_ge(SS.h, SS.n)
            VE.drain()
            VE.tensor_tensor(a0[:, :, c_start * CH:nrows], a0[:, :, c_start * CH:nrows],
                             a0[:, :, c_start * CH:nrows], ALU.mult)
            VE.drain().then_inc(SV.h, 1)
            SV.bump()
            sq_done = SV.n
            ska = split(a0, nrows, c_start, [(SV, sq_done)])
            g_wait(VE, "AR0")
            sweep3("vf", nrows, c_start, evac_v_prod(a0, a2), ska, pf=("v", "o"))
            prod_done = SV.n
            g_set("AR2", SV, prod_done)

            if res:
                src_fn = None
            elif is2:
                def src_fn(c):
                    if st == 0 and c == 0:
                        return x_halo[1:257, :].rearrange("(b p) d -> p b d", p=128), None
                    t0 = sbase + c * CH - HALO
                    return x_main[t0:t0 + 256, :].rearrange("(b p) d -> p b d", p=128), None
            else:
                x3f = x_written["x3"]

                def src_fn(c):
                    return scr_rows(x3s, sbase + c * CH), (
                        x3f[c] if isinstance(x3f, dict) else x3f)
            cnt = resid_store(AR0, SV, prod_done, src_fn,
                              x2s if is2 else None, not is2, sbase, chunks,
                              add_y1b=(is2 and not res), resident=res)
            g_set("AR0", TT)
            if is2:
                x_written["x2"] = cnt

        def attn_sublayer(st, sbase, nrows):
            nblk = nrows // BLK
            chunks = list(range(nrows // CH))

            sweep_preload("q")

            ln3_ready = {}
            res = nrows == ST_real
            g_wait(VE, "AR0")
            for c in chunks:
                xti = xt[c % 2]
                if res:
                    VE.drain()
                else:
                    g_wait(SY, f"xt{c % 2}")
                    x2f = x_written["x2"]
                    SY.wait_ge(SG.h, x2f[c] if isinstance(x2f, dict) else x2f)
                    xrd = sy_dma(tile_v(xti), scr_rows(x2s, sbase + c * CH), XSEM[c % 2])
                    VE.wait_ge(XSEM[c % 2].h, xrd)
                lnt = st_[c % 2]
                g_wait(SE, f"st{c % 2}")
                ss_apply = ln_chunk(xti, lnt, c % 2)
                g_set(f"xt{c % 2}", SS, ss_apply)
                ln3_ready[c * CH] = xpose_tile_pair(lnt, c * CH, SS, ss_apply)
                g_set(f"st{c % 2}", TT)

            def rope(view, coff, ev_done):
                done = 0
                for c in chunks:
                    pt = st_[c % 2]
                    g_wait(GE, f"st{c % 2}")
                    GE.wait_ge(SS.h, ev_done)
                    pv = v8(pt, 256)
                    for dp, sp in [(0, 32), (32, 0), (64, 96), (96, 64)]:
                        ge_dma(pv[dp:dp + 32, :, :],
                               view[sp:sp + 32, :, coff + c * CH:coff + (c + 1) * CH])
                    pf = ge_fence()
                    VE.wait_ge(SG.h, pf)
                    VE.wait_ge(TB.h, sl_state["tabs"])
                    ch_sl = view[:, :, coff + c * CH:coff + (c + 1) * CH]
                    VE.tensor_tensor(ch_sl, ch_sl, bc(cosT[:, c * CH:(c + 1) * CH], 8),
                                     ALU.mult)
                    VE.drain()
                    VE.tensor_tensor(pv, pv, bc(sinT[:, c * CH:(c + 1) * CH], 8), ALU.mult)
                    VE.drain()
                    VE.tensor_tensor(ch_sl, ch_sl, pv, ALU.add)
                    VE.drain().then_inc(SV.h, 1)
                    SV.bump()
                    g_set(f"st{c % 2}", SV)
                    done = SV.n
                return done

            g_wait(SY, "tabs")
            if st == 0:
                g_wait(SY, "wsl0", "wsl1")
            sy_dma(cosT[:, 0:nrows], cos_d[:, sbase:sbase + nrows], TB)
            sl_state["tabs"] = sy_dma(sinT[:, 0:nrows], sin_d[:, sbase:sbase + nrows], TB)
            sl3 = {c0: max(v for k, v in ln3_ready.items() if k < c0 + ln)
                   for c0, ln in ragged(nrows, 0)}
            g_wait(SE, "AR0")
            sweep3("q", nrows, 0, evac_s(a0, AF.Identity, bias_fm["q"]), sl3)
            qrot_done = rope(a0, 0, SS.n)
            g_wait(SE, "ARL")
            sweep3("k", nrows, 0, evac_s(aL, AF.Identity, bias_fm["k"], off=128), sl3)
            krot_done = rope(aL, 128, SS.n)
            g_set("tabs", SV, krot_done)
            VE.wait_ge(SG.h, SG.n)
            VE.tensor_copy(aL[:, :, 0:128], kcv)
            VE.drain()
            VE.tensor_copy(kcv, aL[:, :, nrows:nrows + 128])
            VE.drain().then_inc(SV.h, 1)
            SV.bump()
            kready = SV.n
            g_wait(SE, "AR2")
            sweep3("v", nrows, 0, evac_s(a2, AF.Identity, bias_fm["v"]), sl3)
            v_ev = SS.n
            sweep_preload("o")

            scores_last = TT.n
            exp_prev = SS.n
            recip_prev = SV.n
            for n in range(nblk):
                TE.wait_ge(SS.h, v_ev)
                if n == 0:
                    TE.wait_ge(SV.h, max(qrot_done, krot_done, kready))
                for half in range(2):
                    bank = pb[2 + half]
                    g_wait(TE, f"pb{2 + half}")
                    last = None
                    for j in range(4):
                        fc = half * 4 + j
                        last = TE.transpose(bank[:, j * 128:(j + 1) * 128],
                                            a2[:, fc, n * 128:(n + 1) * 128], ident[:])
                    last.then_inc(TT.h, 1)
                    TT.bump()
                    VE.wait_ge(TT.h, TT.n)
                    VE.tensor_copy(a2[:, half * 4:half * 4 + 4, n * 128:(n + 1) * 128],
                                   bank[:].rearrange("p (f c) -> p f c", f=4))
                    VE.drain().then_inc(SV.h, 1)
                    SV.bump()
                    g_set(f"pb{2 + half}", SV)
                vtm_done = SV.n

                mk = ({0: "first", 1: "band", 2: "blk2"}.get(n, "band")
                      if st == 0 else "band")
                mt = masks[mk]

                pc_started = [False, False]

                def scores_mm(hp):
                    spar = hp % 2
                    sb0, sb1 = (4, 5) if spar == 0 else (0, 1)
                    g_wait(TE, f"pb{sb0}", f"pb{sb1}")
                    last = None
                    for j in range(2):
                        h = 2 * hp + j
                        fc, po = h // 2, (h % 2) * 64
                        q_ap = a0[po:po + 64, fc, n * 128:(n + 1) * 128]
                        k_ap = aL[po:po + 64, fc, n * 128:n * 128 + 256]
                        last = TE.matmul(pb[(sb0, sb1)[j]][:, 0:256], q_ap, k_ap,
                                         start=True, stop=True)
                    last.then_inc(TT.h, 1)
                    TT.bump()
                    return (sb0, sb1, TT.n)

                def mask_part(hp, meta):
                    sb0, sb1, sc_n = meta
                    spar = hp % 2
                    prs = st_[0][:, spar * 1024:spar * 1024 + 512]
                    prsv = prs.rearrange("p (j x) -> p j x", j=2)
                    psx = ps_of[sb0][:].rearrange("p (j x) -> p j x", j=2)[:, :, 0:256]
                    g_wait(VE, f"pr{spar}")
                    VE.wait_ge(TT.h, sc_n)
                    if hp == 0:
                        VE.wait_ge(SS.h, exp_prev)
                        VE.wait_ge(SM.h, small_done)
                    VE.tensor_tensor(prsv, psx, bc(mt[:], 2), ALU.add)
                    VE.drain()
                    VE.tensor_reduce(nmax[:, 2 * hp:2 * hp + 2], prsv,
                                     axis=AX.X, op=ALU.max, negate=True)
                    VE.drain()
                    VE.tensor_scalar(nmax[:, 2 * hp:2 * hp + 2], nmax[:, 2 * hp:2 * hp + 2],
                                     0.125, None, ALU.mult)
                    VE.drain().then_inc(SV.h, 1)
                    SV.bump()
                    g_set(f"pb{sb0}", SV)
                    g_set(f"pb{sb1}", SV)
                    return SV.n

                def pair_rest(hp, m_n):
                    spar = hp % 2
                    prs = st_[0][:, spar * 1024:spar * 1024 + 512]
                    SE.wait_ge(SV.h, m_n)
                    if hp == 0:
                        SE.wait_ge(SV.h, recip_prev)
                    last = None
                    for j in range(2):
                        h = 2 * hp + j
                        last = SE.activation(prs[:, j * 256:(j + 1) * 256],
                                             prs[:, j * 256:(j + 1) * 256], AF.Exp,
                                             bias=nmax[:, h:h + 1], scale=0.125,
                                             accum_out=rsum[:, h:h + 1])
                    last.then_inc(SS.h, 1)
                    SS.bump()
                    exp_n = SS.n
                    tpar = xp_par[0]
                    xp_par[0] ^= 1
                    tbank = pb[2 + tpar]
                    g_wait(TE, f"pb{2 + tpar}")
                    TE.wait_ge(SS.h, exp_n)
                    last = None
                    for j in range(2):
                        for kc in range(2):
                            last = TE.transpose(
                                tbank[:, (2 * j + kc) * 128:(2 * j + kc + 1) * 128],
                                prs[:, j * 256 + kc * 128:j * 256 + (kc + 1) * 128],
                                ident[:])
                    last.then_inc(TT.h, 1)
                    TT.bump()
                    g_set(f"pr{spar}", TT)
                    ats = st_[1][:, spar * 1024:spar * 1024 + 512]
                    g_wait(VE, f"at{spar}")
                    VE.wait_ge(TT.h, TT.n)
                    VE.tensor_copy(ats, tbank[:]).then_inc(SV.h, 1)
                    SV.bump()
                    g_set(f"pb{2 + tpar}", SV)
                    return ats, SV.n

                def pair_ctx(hp, fm):
                    nonlocal scores_last
                    ats, at_n = fm
                    spar = hp % 2
                    if hp == 0:
                        g_wait(TE, "pc")
                    TE.wait_ge(SV.h, max(at_n, vtm_done))
                    last = None
                    for j in range(2):
                        h = 2 * hp + j
                        fc, po = h // 2, (h % 2) * 64
                        bank_i = h // 8
                        for kc in range(2):
                            bv = n - 1 + kc
                            if bv < 0:
                                v_ap = vcv[:, fc, po:po + 64]
                            else:
                                v_ap = a2[:, fc, bv * 128 + po:bv * 128 + po + 64]
                            stt = not pc_started[bank_i]
                            pc_started[bank_i] = True
                            stop = (h % 8 == 7) and kc == 1
                            last = TE.matmul(pc[:, h * 64:(h + 1) * 64],
                                             ats[:, (2 * j + kc) * 128:(2 * j + kc + 1) * 128],
                                             v_ap, start=stt, stop=stop)
                    last.then_inc(TT.h, 1)
                    TT.bump()
                    g_set(f"at{spar}", TT)
                    scores_last = TT.n

                metas = {0: scores_mm(0), 1: scores_mm(1)}
                msv = {0: mask_part(0, metas.pop(0))}
                fms = {}
                for hp in range(8):
                    if hp + 1 < 8:
                        msv[hp + 1] = mask_part(hp + 1, metas.pop(hp + 1))
                    if hp + 2 < 8:
                        metas[hp + 2] = scores_mm(hp + 2)
                    if hp >= 1:
                        pair_ctx(hp - 1, fms.pop(hp - 1))
                    fms[hp] = pair_rest(hp, msv.pop(hp))
                pair_ctx(7, fms.pop(7))
                VE.wait_ge(SS.h, SS.n)
                VE.reciprocal(recip[:, 0:16], rsum[:, 0:16])
                VE.drain()
                VE.wait_ge(TT.h, scores_last)
                pcv = pc[:].rearrange("p (h e) -> p h e", h=16)
                rb = recip[:, 0:16].unsqueeze(2).broadcast_to([128, 16, 64])
                VE.tensor_tensor(ctm[:].rearrange("p (h e) -> p h e", h=16), pcv, rb,
                                 ALU.mult)
                VE.drain().then_inc(SV.h, 1)
                SV.bump()
                g_set("pc", SV)
                recip_prev = SV.n
                exp_prev = SS.n
                ctx_split = xpose_split_tile(ctm, n * 128, SV, SV.n)
            g_set("vc", TT, scores_last)
            g_wait(VE, "vc")
            VE.tensor_copy(vcv, a2[:, :, (nblk - 1) * 128:nblk * 128])
            VE.drain().then_inc(SV.h, 1)
            SV.bump()
            ctx_done = SV.n
            g_set("ARL", TT, scores_last)
            g_set("AR2", SV, ctx_done)
            g_set("AR2", TT, scores_last)
            g_set("AR0", TT, scores_last)

            g_wait(SE, "AR0")
            sweep3("o", nrows, 0, evac_s(a0, AF.Identity, bias_fm["o"]),
                  [(SV, ctx_split)])
            o_done = SS.n

            if res:
                src_fn = None
            else:
                x2f = x_written["x2"]

                def src_fn(c):
                    return scr_rows(x2s, sbase + c * CH), (
                        x2f[c] if isinstance(x2f, dict) else x2f)
            x_written["x3"] = resid_store(AR0, SS, o_done, src_fn,
                                          x3s, False, sbase, chunks, resident=res)
            g_set("AR0", TT)

        _ph = [0]

        def phase_ok():
            _ph[0] += 1
            return trunc is None or _ph[0] <= trunc

        for st in range(n_st):
            nrows = ST_real + (HALO if st == 0 else 0)
            sbase = 0 if st == 0 else HALO + st * ST_real
            if not phase_ok():
                break
            ffn_sublayer(st, sbase, nrows, "sl2")
            if not phase_ok():
                break
            attn_sublayer(st, sbase, nrows)
            if not phase_ok():
                break
            if st == 0:
                x3f = x_written["x3"]
                GE.wait_ge(SG.h, max(x3f.values()) if isinstance(x3f, dict) else x3f)
                ge_dma(rowa, x3s[255:256, :])
                rrd = ge_fence()
                VE.wait_ge(SG.h, rrd)
                g_wait(SE, "AR0")
                ln_row(rowa, rowb)
                row_to_fm(rowb, lncol4[:])
                g_set("AR0", TT)
                VE.drain()
                VE.tensor_tensor(lncol4[:], lncol4[:], gate[:], ALU.mult)
                VE.drain().then_inc(SV.h, 1)
                SV.bump()
            ffn_sublayer(st, sbase, nrows, "sl4")

        GE.wait_ge(GD.h, GD.n)

    return nc



def _prep_w(W):
    a = W.reshape(8, 128, 8, 128).transpose(0, 3, 2, 1)
    return np.ascontiguousarray(a).reshape(8, 128, 1024)


def _prep_wa(W):
    return np.ascontiguousarray(np.ascontiguousarray(W.T).reshape(8, 128, 1024))


def _masks():
    import ml_dtypes
    qi = np.arange(128)[:, None]
    kj = np.arange(256)[None, :]
    band = (kj >= qi) & (kj <= qi + 128)
    first = band & (kj >= 128)
    mb = np.where(band, 0.0, NEG).astype(np.float32)
    mf = np.where(first, 0.0, NEG).astype(np.float32)
    t2 = lambda m: np.ascontiguousarray(m.astype(ml_dtypes.bfloat16))
    return t2(mf), t2(mb)


def _tables(h, T_real):
    n_scr = HALO + T_real
    pos = np.maximum(h * T_real - HALO + np.arange(n_scr), 0).astype(np.float64)
    p = np.arange(128)
    j = p % 32
    inv = 10000.0 ** (-2.0 * j / 64.0)
    ang = pos[None, :] * inv[:, None]
    cos_t = np.cos(ang).astype(np.float32)
    sign = np.where((p % 64) < 32, -1.0, 1.0)
    sin_t = (np.sin(ang) * sign[:, None]).astype(np.float32)
    return cos_t, sin_t


def make_core_inputs(x, weights, b, h, T_real=4096):
    core = {}
    s0 = h * T_real
    core["x_main"] = np.ascontiguousarray(x[b, s0:s0 + T_real])
    if h == 0:
        core["x_halo"] = np.zeros((HALO + 1, D), np.float32)
    else:
        core["x_halo"] = np.ascontiguousarray(x[b, s0 - HALO - 1:s0])
    core["xrow1"] = np.ascontiguousarray(x[b, 1:2])
    core["brow_a"] = np.ascontiguousarray(np.asarray(weights["L0_attn_a_b"],
                                                     np.float32).reshape(1, D))
    core["wta"] = _prep_wa(np.asarray(weights["L0_attn_a_W"], np.float32))
    core["wt_r"] = _prep_w(np.asarray(weights["L0_ffn_W_r"], np.float32))
    core["wt_kf"] = _prep_w(np.asarray(weights["L0_ffn_W_k"], np.float32))
    core["wt_vf"] = _prep_w(np.asarray(weights["L0_ffn_W_v"], np.float32))
    core["wt_q"] = _prep_w(np.asarray(weights["L0_attn_b_Wq"], np.float32))
    core["wt_k"] = _prep_w(np.asarray(weights["L0_attn_b_Wk"], np.float32))
    core["wt_v"] = _prep_w(np.asarray(weights["L0_attn_b_Wv"], np.float32))
    core["wt_o"] = _prep_w(np.asarray(weights["L0_attn_b_Wo"], np.float32))
    rs = lambda v: np.ascontiguousarray(np.asarray(v, np.float32).reshape(8, 128).T)
    core["mufm_r"] = rs(weights["L0_ffn_mu_r"])
    core["mufm_k"] = rs(weights["L0_ffn_mu_k"])
    core["bfm_q"] = rs(weights["L0_attn_b_bq"])
    core["bfm_k"] = rs(weights["L0_attn_b_bk"])
    core["bfm_v"] = rs(weights["L0_attn_b_bv"])
    core["bfm_o"] = rs(weights["L0_attn_b_bo"])
    core["ident"] = np.eye(128, dtype=np.float32)
    mf, mb = _masks()
    core["mask_first"] = mf
    core["mask_band"] = mb
    core["mask_blk2"] = mf if h == 0 else mb
    core["gate"] = np.full((128, 8), 0.0 if h == 0 else 1.0, np.float32)
    cos_t, sin_t = _tables(h, T_real)
    core["cos_t"] = cos_t
    core["sin_t"] = sin_t
    return core


def kernel(**inputs):
    x = np.asarray(inputs["L0_attn_a_x"], np.float32)
    B, S, _ = x.shape
    T_real = S // 2
    nc = build_program(T_real=T_real, ST_real=512)
    in_maps = [make_core_inputs(x, inputs, core // 2, core % 2, T_real)
               for core in range(8)]
    res = run_bass_kernel_spmd(nc, in_maps, core_ids=list(range(8)))
    out = np.empty((B, S, D), np.float32)
    for core in range(8):
        b, h = core // 2, core % 2
        out[b, h * T_real:(h + 1) * T_real] = res.results[core]["y"]
    return out


if __name__ == "__main__":
    import reference
    ins = {k: np.asarray(v) for k, v in reference.setup_inputs().items()}
    out = kernel(**ins)
    ref = np.asarray(reference.reference(**ins))
    print("rel:", np.linalg.norm(out - ref) / np.linalg.norm(ref))


# version 27
# speedup vs baseline: 1.0084x; 1.0084x over previous
import numpy as np
from contextlib import ExitStack

import concourse.bass as bass
import concourse.mybir as mybir
from concourse.bass_utils import run_bass_kernel_spmd

F32 = mybir.dt.float32
F32R = mybir.dt.float32r
BF16 = mybir.dt.bfloat16
ALU = mybir.AluOpType
AF = mybir.ActivationFunctionType
AX = mybir.AxisListType

D = 1024
BLK = 128
CH = 256
EPS = 1e-5
NEG = -1e30
HALO = 256


class Sem:
    def __init__(self, h):
        self.h = h
        self.n = 0

    def bump(self, k=1):
        self.n += k
        return self.n


PF_ON = True


def build_program(T_real=4096, ST_real=512, trunc=None):
    assert T_real % ST_real == 0 and ST_real % CH == 0
    n_st = T_real // ST_real
    n_scr = HALO + T_real
    STW = ST_real + HALO

    nc = bass.Bass("TRN2", target_bir_lowering=False)

    din = lambda name, shape, dt=F32: nc.dram_tensor(name, shape, dt, kind="ExternalInput").ap()
    x_main = din("x_main", [T_real, D])
    x_halo = din("x_halo", [HALO + 1, D])
    xrow1 = din("xrow1", [1, D])
    brow_a = din("brow_a", [1, D])
    wt_d = {k: din("wt_" + k, [8, 128, 1024]) for k in ["r", "kf", "vf", "q", "k", "v", "o"]}
    wta_d = din("wta", [8, 128, 1024])
    mufm_r_d = din("mufm_r", [128, 8])
    mufm_k_d = din("mufm_k", [128, 8])
    bfm_d = {k: din("bfm_" + k, [128, 8]) for k in ["q", "k", "v", "o"]}
    ident_d = din("ident", [128, 128])
    mask_d = {k: din("mask_" + k, [128, 256], BF16) for k in ["first", "band", "blk2"]}
    gate_d = din("gate", [128, 8])
    cos_d = din("cos_t", [128, n_scr])
    sin_d = din("sin_t", [128, n_scr])

    y_out = nc.dram_tensor("y", [T_real, D], F32, kind="ExternalOutput").ap()
    wt_hl = nc.dram_tensor("wt_hl", [7, 8, 128, 2048], F32R, kind="Internal").ap()
    x2s = nc.dram_tensor("x2s", [n_scr, D], F32, kind="Internal").ap()
    x3s = nc.dram_tensor("x3s", [n_scr, D], F32, kind="Internal").ap()

    TE, VE, SE, GE, SY = nc.tensor, nc.vector, nc.scalar, nc.gpsimd, nc.sync

    with ExitStack() as ectx:
        sb = lambda name, shape, dt=F32: ectx.enter_context(nc.sbuf_tensor(name, shape, dt))
        psal = lambda name, shape: ectx.enter_context(nc.psum_tensor(name, shape, F32))

        AR0 = sb("AR0", [128, 8 * STW])
        AR2 = sb("AR2", [128, 8 * STW])
        ARL = sb("ARL", [128, 8 * (STW + 128)])
        SPH = sb("SPH", [128, 8 * STW], F32R)
        SPL = sb("SPL", [128, 8 * STW], F32R)
        wsl = sb("wsl", [128, 2 * 1024])
        wsl2 = sb("wsl2", [128, 3 * 2048], F32R)
        xt = [sb("xt0", [128, 2048]), sb("xt1", [128, 2048])]
        st_ = [sb("st0", [128, 2048]), sb("st1", [128, 2048])]
        ctm = sb("ctm", [128, 1024])
        vcarry = sb("vcarry", [128, 1024])
        kcarry = sb("kcarry", [128, 1024])
        y1b = sb("y1b", [128, 1024])
        masks = {k: sb("m_" + k, [128, 256], BF16) for k in ["first", "band", "blk2"]}
        ident = sb("identt", [128, 128])
        gate = sb("gatet", [128, 8])
        mur = sb("mur", [128, 8])
        muk = sb("muk", [128, 8])
        bias_fm = {k: sb("b_" + k, [128, 8]) for k in ["q", "k", "v", "o"]}
        lncol2 = sb("lncol2", [128, 8])
        lncol4 = sb("lncol4", [128, 8])
        l1fm = sb("l1fm", [128, 8])
        stats = sb("statst", [128, 48])
        mv = sb("mvt", [128, 8])
        rstd = sb("rstdt", [128, 4])
        nbias = sb("nbiast", [128, 4])
        nmax = sb("nmaxt", [128, 16])
        rsum = sb("rsumt", [128, 16])
        recip = sb("recipt", [128, 16])
        rowa = ctm[0:1, 0:1024]
        rowb = AR0[0:1, 0:1024]
        rowc = AR2[0:1, 0:1024]
        rowd = st_[0][0:1, 0:1024]
        one1 = sb("one1", [1, 128])
        cosT = wsl[:, 0:STW]
        sinT = wsl[:, STW:2 * STW]
        stat1 = sb("stat1", [1, 12])
        mv1 = sb("mv1", [1, 2])
        rst1 = sb("rst1", [1, 2])

        psA = psal("psA", [128, 1024])
        psB = psal("psB", [128, 1024])
        psC = psal("psC", [128, 1024])
        pb = [psA[:, 0:512], psA[:, 512:1024], psB[:, 0:512], psB[:, 512:1024],
              psC[:, 0:512], psC[:, 512:1024]]
        pc = psal("pc", [128, 1024])
        ps_of = {0: psA, 1: psA, 2: psB, 3: psB, 4: psC, 5: psC}

        GD = Sem(ectx.enter_context(nc.semaphore("GD")))
        SV = Sem(ectx.enter_context(nc.semaphore("SV")))
        SS = Sem(ectx.enter_context(nc.semaphore("SS")))
        TT = Sem(ectx.enter_context(nc.semaphore("TT")))
        SG = Sem(ectx.enter_context(nc.semaphore("SG")))
        SM = Sem(ectx.enter_context(nc.semaphore("SM")))
        W0 = Sem(ectx.enter_context(nc.semaphore("W0")))
        W1 = Sem(ectx.enter_context(nc.semaphore("W1")))
        X0 = Sem(ectx.enter_context(nc.semaphore("X0")))
        X1 = Sem(ectx.enter_context(nc.semaphore("X1")))
        TB = Sem(ectx.enter_context(nc.semaphore("TB")))
        WSEM = [W0, W1]
        XSEM = [X0, X1]
        GDS = [Sem(ectx.enter_context(nc.semaphore(f"GDS{i}"))) for i in range(4)]
        WSEM2 = [Sem(ectx.enter_context(nc.semaphore(f"WS2{i}"))) for i in range(3)]

        v8 = lambda buf, w: buf[:].rearrange("p (f c) -> p f c", f=8)
        a0, a2 = v8(AR0, STW), v8(AR2, STW)
        aL = v8(ARL, STW + 128)
        sph = v8(SPH, STW)
        spl = v8(SPL, STW)
        sphf = SPH[:].bitcast(F32).rearrange("p (f c) -> p f c", f=8)
        vcv = v8(vcarry, 128)
        kcv = v8(kcarry, 128)
        wslot = [wsl[:, 0:1024], wsl[:, 1024:2048]]
        wslot2 = [wsl2[:, 0:2048], wsl2[:, 2048:4096], wsl2[:, 4096:6144]]
        WIDX = {"r": 0, "kf": 1, "vf": 2, "q": 3, "k": 4, "v": 5, "o": 6}

        guards = {}

        def g_wait(eng, *keys):
            for key in keys:
                for s, n in guards.pop(key, []):
                    eng.wait_ge(s.h, n)

        def g_set(key, sem, n=None):
            guards.setdefault(key, []).append((sem, n if n is not None else sem.n))

        def sy_dma(out, in_, sem):
            SY.dma_start(out=out, in_=in_).then_inc(sem.h, 16)
            return sem.bump(16)

        def ge_dma(out, in_):
            GE.dma_start(out=out, in_=in_).then_inc(GD.h, 16)
            return GD.bump(16)

        def ge_fence():
            GE.wait_ge(GD.h, GD.n)
            GE.sem_inc(SG.h, 1)
            return SG.bump()

        def bc(ap, reps):
            return ap.unsqueeze(1).broadcast_to([ap.shape[0], reps, ap.shape[1]])

        def scr_rows(t, r0):
            return t[r0:r0 + 256, :].rearrange("(b p) d -> p b d", p=128)

        def tile_v(tile):
            return tile[:].rearrange("p (b d) -> p b d", b=2)

        sy_dma(rowa, xrow1, SM)
        wta_bufs = [wslot[0][:], wslot[1][:],
                    st_[1][:, 0:1024], st_[1][:, 1024:2048], st_[0][:, 1024:2048],
                    AR0[:, 1024:2048], AR0[:, 2048:3072], AR2[:, 1024:2048]]
        wta_sems = [W0, W1, X0, X1, TB, WSEM2[0], WSEM2[1], WSEM2[2]]
        wrd = {kk: sy_dma(wta_bufs[kk], wta_d[kk], wta_sems[kk]) for kk in range(2)}
        for dst, src in [(masks["first"][:], mask_d["first"]), (masks["band"][:], mask_d["band"]),
                         (masks["blk2"][:], mask_d["blk2"]), (ident[:], ident_d),
                         (gate[:], gate_d), (mur[:], mufm_r_d), (muk[:], mufm_k_d),
                         (bias_fm["q"][:], bfm_d["q"]), (bias_fm["k"][:], bfm_d["k"]),
                         (bias_fm["v"][:], bfm_d["v"]), (bias_fm["o"][:], bfm_d["o"]),
                         (rowb, brow_a), (rowc, x_halo[0:1, :])]:
            sy_dma(dst, src, SM)
        small_done = SM.n

        GE.memset(kcarry[:], 0.0)
        GE.memset(vcarry[:], 0.0)
        GE.memset(one1[:], 1.0).then_inc(SG.h, 1)
        SG.bump()

        def ln_row(src_row, dst_row):
            VE.wait_ge(SS.h, SS.n)
            VE.bn_stats(stat1[:, 0:6], src_row[0:1, 0:512])
            VE.bn_stats(stat1[:, 6:12], src_row[0:1, 512:1024])
            VE.drain()
            VE.bn_aggr(mv1[:], stat1[:])
            VE.drain()
            VE.tensor_scalar(rst1[:, 0:1], mv1[:, 1:2], EPS, None, ALU.add)
            VE.drain().then_inc(SV.h, 1)
            SV.bump()
            SE.wait_ge(SV.h, SV.n)
            SE.activation(rst1[:, 0:1], rst1[:, 0:1], AF.Ln)
            SE.drain()
            SE.activation(rst1[:, 0:1], rst1[:, 0:1], AF.Exp, scale=-0.5).then_inc(SS.h, 1)
            SS.bump()
            VE.wait_ge(SS.h, SS.n)
            VE.scalar_tensor_tensor(rst1[:, 1:2], mv1[:, 0:1], -1.0, rst1[:, 0:1],
                                    ALU.mult, ALU.mult).then_inc(SV.h, 1)
            SV.bump()
            SE.wait_ge(SV.h, SV.n)
            SE.activation(dst_row[0:1, :], src_row[0:1, :], AF.Identity,
                          bias=rst1[:, 1:2], scale=rst1[:, 0:1]).then_inc(SS.h, 1)
            SS.bump()

        def row_to_fm(src_row, dst_col):
            g_wait(TE, "pb5")
            TE.wait_ge(SS.h, SS.n)
            TE.wait_ge(SG.h, SG.n)
            last = None
            for fc in range(8):
                last = TE.transpose(pb[5][:, fc:fc + 1],
                                    src_row[0:1, fc * 128:(fc + 1) * 128], one1[0:1, 0:1])
            last.then_inc(TT.h, 1)
            TT.bump()
            VE.wait_ge(TT.h, TT.n)
            VE.tensor_copy(dst_col, pb[5][:, 0:8]).then_inc(SV.h, 1)
            SV.bump()
            g_set("pb5", SV)

        VE.wait_ge(SM.h, small_done)
        ln_row(rowa, rowd)
        row_to_fm(rowd, l1fm[:])
        for kk in range(2, 8):
            wrd[kk] = sy_dma(wta_bufs[kk], wta_d[kk], wta_sems[kk])
        for kk in range(8):
            TE.wait_ge(wta_sems[kk].h, wrd[kk])
            if kk == 0:
                TE.wait_ge(SV.h, SV.n)
                g_wait(TE, "pb0", "pb1")
            last = None
            for dh in range(2):
                last = TE.matmul(pb[dh][0:1, 0:512], l1fm[:, kk:kk + 1],
                                 wta_bufs[kk][:, dh * 512:(dh + 1) * 512],
                                 start=(kk == 0), stop=False)
            last.then_inc(TT.h, 1)
            TT.bump()
        last = None
        for dh in range(2):
            last = TE.matmul(pb[dh][0:1, 0:512], one1[0:1, 0:1],
                             rowb[0:1, dh * 512:(dh + 1) * 512], start=False, stop=True)
        last.then_inc(TT.h, 1)
        TT.bump()
        g_set("wsl0", TT)
        g_set("wsl1", TT)
        g_set("st1", TT)
        g_set("st0", TT)
        VE.wait_ge(TT.h, TT.n)
        VE.tensor_copy(rowd[0:1, 0:512], pb[0][0:1, 0:512])
        VE.tensor_copy(rowd[0:1, 512:1024], pb[1][0:1, 0:512])
        VE.drain().then_inc(SV.h, 1)
        SV.bump()
        g_set("pb0", SV)
        g_set("pb1", SV)
        g_wait(TE, "pc")
        TE.wait_ge(SV.h, SV.n)
        last = None
        for dh in range(2):
            last = TE.matmul(pc[:, dh * 512:(dh + 1) * 512], one1[0:1, :],
                             rowd[0:1, dh * 512:(dh + 1) * 512], start=True, stop=True)
        last.then_inc(TT.h, 1)
        TT.bump()
        VE.wait_ge(TT.h, TT.n)
        VE.tensor_copy(y1b[:], pc[:]).then_inc(SV.h, 1)
        SV.bump()
        g_set("pc", SV)
        y1b_ready = SV.n

        VE.tensor_tensor(rowa, rowc, rowd, ALU.add)
        VE.drain().then_inc(SV.h, 1)
        SV.bump()
        g_set("st0", SV)
        ln_row(rowa, rowc)
        row_to_fm(rowc, lncol2[:])
        g_set("AR2", TT)
        g_set("AR0", TT)

        wt_ready = {}
        _stgs = [(st_[0], "st0"), (st_[1], "st1"), (xt[0], "xt0"), (xt[1], "xt1")]
        _store_cnt = []
        _prep_it = [0]
        _prep_done = {}

        def prep_iter(wkey, m):
            wi = WIDX[wkey]
            it = _prep_it[0]
            _prep_it[0] += 1
            slot = it % 2
            g_wait(SY, f"wsl{slot}")
            wr = sy_dma(wslot[slot][:], wt_d[wkey][m], WSEM[slot])
            stg, stgk = _stgs[it % 4]
            g_wait(SE, stgk)
            SE.wait_ge(WSEM[slot].h, wr)
            SE.drain()
            SE.activation(stg[:, 0:1024].bitcast(F32R), wslot[slot][:],
                          AF.Identity).then_inc(SS.h, 1)
            SS.bump()
            VE.wait_ge(SS.h, SS.n)
            VE.tensor_tensor(stg[:, 1024:2048].bitcast(F32R), wslot[slot][:],
                             stg[:, 0:1024], ALU.subtract)
            VE.drain().then_inc(SV.h, 1)
            SV.bump()
            g_set(f"wsl{slot}", SV)
            GE.wait_ge(SV.h, SV.n)
            gsem = GDS[it % 4]
            GE.dma_start(out=wt_hl[wi, m], in_=stg[:].bitcast(F32R)).then_inc(gsem.h, 16)
            gsem.bump(16)
            _store_cnt.append((stgk, gsem, gsem.n))
            if len(_store_cnt) >= 4:
                pk, ps, pn = _store_cnt[-4]
                GE.wait_ge(ps.h, pn)
                GE.sem_inc(SG.h, 1)
                g_set(pk, SG, SG.bump())
            _prep_done[wkey] = _prep_done.get(wkey, 0) + 1

        def prep_fin(wkey):
            for gs in GDS:
                GE.wait_ge(gs.h, gs.n)
            GE.sem_inc(SG.h, 1)
            f = SG.bump()
            for pk, _, _ in _store_cnt[-3:]:
                g_set(pk, SG, f)
            wt_ready[wkey] = f

        def prep_weight(wkey):
            if wkey in wt_ready:
                return
            for m in range(_prep_done.get(wkey, 0), 8):
                prep_iter(wkey, m)
            prep_fin(wkey)

        prep_weight("r")

        ln_lock = [SS.n, SS.n]

        def ln_chunk(xtile, lnt, par):
            so = par * 24
            mo = par * 4
            VE.wait_ge(SS.h, ln_lock[par])
            for b2 in range(2):
                VE.bn_stats(stats[:, so + b2 * 12:so + b2 * 12 + 6],
                            xtile[:, b2 * 1024:b2 * 1024 + 512])
                VE.bn_stats(stats[:, so + b2 * 12 + 6:so + b2 * 12 + 12],
                            xtile[:, b2 * 1024 + 512:(b2 + 1) * 1024])
            VE.drain()
            for b2 in range(2):
                VE.bn_aggr(mv[:, mo + b2 * 2:mo + (b2 + 1) * 2],
                           stats[:, so + b2 * 12:so + (b2 + 1) * 12])
            VE.drain()
            mvv = mv[:, mo:mo + 4].rearrange("p (b t) -> p b t", t=2)
            VE.tensor_scalar(rstd[:, 2 * par:2 * par + 2], mvv[:, :, 1], EPS, None,
                             ALU.add)
            VE.drain().then_inc(SV.h, 1)
            SV.bump()
            SE.wait_ge(SV.h, SV.n)
            SE.activation(rstd[:, 2 * par:2 * par + 2], rstd[:, 2 * par:2 * par + 2],
                          AF.Ln)
            SE.drain()
            SE.activation(rstd[:, 2 * par:2 * par + 2], rstd[:, 2 * par:2 * par + 2],
                          AF.Exp, scale=-0.5).then_inc(SS.h, 1)
            SS.bump()
            VE.wait_ge(SS.h, SS.n)
            VE.scalar_tensor_tensor(nbias[:, 2 * par:2 * par + 2], mvv[:, :, 0], -1.0,
                                    rstd[:, 2 * par:2 * par + 2],
                                    ALU.mult, ALU.mult).then_inc(SV.h, 1)
            SV.bump()
            SE.wait_ge(SV.h, SV.n)
            last = None
            for b2 in range(2):
                last = SE.activation(lnt[:, b2 * 1024:(b2 + 1) * 1024],
                                     xtile[:, b2 * 1024:(b2 + 1) * 1024], AF.Identity,
                                     bias=nbias[:, 2 * par + b2:2 * par + b2 + 1],
                                     scale=rstd[:, 2 * par + b2:2 * par + b2 + 1])
            last.then_inc(SS.h, 1)
            SS.bump()
            ln_lock[par] = SS.n
            return SS.n

        xp_par = [0]

        def xpose_tile(src_tile, src_off, n_tok, dst_view, dst_c0, rsem, rn, stride=None):
            out_n = 0
            for b2 in range(n_tok // 128):
                for half in range(2):
                    par = xp_par[0]
                    xp_par[0] ^= 1
                    bank = pb[2 + par]
                    g_wait(TE, f"pb{2 + par}")
                    TE.wait_ge(rsem.h, rn)
                    last = None
                    for j in range(4):
                        fc = half * 4 + j
                        o = src_off + b2 * 1024 + fc * 128
                        last = TE.transpose(bank[:, j * 128:(j + 1) * 128],
                                            src_tile[:, o:o + 128], ident[:])
                    last.then_inc(TT.h, 1)
                    TT.bump()
                    VE.wait_ge(TT.h, TT.n)
                    pbv = bank[:].rearrange("p (f c) -> p f c", f=4)
                    VE.tensor_copy(dst_view[:, half * 4:half * 4 + 4,
                                            dst_c0 + b2 * 128:dst_c0 + (b2 + 1) * 128],
                                   pbv).then_inc(SV.h, 1)
                    SV.bump()
                    g_set(f"pb{2 + par}", SV)
                    out_n = SV.n
            return out_n

        pair_busy = [0]

        def xpose_split_tile(src_tile, dst_c0, rsem, rn):
            out_n = 0
            for half in range(2):
                par = xp_par[0]
                xp_par[0] ^= 1
                bank = pb[2 + par]
                g_wait(TE, f"pb{2 + par}")
                TE.wait_ge(rsem.h, rn)
                last = None
                for j in range(4):
                    fc = half * 4 + j
                    last = TE.transpose(bank[:, j * 128:(j + 1) * 128],
                                        src_tile[:, fc * 128:(fc + 1) * 128], ident[:])
                last.then_inc(TT.h, 1)
                TT.bump()
                VE.wait_ge(TT.h, TT.n)
                VE.wait_ge(TT.h, pair_busy[0])
                pbv = bank[:].rearrange("p (f c) -> p f c", f=4)
                VE.tensor_copy(sph[:, half * 4:half * 4 + 4, dst_c0:dst_c0 + 128], pbv)
                VE.drain()
                VE.tensor_tensor(spl[:, half * 4:half * 4 + 4, dst_c0:dst_c0 + 128], pbv,
                                 sphf[:, half * 4:half * 4 + 4, dst_c0:dst_c0 + 128],
                                 ALU.subtract)
                VE.drain().then_inc(SV.h, 1)
                SV.bump()
                g_set(f"pb{2 + par}", SV)
                out_n = SV.n
            return out_n

        def xpose_tile_pair(lnt, dst_c0, rsem, rn):
            out_n = 0
            for b2 in range(2):
                for half in range(2):
                    par = xp_par[0]
                    xp_par[0] ^= 1
                    bank = pb[2 + par]
                    g_wait(TE, f"pb{2 + par}")
                    TE.wait_ge(rsem.h, rn)
                    last = None
                    for j in range(4):
                        fc = half * 4 + j
                        last = TE.transpose(bank[:, j * 128:(j + 1) * 128],
                                            lnt[:, b2 * 1024 + fc * 128:
                                                b2 * 1024 + (fc + 1) * 128], ident[:])
                    last.then_inc(TT.h, 1)
                    TT.bump()
                    VE.wait_ge(TT.h, TT.n)
                    VE.wait_ge(TT.h, pair_busy[0])
                    pbv = bank[:].rearrange("p (f c) -> p f c", f=4)
                    d0 = dst_c0 + b2 * 128
                    VE.tensor_copy(sph[:, half * 4:half * 4 + 4, d0:d0 + 128], pbv)
                    VE.drain()
                    VE.tensor_tensor(spl[:, half * 4:half * 4 + 4, d0:d0 + 128], pbv,
                                     sphf[:, half * 4:half * 4 + 4, d0:d0 + 128],
                                     ALU.subtract)
                    VE.drain().then_inc(SV.h, 1)
                    SV.bump()
                    g_set(f"pb{2 + par}", SV)
                    out_n = SV.n
            return out_n

        def split(src_view, nrows, c_start, pre, per_chunk=None):
            out = {}
            first = True
            for c0, ln in ragged(nrows, c_start):
                for cc in range(c0, c0 + ln, CH):
                    if first:
                        SE.wait_ge(TT.h, pair_busy[0])
                    if per_chunk is not None:
                        if cc in per_chunk:
                            s, n = per_chunk[cc]
                            SE.wait_ge(s.h, n)
                    else:
                        for s, n in pre:
                            SE.wait_ge(s.h, n)
                    SE.drain()
                    SE.activation(sph[:, :, cc:cc + CH], src_view[:, :, cc:cc + CH],
                                  AF.Identity).then_inc(SS.h, 1)
                    SS.bump()
                    VE.wait_ge(SS.h, SS.n)
                    if first:
                        VE.wait_ge(TT.h, pair_busy[0])
                        first = False
                    VE.tensor_tensor(spl[:, :, cc:cc + CH], src_view[:, :, cc:cc + CH],
                                     sphf[:, :, cc:cc + CH], ALU.subtract)
                    VE.drain().then_inc(SV.h, 1)
                    SV.bump()
                out[c0] = SV.n
            return out

        def ragged(nrows, c_start):
            out = []
            c0 = c_start * CH
            while c0 < nrows:
                ln = min(512, nrows - c0)
                out.append((c0, ln))
                c0 += ln
            return out

        pre_wrd = {}

        def sweep_preload(wkey):
            if wkey in pre_wrd or wkey not in wt_ready:
                return
            wi = WIDX[wkey]
            d = {}
            for m in (0, 1, 2):
                g_wait(SY, f"ws2_{m}")
                SY.wait_ge(SG.h, wt_ready[wkey])
                d[m] = sy_dma(wslot2[m][:], wt_hl[wi, m], WSEM2[m])
            pre_wrd[wkey] = d

        def sweep3(wkey, nrows, c_start, evac, pre, pf=()):
            prep_weight(wkey)
            if not PF_ON:
                pf = ()
            pending = [(w, m) for w in pf if w not in wt_ready
                       for m in range(_prep_done.get(w, 0), 8)]
            wi = WIDX[wkey]
            chunks = ragged(nrows, c_start)
            if wkey in pre_wrd:
                wrd = pre_wrd.pop(wkey)
            else:
                wrd = {}
                for m in (0, 1, 2):
                    g_wait(SY, f"ws2_{m}")
                    SY.wait_ge(SG.h, wt_ready[wkey])
                    wrd[m] = sy_dma(wslot2[m][:], wt_hl[wi, m], WSEM2[m])
            first = True
            for m in range(8):
                TE.wait_ge(WSEM2[m % 3].h, wrd[m])
                if first:
                    if not isinstance(pre, dict):
                        for s, n in pre:
                            TE.wait_ge(s.h, n)
                    first = False
                for ci, (c0, ln) in enumerate(chunks):
                    par = (m * len(chunks) + ci) % 2
                    g_wait(TE, f"pb{par}")
                    if m == 0 and isinstance(pre, dict):
                        TE.wait_ge(SV.h, pre[c0])
                    last = None
                    nmm = 0
                    for k in range(8):
                        for wo, mvv in ((0, sph), (0, spl), (1024, sph)):
                            last = TE.matmul(pb[par][:, 0:ln],
                                             wslot2[m % 3][:, wo + k * 128:wo + (k + 1) * 128],
                                             mvv[:, k, c0:c0 + ln],
                                             start=(nmm == 0), stop=(nmm == 23))
                            nmm += 1
                    last.then_inc(TT.h, 1)
                    TT.bump()
                    if ci == len(chunks) - 1:
                        g_set(f"ws2_{m % 3}", TT)
                    evac(m, c0, ln, par, TT.n)
                for _ in range(2):
                    if pending:
                        w_, m_ = pending.pop(0)
                        prep_iter(w_, m_)
                        if m_ == 7:
                            prep_fin(w_)
                if m + 3 < 8:
                    g_wait(SY, f"ws2_{m % 3}")
                    wrd[m + 3] = sy_dma(wslot2[m % 3][:], wt_hl[wi, m + 3], WSEM2[m % 3])
            while pending:
                w_, m_ = pending.pop(0)
                prep_iter(w_, m_)
                if m_ == 7:
                    prep_fin(w_)
            pair_busy[0] = TT.n
            return

        def evac_s(dst_view, func, bias_t=None, off=0):
            def f(m, c0, ln, par, ttn):
                SE.wait_ge(TT.h, ttn)
                kw = {}
                if bias_t is not None:
                    kw = dict(bias=bias_t[:, m:m + 1], scale=1.0)
                SE.activation(dst_view[:, m, off + c0:off + c0 + ln],
                              pb[par][:, 0:ln], func, **kw).then_inc(SS.h, 1)
                SS.bump()
                g_set(f"pb{par}", SS)
            return f

        def evac_sigmoid(dst_view):
            def f(m, c0, ln, par, ttn):
                SE.wait_ge(TT.h, ttn)
                SE.activation(dst_view[:, m, c0:c0 + ln], pb[par][:, 0:ln],
                              AF.Tanh, scale=0.5).then_inc(SS.h, 1)
                SS.bump()
                g_set(f"pb{par}", SS)
                VE.wait_ge(SS.h, SS.n)
                VE.tensor_scalar(dst_view[:, m, c0:c0 + ln], dst_view[:, m, c0:c0 + ln],
                                 0.5, 0.5, ALU.mult, ALU.add)
                VE.drain().then_inc(SV.h, 1)
                SV.bump()
            return f

        def evac_relu_sq(dst_view):
            def f(m, c0, ln, par, ttn):
                SE.wait_ge(TT.h, ttn)
                SE.activation(dst_view[:, m, c0:c0 + ln], pb[par][:, 0:ln],
                              AF.Relu).then_inc(SS.h, 1)
                SS.bump()
                g_set(f"pb{par}", SS)
                VE.wait_ge(SS.h, SS.n)
                VE.tensor_tensor(dst_view[:, m, c0:c0 + ln],
                                 dst_view[:, m, c0:c0 + ln],
                                 dst_view[:, m, c0:c0 + ln], ALU.mult)
                VE.drain().then_inc(SV.h, 1)
                SV.bump()
            return f

        def evac_v_prod(dst_view, r_view):
            def f(m, c0, ln, par, ttn):
                VE.wait_ge(TT.h, ttn)
                VE.tensor_tensor(dst_view[:, m, c0:c0 + ln], pb[par][:, 0:ln],
                                 r_view[:, m, c0:c0 + ln], ALU.mult).then_inc(SV.h, 1)
                SV.bump()
                g_set(f"pb{par}", SV)
            return f

        def resid_store(src_raw, prod_sem, prod_n, src_fn, dst, dst_is_y,
                        sbase, chunks, add_y1b=False, resident=False):
            fences = {}
            for c in chunks:
                xti = xt[c % 2]
                if src_fn is not None:
                    g_wait(SY, f"xt{c % 2}")
                    ap, wn = src_fn(c)
                    if wn:
                        SY.wait_ge(SG.h, wn)
                    xrd = sy_dma(tile_v(xti), ap, XSEM[c % 2])
                else:
                    xrd = None
                if add_y1b:
                    if xrd is not None:
                        VE.wait_ge(XSEM[c % 2].h, xrd)
                    VE.wait_ge(SV.h, y1b_ready)
                    VE.tensor_tensor(tile_v(xti), tile_v(xti), bc(y1b[:], 2), ALU.add)
                    VE.drain()
                if dst_is_y:
                    g_wait(VE, f"st{c % 2}")
                    dtile = st_[c % 2]
                else:
                    dtile = xti
                for b2 in range(2):
                    for half in range(2):
                        par = xp_par[0]
                        xp_par[0] ^= 1
                        bank = pb[2 + par]
                        g_wait(TE, f"pb{2 + par}")
                        TE.wait_ge(prod_sem.h, prod_n)
                        last = None
                        for j in range(4):
                            fc = half * 4 + j
                            o = fc * STW + c * CH + b2 * 128
                            last = TE.transpose(bank[:, j * 128:(j + 1) * 128],
                                                src_raw[:, o:o + 128], ident[:])
                        last.then_inc(TT.h, 1)
                        TT.bump()
                        VE.wait_ge(TT.h, TT.n)
                        if xrd is not None:
                            VE.wait_ge(XSEM[c % 2].h, xrd)
                        VE.drain()
                        sl = xti[:, b2 * 1024 + half * 512:b2 * 1024 + (half + 1) * 512]
                        dl = dtile[:, b2 * 1024 + half * 512:b2 * 1024 + (half + 1) * 512]
                        VE.tensor_tensor(dl, sl, bank[:], ALU.add).then_inc(SV.h, 1)
                        SV.bump()
                        g_set(f"pb{2 + par}", SV)
                if dst_is_y:
                    g_set(f"xt{c % 2}", SV)
                    GE.wait_ge(SV.h, SV.n)
                    r0 = sbase + c * CH - HALO
                    ge_dma(y_out[r0:r0 + 256, :].rearrange("(b p) d -> p b d", p=128),
                           tile_v(dtile))
                    fences[c] = ge_fence()
                    g_set(f"st{c % 2}", SG, fences[c])
                elif resident:
                    fences[c] = SV.n
                else:
                    g_set(f"xt{c % 2}", SV)
                    GE.wait_ge(SV.h, SV.n)
                    ge_dma(scr_rows(dst, sbase + c * CH), tile_v(xti))
                    fences[c] = ge_fence()
                    g_set(f"xt{c % 2}", SG, fences[c])
            return fences

        x_written = {"x1": 0, "x2": 0, "x3": 0}
        sl_state = {"tabs": 0}

        def ffn_sublayer(st, sbase, nrows, which):
            is2 = which == "sl2"
            lnc = lncol2 if is2 else lncol4
            c_start = 1 if (st == 0 and not is2) else 0
            chunks = list(range(c_start, nrows // CH))

            sweep_preload("r")
            g_wait(VE, "ARL")
            VE.drain()
            VE.tensor_copy(aL[:, :, c_start * CH:c_start * CH + 1], lnc[:].unsqueeze(2))
            VE.drain().then_inc(SV.h, 1)
            SV.bump()

            lnfm_ready = {}
            mix_ready = {}
            res = nrows == ST_real
            for c in chunks:
                xti = xt[c % 2]
                if is2:
                    g_wait(SY, f"xt{c % 2}")
                    if st == 0 and c == 0:
                        src = x_halo[1:257, :].rearrange("(b p) d -> p b d", p=128)
                    else:
                        t0 = sbase + c * CH - HALO
                        src = x_main[t0:t0 + 256, :].rearrange("(b p) d -> p b d", p=128)
                    xrd = sy_dma(tile_v(xti), src, XSEM[c % 2])
                    VE.wait_ge(XSEM[c % 2].h, xrd)
                    VE.wait_ge(SV.h, y1b_ready)
                    VE.tensor_tensor(tile_v(xti), tile_v(xti), bc(y1b[:], 2), ALU.add)
                    VE.drain().then_inc(SV.h, 1)
                    SV.bump()
                elif res:
                    VE.drain()
                else:
                    g_wait(SY, f"xt{c % 2}")
                    x3f = x_written["x3"]
                    SY.wait_ge(SG.h, x3f[c] if isinstance(x3f, dict) else x3f)
                    xrd = sy_dma(tile_v(xti), scr_rows(x3s, sbase + c * CH), XSEM[c % 2])
                    VE.wait_ge(XSEM[c % 2].h, xrd)
                lnt = st_[c % 2]
                g_wait(SE, f"st{c % 2}")
                ss_apply = ln_chunk(xti, lnt, c % 2)
                g_set(f"xt{c % 2}", SS, ss_apply)
                lnfm_ready[c] = xpose_tile(lnt, 0, CH, aL, 1 + c * CH, SS, ss_apply)
                g_set(f"st{c % 2}", TT)
                if st == 0 and is2 and c == 1:
                    VE.tensor_tensor(aL[:, :, CH:CH + 1], aL[:, :, CH:CH + 1],
                                     gate[:].unsqueeze(2), ALU.mult)
                    VE.drain().then_inc(SV.h, 1)
                    SV.bump()
                dt_ = st_[c % 2]
                g_wait(VE, f"st{c % 2}")
                if c == chunks[0]:
                    g_wait(VE, "AR0", "AR2")
                dv = v8(dt_, 256)
                c0 = 1 + c * CH
                VE.drain()
                VE.tensor_tensor(dv, aL[:, :, c0:c0 + CH], aL[:, :, c0 - 1:c0 + CH - 1],
                                 ALU.subtract)
                VE.drain()
                for fc in range(8):
                    VE.scalar_tensor_tensor(a2[:, fc, c * CH:(c + 1) * CH], dv[:, fc, :],
                                            mur[:, fc:fc + 1], aL[:, fc, c0 - 1:c0 + CH - 1],
                                            ALU.mult, ALU.add)
                    VE.scalar_tensor_tensor(a0[:, fc, c * CH:(c + 1) * CH], dv[:, fc, :],
                                            muk[:, fc:fc + 1], aL[:, fc, c0 - 1:c0 + CH - 1],
                                            ALU.mult, ALU.add)
                VE.drain().then_inc(SV.h, 1)
                SV.bump()
                g_set(f"st{c % 2}", SV)
                mix_ready[c * CH] = (SV, SV.n)
            mix_done = SV.n
            VE.tensor_copy(lnc[:].unsqueeze(2), aL[:, :, nrows:nrows + 1])
            VE.drain().then_inc(SV.h, 1)
            SV.bump()
            g_set("ARL", SV)

            sxr = split(a2, nrows, c_start, None, per_chunk=mix_ready)
            g_wait(SE, "AR2")
            sweep3("r", nrows, c_start, evac_sigmoid(a2), sxr, pf=("kf", "vf"))
            sxk = split(a0, nrows, c_start, [(SV, mix_done)])
            g_wait(SE, "AR0")
            sweep3("kf", nrows, c_start, evac_relu_sq(a0), sxk, pf=("q", "k"))
            sq_done = SV.n
            ska = split(a0, nrows, c_start, [(SV, sq_done)])
            g_wait(VE, "AR0")
            sweep3("vf", nrows, c_start, evac_v_prod(a0, a2), ska, pf=("v", "o"))
            prod_done = SV.n
            g_set("AR2", SV, prod_done)

            if res:
                src_fn = None
            elif is2:
                def src_fn(c):
                    if st == 0 and c == 0:
                        return x_halo[1:257, :].rearrange("(b p) d -> p b d", p=128), None
                    t0 = sbase + c * CH - HALO
                    return x_main[t0:t0 + 256, :].rearrange("(b p) d -> p b d", p=128), None
            else:
                x3f = x_written["x3"]

                def src_fn(c):
                    return scr_rows(x3s, sbase + c * CH), (
                        x3f[c] if isinstance(x3f, dict) else x3f)
            cnt = resid_store(AR0, SV, prod_done, src_fn,
                              x2s if is2 else None, not is2, sbase, chunks,
                              add_y1b=(is2 and not res), resident=res)
            g_set("AR0", TT)
            if is2:
                x_written["x2"] = cnt

        def attn_sublayer(st, sbase, nrows):
            nblk = nrows // BLK
            chunks = list(range(nrows // CH))

            sweep_preload("q")

            ln3_ready = {}
            res = nrows == ST_real
            g_wait(VE, "AR0")
            for c in chunks:
                xti = xt[c % 2]
                if res:
                    VE.drain()
                else:
                    g_wait(SY, f"xt{c % 2}")
                    x2f = x_written["x2"]
                    SY.wait_ge(SG.h, x2f[c] if isinstance(x2f, dict) else x2f)
                    xrd = sy_dma(tile_v(xti), scr_rows(x2s, sbase + c * CH), XSEM[c % 2])
                    VE.wait_ge(XSEM[c % 2].h, xrd)
                lnt = st_[c % 2]
                g_wait(SE, f"st{c % 2}")
                ss_apply = ln_chunk(xti, lnt, c % 2)
                g_set(f"xt{c % 2}", SS, ss_apply)
                ln3_ready[c * CH] = xpose_tile_pair(lnt, c * CH, SS, ss_apply)
                g_set(f"st{c % 2}", TT)

            def rope(view, coff, ev_done):
                done = 0
                for c in chunks:
                    pt = st_[c % 2]
                    g_wait(GE, f"st{c % 2}")
                    GE.wait_ge(SS.h, ev_done)
                    pv = v8(pt, 256)
                    for dp, sp in [(0, 32), (32, 0), (64, 96), (96, 64)]:
                        ge_dma(pv[dp:dp + 32, :, :],
                               view[sp:sp + 32, :, coff + c * CH:coff + (c + 1) * CH])
                    pf = ge_fence()
                    VE.wait_ge(SG.h, pf)
                    VE.wait_ge(TB.h, sl_state["tabs"])
                    ch_sl = view[:, :, coff + c * CH:coff + (c + 1) * CH]
                    VE.tensor_tensor(ch_sl, ch_sl, bc(cosT[:, c * CH:(c + 1) * CH], 8),
                                     ALU.mult)
                    VE.drain()
                    VE.tensor_tensor(pv, pv, bc(sinT[:, c * CH:(c + 1) * CH], 8), ALU.mult)
                    VE.drain()
                    VE.tensor_tensor(ch_sl, ch_sl, pv, ALU.add)
                    VE.drain().then_inc(SV.h, 1)
                    SV.bump()
                    g_set(f"st{c % 2}", SV)
                    done = SV.n
                return done

            g_wait(SY, "tabs")
            if st == 0:
                g_wait(SY, "wsl0", "wsl1")
            sy_dma(cosT[:, 0:nrows], cos_d[:, sbase:sbase + nrows], TB)
            sl_state["tabs"] = sy_dma(sinT[:, 0:nrows], sin_d[:, sbase:sbase + nrows], TB)
            sl3 = {c0: max(v for k, v in ln3_ready.items() if k < c0 + ln)
                   for c0, ln in ragged(nrows, 0)}
            g_wait(SE, "AR0")
            sweep3("q", nrows, 0, evac_s(a0, AF.Identity, bias_fm["q"]), sl3)
            qrot_done = rope(a0, 0, SS.n)
            g_wait(SE, "ARL")
            sweep3("k", nrows, 0, evac_s(aL, AF.Identity, bias_fm["k"], off=128), sl3)
            krot_done = rope(aL, 128, SS.n)
            g_set("tabs", SV, krot_done)
            VE.wait_ge(SG.h, SG.n)
            VE.tensor_copy(aL[:, :, 0:128], kcv)
            VE.drain()
            VE.tensor_copy(kcv, aL[:, :, nrows:nrows + 128])
            VE.drain().then_inc(SV.h, 1)
            SV.bump()
            kready = SV.n
            g_wait(SE, "AR2")
            sweep3("v", nrows, 0, evac_s(a2, AF.Identity, bias_fm["v"]), sl3)
            v_ev = SS.n
            sweep_preload("o")

            scores_last = TT.n
            exp_prev = SS.n
            recip_prev = SV.n
            for n in range(nblk):
                g_wait(TE, "pc")
                TE.wait_ge(SS.h, v_ev)
                if n == 0:
                    TE.wait_ge(SV.h, max(qrot_done, krot_done, kready))
                last = None
                for fc in range(8):
                    last = TE.transpose(pc[:, fc * 128:(fc + 1) * 128],
                                        a2[:, fc, n * 128:(n + 1) * 128], ident[:])
                last.then_inc(TT.h, 1)
                TT.bump()
                VE.wait_ge(TT.h, TT.n)
                VE.tensor_copy(a2[:, :, n * 128:(n + 1) * 128],
                               pc[:].rearrange("p (f c) -> p f c", f=8))
                VE.drain().then_inc(SV.h, 1)
                SV.bump()
                g_set("pc", SV)
                vtm_done = SV.n

                mk = ({0: "first", 1: "band", 2: "blk2"}.get(n, "band")
                      if st == 0 else "band")
                mt = masks[mk]

                pc_started = [False, False]

                def scores_mm(hp):
                    spar = hp % 2
                    sb0, sb1 = (4, 5) if spar == 0 else (0, 1)
                    g_wait(TE, f"pb{sb0}", f"pb{sb1}")
                    last = None
                    for j in range(2):
                        h = 2 * hp + j
                        fc, po = h // 2, (h % 2) * 64
                        q_ap = a0[po:po + 64, fc, n * 128:(n + 1) * 128]
                        k_ap = aL[po:po + 64, fc, n * 128:n * 128 + 256]
                        last = TE.matmul(pb[(sb0, sb1)[j]][:, 0:256], q_ap, k_ap,
                                         start=True, stop=True)
                    last.then_inc(TT.h, 1)
                    TT.bump()
                    return (sb0, sb1, TT.n)

                def mask_part(hp, meta):
                    sb0, sb1, sc_n = meta
                    spar = hp % 2
                    prs = st_[0][:, spar * 1024:spar * 1024 + 512]
                    prsv = prs.rearrange("p (j x) -> p j x", j=2)
                    psx = ps_of[sb0][:].rearrange("p (j x) -> p j x", j=2)[:, :, 0:256]
                    g_wait(VE, f"pr{spar}")
                    VE.wait_ge(TT.h, sc_n)
                    if hp == 0:
                        VE.wait_ge(SS.h, exp_prev)
                        VE.wait_ge(SM.h, small_done)
                    VE.tensor_tensor(prsv, psx, bc(mt[:], 2), ALU.add)
                    VE.drain()
                    VE.tensor_reduce(nmax[:, 2 * hp:2 * hp + 2], prsv,
                                     axis=AX.X, op=ALU.max, negate=True)
                    VE.drain()
                    VE.tensor_scalar(nmax[:, 2 * hp:2 * hp + 2], nmax[:, 2 * hp:2 * hp + 2],
                                     0.125, None, ALU.mult)
                    VE.drain().then_inc(SV.h, 1)
                    SV.bump()
                    g_set(f"pb{sb0}", SV)
                    g_set(f"pb{sb1}", SV)
                    return SV.n

                def pair_rest(hp, m_n):
                    spar = hp % 2
                    prs = st_[0][:, spar * 1024:spar * 1024 + 512]
                    SE.wait_ge(SV.h, m_n)
                    if hp == 0:
                        SE.wait_ge(SV.h, recip_prev)
                    last = None
                    for j in range(2):
                        h = 2 * hp + j
                        last = SE.activation(prs[:, j * 256:(j + 1) * 256],
                                             prs[:, j * 256:(j + 1) * 256], AF.Exp,
                                             bias=nmax[:, h:h + 1], scale=0.125,
                                             accum_out=rsum[:, h:h + 1])
                    last.then_inc(SS.h, 1)
                    SS.bump()
                    exp_n = SS.n
                    tpar = xp_par[0]
                    xp_par[0] ^= 1
                    tbank = pb[2 + tpar]
                    g_wait(TE, f"pb{2 + tpar}")
                    TE.wait_ge(SS.h, exp_n)
                    last = None
                    for j in range(2):
                        for kc in range(2):
                            last = TE.transpose(
                                tbank[:, (2 * j + kc) * 128:(2 * j + kc + 1) * 128],
                                prs[:, j * 256 + kc * 128:j * 256 + (kc + 1) * 128],
                                ident[:])
                    last.then_inc(TT.h, 1)
                    TT.bump()
                    g_set(f"pr{spar}", TT)
                    ats = st_[1][:, spar * 1024:spar * 1024 + 512]
                    g_wait(VE, f"at{spar}")
                    VE.wait_ge(TT.h, TT.n)
                    VE.tensor_copy(ats, tbank[:]).then_inc(SV.h, 1)
                    SV.bump()
                    g_set(f"pb{2 + tpar}", SV)
                    return ats, SV.n

                def pair_ctx(hp, fm):
                    nonlocal scores_last
                    ats, at_n = fm
                    spar = hp % 2
                    TE.wait_ge(SV.h, max(at_n, vtm_done))
                    last = None
                    for j in range(2):
                        h = 2 * hp + j
                        fc, po = h // 2, (h % 2) * 64
                        bank_i = h // 8
                        for kc in range(2):
                            bv = n - 1 + kc
                            if bv < 0:
                                v_ap = vcv[:, fc, po:po + 64]
                            else:
                                v_ap = a2[:, fc, bv * 128 + po:bv * 128 + po + 64]
                            stt = not pc_started[bank_i]
                            pc_started[bank_i] = True
                            stop = (h % 8 == 7) and kc == 1
                            last = TE.matmul(pc[:, h * 64:(h + 1) * 64],
                                             ats[:, (2 * j + kc) * 128:(2 * j + kc + 1) * 128],
                                             v_ap, start=stt, stop=stop)
                    last.then_inc(TT.h, 1)
                    TT.bump()
                    g_set(f"at{spar}", TT)
                    scores_last = TT.n

                metas = {0: scores_mm(0), 1: scores_mm(1)}
                msv = {0: mask_part(0, metas.pop(0))}
                fms = {}
                for hp in range(8):
                    if hp + 1 < 8:
                        msv[hp + 1] = mask_part(hp + 1, metas.pop(hp + 1))
                    if hp + 2 < 8:
                        metas[hp + 2] = scores_mm(hp + 2)
                    if hp >= 1:
                        pair_ctx(hp - 1, fms.pop(hp - 1))
                    fms[hp] = pair_rest(hp, msv.pop(hp))
                pair_ctx(7, fms.pop(7))
                VE.wait_ge(SS.h, SS.n)
                VE.reciprocal(recip[:, 0:16], rsum[:, 0:16])
                VE.drain()
                VE.wait_ge(TT.h, scores_last)
                pcv = pc[:].rearrange("p (h e) -> p h e", h=16)
                rb = recip[:, 0:16].unsqueeze(2).broadcast_to([128, 16, 64])
                VE.tensor_tensor(ctm[:].rearrange("p (h e) -> p h e", h=16), pcv, rb,
                                 ALU.mult)
                VE.drain().then_inc(SV.h, 1)
                SV.bump()
                g_set("pc", SV)
                recip_prev = SV.n
                exp_prev = SS.n
                ctx_split = xpose_split_tile(ctm, n * 128, SV, SV.n)
            g_set("vc", TT, scores_last)
            g_wait(VE, "vc")
            VE.tensor_copy(vcv, a2[:, :, (nblk - 1) * 128:nblk * 128])
            VE.drain().then_inc(SV.h, 1)
            SV.bump()
            ctx_done = SV.n
            g_set("ARL", TT, scores_last)
            g_set("AR2", SV, ctx_done)
            g_set("AR2", TT, scores_last)
            g_set("AR0", TT, scores_last)

            g_wait(SE, "AR0")
            sweep3("o", nrows, 0, evac_s(a0, AF.Identity, bias_fm["o"]),
                  [(SV, ctx_split)])
            o_done = SS.n

            if res:
                src_fn = None
            else:
                x2f = x_written["x2"]

                def src_fn(c):
                    return scr_rows(x2s, sbase + c * CH), (
                        x2f[c] if isinstance(x2f, dict) else x2f)
            x_written["x3"] = resid_store(AR0, SS, o_done, src_fn,
                                          x3s, False, sbase, chunks, resident=res)
            g_set("AR0", TT)

        _ph = [0]

        def phase_ok():
            _ph[0] += 1
            return trunc is None or _ph[0] <= trunc

        for st in range(n_st):
            nrows = ST_real + (HALO if st == 0 else 0)
            sbase = 0 if st == 0 else HALO + st * ST_real
            if not phase_ok():
                break
            ffn_sublayer(st, sbase, nrows, "sl2")
            if not phase_ok():
                break
            attn_sublayer(st, sbase, nrows)
            if not phase_ok():
                break
            if st == 0:
                x3f = x_written["x3"]
                GE.wait_ge(SG.h, max(x3f.values()) if isinstance(x3f, dict) else x3f)
                ge_dma(rowa, x3s[255:256, :])
                rrd = ge_fence()
                VE.wait_ge(SG.h, rrd)
                g_wait(SE, "AR0")
                ln_row(rowa, rowb)
                row_to_fm(rowb, lncol4[:])
                g_set("AR0", TT)
                VE.drain()
                VE.tensor_tensor(lncol4[:], lncol4[:], gate[:], ALU.mult)
                VE.drain().then_inc(SV.h, 1)
                SV.bump()
            ffn_sublayer(st, sbase, nrows, "sl4")

        GE.wait_ge(GD.h, GD.n)

    return nc



def _prep_w(W):
    a = W.reshape(8, 128, 8, 128).transpose(0, 3, 2, 1)
    return np.ascontiguousarray(a).reshape(8, 128, 1024)


def _prep_wa(W):
    return np.ascontiguousarray(np.ascontiguousarray(W.T).reshape(8, 128, 1024))


def _masks():
    import ml_dtypes
    qi = np.arange(128)[:, None]
    kj = np.arange(256)[None, :]
    band = (kj >= qi) & (kj <= qi + 128)
    first = band & (kj >= 128)
    mb = np.where(band, 0.0, NEG).astype(np.float32)
    mf = np.where(first, 0.0, NEG).astype(np.float32)
    t2 = lambda m: np.ascontiguousarray(m.astype(ml_dtypes.bfloat16))
    return t2(mf), t2(mb)


def _tables(h, T_real):
    n_scr = HALO + T_real
    pos = np.maximum(h * T_real - HALO + np.arange(n_scr), 0).astype(np.float64)
    p = np.arange(128)
    j = p % 32
    inv = 10000.0 ** (-2.0 * j / 64.0)
    ang = pos[None, :] * inv[:, None]
    cos_t = np.cos(ang).astype(np.float32)
    sign = np.where((p % 64) < 32, -1.0, 1.0)
    sin_t = (np.sin(ang) * sign[:, None]).astype(np.float32)
    return cos_t, sin_t


def make_core_inputs(x, weights, b, h, T_real=4096):
    core = {}
    s0 = h * T_real
    core["x_main"] = np.ascontiguousarray(x[b, s0:s0 + T_real])
    if h == 0:
        core["x_halo"] = np.zeros((HALO + 1, D), np.float32)
    else:
        core["x_halo"] = np.ascontiguousarray(x[b, s0 - HALO - 1:s0])
    core["xrow1"] = np.ascontiguousarray(x[b, 1:2])
    core["brow_a"] = np.ascontiguousarray(np.asarray(weights["L0_attn_a_b"],
                                                     np.float32).reshape(1, D))
    core["wta"] = _prep_wa(np.asarray(weights["L0_attn_a_W"], np.float32))
    core["wt_r"] = _prep_w(np.asarray(weights["L0_ffn_W_r"], np.float32))
    core["wt_kf"] = _prep_w(np.asarray(weights["L0_ffn_W_k"], np.float32))
    core["wt_vf"] = _prep_w(np.asarray(weights["L0_ffn_W_v"], np.float32))
    core["wt_q"] = _prep_w(np.asarray(weights["L0_attn_b_Wq"], np.float32))
    core["wt_k"] = _prep_w(np.asarray(weights["L0_attn_b_Wk"], np.float32))
    core["wt_v"] = _prep_w(np.asarray(weights["L0_attn_b_Wv"], np.float32))
    core["wt_o"] = _prep_w(np.asarray(weights["L0_attn_b_Wo"], np.float32))
    rs = lambda v: np.ascontiguousarray(np.asarray(v, np.float32).reshape(8, 128).T)
    core["mufm_r"] = rs(weights["L0_ffn_mu_r"])
    core["mufm_k"] = rs(weights["L0_ffn_mu_k"])
    core["bfm_q"] = rs(weights["L0_attn_b_bq"])
    core["bfm_k"] = rs(weights["L0_attn_b_bk"])
    core["bfm_v"] = rs(weights["L0_attn_b_bv"])
    core["bfm_o"] = rs(weights["L0_attn_b_bo"])
    core["ident"] = np.eye(128, dtype=np.float32)
    mf, mb = _masks()
    core["mask_first"] = mf
    core["mask_band"] = mb
    core["mask_blk2"] = mf if h == 0 else mb
    core["gate"] = np.full((128, 8), 0.0 if h == 0 else 1.0, np.float32)
    cos_t, sin_t = _tables(h, T_real)
    core["cos_t"] = cos_t
    core["sin_t"] = sin_t
    return core


def kernel(**inputs):
    x = np.asarray(inputs["L0_attn_a_x"], np.float32)
    B, S, _ = x.shape
    T_real = S // 2
    nc = build_program(T_real=T_real, ST_real=512)
    in_maps = [make_core_inputs(x, inputs, core // 2, core % 2, T_real)
               for core in range(8)]
    res = run_bass_kernel_spmd(nc, in_maps, core_ids=list(range(8)))
    out = np.empty((B, S, D), np.float32)
    for core in range(8):
        b, h = core // 2, core % 2
        out[b, h * T_real:(h + 1) * T_real] = res.results[core]["y"]
    return out


if __name__ == "__main__":
    import reference
    ins = {k: np.asarray(v) for k, v in reference.setup_inputs().items()}
    out = kernel(**ins)
    ref = np.asarray(reference.reference(**ins))
    print("rel:", np.linalg.norm(out - ref) / np.linalg.norm(ref))


# version 28 (final)
# speedup vs baseline: 1.0402x; 1.0316x over previous
import numpy as np
from contextlib import ExitStack

import concourse.bass as bass
import concourse.mybir as mybir
from concourse.bass_utils import run_bass_kernel_spmd

F32 = mybir.dt.float32
F32R = mybir.dt.float32r
BF16 = mybir.dt.bfloat16
ALU = mybir.AluOpType
AF = mybir.ActivationFunctionType
AX = mybir.AxisListType

D = 1024
BLK = 128
CH = 256
EPS = 1e-5
NEG = -1e30
HALO = 256


class Sem:
    def __init__(self, h):
        self.h = h
        self.n = 0

    def bump(self, k=1):
        self.n += k
        return self.n


PF_ON = True


def build_program(T_real=4096, ST_real=512, trunc=None):
    assert T_real % ST_real == 0 and ST_real % CH == 0
    n_st = T_real // ST_real
    n_scr = HALO + T_real
    STW = ST_real + HALO

    nc = bass.Bass("TRN2", target_bir_lowering=False)

    din = lambda name, shape, dt=F32: nc.dram_tensor(name, shape, dt, kind="ExternalInput").ap()
    x_main = din("x_main", [T_real, D])
    x_halo = din("x_halo", [HALO + 1, D])
    xrow1 = din("xrow1", [1, D])
    brow_a = din("brow_a", [1, D])
    wt_d = {k: din("wt_" + k, [8, 128, 1024]) for k in ["r", "kf", "vf", "q", "k", "v", "o"]}
    wta_d = din("wta", [8, 128, 1024])
    mufm_r_d = din("mufm_r", [128, 8])
    mufm_k_d = din("mufm_k", [128, 8])
    bfm_d = {k: din("bfm_" + k, [128, 8]) for k in ["q", "k", "v", "o"]}
    ident_d = din("ident", [128, 128])
    mask_d = {k: din("mask_" + k, [128, 256], BF16) for k in ["first", "band", "blk2"]}
    gate_d = din("gate", [128, 8])
    cos_d = din("cos_t", [128, n_scr])
    sin_d = din("sin_t", [128, n_scr])

    y_out = nc.dram_tensor("y", [T_real, D], F32, kind="ExternalOutput").ap()
    wt_hl = nc.dram_tensor("wt_hl", [7, 8, 128, 2048], F32R, kind="Internal").ap()
    x2s = nc.dram_tensor("x2s", [n_scr, D], F32, kind="Internal").ap()
    x3s = nc.dram_tensor("x3s", [n_scr, D], F32, kind="Internal").ap()

    TE, VE, SE, GE, SY = nc.tensor, nc.vector, nc.scalar, nc.gpsimd, nc.sync

    with ExitStack() as ectx:
        sb = lambda name, shape, dt=F32: ectx.enter_context(nc.sbuf_tensor(name, shape, dt))
        psal = lambda name, shape: ectx.enter_context(nc.psum_tensor(name, shape, F32))

        AR0 = sb("AR0", [128, 8 * STW])
        AR2 = sb("AR2", [128, 8 * STW])
        ARL = sb("ARL", [128, 8 * (STW + 128)])
        SPH = sb("SPH", [128, 8 * STW], F32R)
        SPL = sb("SPL", [128, 8 * STW], F32R)
        wsl = sb("wsl", [128, 2 * 1024])
        wsl2 = sb("wsl2", [128, 3 * 2048], F32R)
        xt = [sb("xt0", [128, 2048]), sb("xt1", [128, 2048])]
        st_ = [sb("st0", [128, 2048]), sb("st1", [128, 2048])]
        ctm = sb("ctm", [128, 1024])
        vcarry = sb("vcarry", [128, 1024])
        kcarry = sb("kcarry", [128, 1024])
        y1b = sb("y1b", [128, 1024])
        masks = {k: sb("m_" + k, [128, 256], BF16) for k in ["first", "band", "blk2"]}
        ident = sb("identt", [128, 128])
        gate = sb("gatet", [128, 8])
        mur = sb("mur", [128, 8])
        muk = sb("muk", [128, 8])
        bias_fm = {k: sb("b_" + k, [128, 8]) for k in ["q", "k", "v", "o"]}
        lncol2 = sb("lncol2", [128, 8])
        lncol4 = sb("lncol4", [128, 8])
        l1fm = sb("l1fm", [128, 8])
        stats = sb("statst", [128, 48])
        mv = sb("mvt", [128, 8])
        rstd = sb("rstdt", [128, 4])
        nbias = sb("nbiast", [128, 4])
        nmax = sb("nmaxt", [128, 16])
        rsum = sb("rsumt", [128, 16])
        recip = sb("recipt", [128, 16])
        rowa = ctm[0:1, 0:1024]
        rowb = AR0[0:1, 0:1024]
        rowc = AR2[0:1, 0:1024]
        rowd = st_[0][0:1, 0:1024]
        one1 = sb("one1", [1, 128])
        cosT = wsl[:, 0:STW]
        sinT = wsl[:, STW:2 * STW]
        stat1 = sb("stat1", [1, 12])
        mv1 = sb("mv1", [1, 2])
        rst1 = sb("rst1", [1, 2])

        psA = psal("psA", [128, 1024])
        psB = psal("psB", [128, 1024])
        psC = psal("psC", [128, 1024])
        pb = [psA[:, 0:512], psA[:, 512:1024], psB[:, 0:512], psB[:, 512:1024],
              psC[:, 0:512], psC[:, 512:1024]]
        pc = psal("pc", [128, 1024])
        ps_of = {0: psA, 1: psA, 2: psB, 3: psB, 4: psC, 5: psC}

        GD = Sem(ectx.enter_context(nc.semaphore("GD")))
        SV = Sem(ectx.enter_context(nc.semaphore("SV")))
        SS = Sem(ectx.enter_context(nc.semaphore("SS")))
        TT = Sem(ectx.enter_context(nc.semaphore("TT")))
        SG = Sem(ectx.enter_context(nc.semaphore("SG")))
        SM = Sem(ectx.enter_context(nc.semaphore("SM")))
        W0 = Sem(ectx.enter_context(nc.semaphore("W0")))
        W1 = Sem(ectx.enter_context(nc.semaphore("W1")))
        X0 = Sem(ectx.enter_context(nc.semaphore("X0")))
        X1 = Sem(ectx.enter_context(nc.semaphore("X1")))
        TB = Sem(ectx.enter_context(nc.semaphore("TB")))
        WSEM = [W0, W1]
        XSEM = [X0, X1]
        GDS = [Sem(ectx.enter_context(nc.semaphore(f"GDS{i}"))) for i in range(4)]
        WSEM2 = [Sem(ectx.enter_context(nc.semaphore(f"WS2{i}"))) for i in range(3)]

        v8 = lambda buf, w: buf[:].rearrange("p (f c) -> p f c", f=8)
        a0, a2 = v8(AR0, STW), v8(AR2, STW)
        aL = v8(ARL, STW + 128)
        sph = v8(SPH, STW)
        spl = v8(SPL, STW)
        sphf = SPH[:].bitcast(F32).rearrange("p (f c) -> p f c", f=8)
        vcv = v8(vcarry, 128)
        kcv = v8(kcarry, 128)
        wslot = [wsl[:, 0:1024], wsl[:, 1024:2048]]
        wslot2 = [wsl2[:, 0:2048], wsl2[:, 2048:4096], wsl2[:, 4096:6144]]
        WIDX = {"r": 0, "kf": 1, "vf": 2, "q": 3, "k": 4, "v": 5, "o": 6}

        guards = {}

        def g_wait(eng, *keys):
            for key in keys:
                for s, n in guards.pop(key, []):
                    eng.wait_ge(s.h, n)

        def g_set(key, sem, n=None):
            guards.setdefault(key, []).append((sem, n if n is not None else sem.n))

        def sy_dma(out, in_, sem):
            SY.dma_start(out=out, in_=in_).then_inc(sem.h, 16)
            return sem.bump(16)

        def ge_dma(out, in_):
            GE.dma_start(out=out, in_=in_).then_inc(GD.h, 16)
            return GD.bump(16)

        def ge_fence():
            GE.wait_ge(GD.h, GD.n)
            GE.sem_inc(SG.h, 1)
            return SG.bump()

        def bc(ap, reps):
            return ap.unsqueeze(1).broadcast_to([ap.shape[0], reps, ap.shape[1]])

        def scr_rows(t, r0):
            return t[r0:r0 + 256, :].rearrange("(b p) d -> p b d", p=128)

        def tile_v(tile):
            return tile[:].rearrange("p (b d) -> p b d", b=2)

        sy_dma(rowa, xrow1, SM)
        wta_bufs = [wslot[0][:], wslot[1][:],
                    st_[1][:, 0:1024], st_[1][:, 1024:2048], st_[0][:, 1024:2048],
                    AR0[:, 1024:2048], AR0[:, 2048:3072], AR2[:, 1024:2048]]
        wta_sems = [W0, W1, X0, X1, TB, WSEM2[0], WSEM2[1], WSEM2[2]]
        wrd = {kk: sy_dma(wta_bufs[kk], wta_d[kk], wta_sems[kk]) for kk in range(2)}
        for dst, src in [(masks["first"][:], mask_d["first"]), (masks["band"][:], mask_d["band"]),
                         (masks["blk2"][:], mask_d["blk2"]), (ident[:], ident_d),
                         (gate[:], gate_d), (mur[:], mufm_r_d), (muk[:], mufm_k_d),
                         (bias_fm["q"][:], bfm_d["q"]), (bias_fm["k"][:], bfm_d["k"]),
                         (bias_fm["v"][:], bfm_d["v"]), (bias_fm["o"][:], bfm_d["o"]),
                         (rowb, brow_a), (rowc, x_halo[0:1, :])]:
            sy_dma(dst, src, SM)
        small_done = SM.n

        GE.memset(kcarry[:], 0.0)
        GE.memset(vcarry[:], 0.0)
        GE.memset(one1[:], 1.0).then_inc(SG.h, 1)
        SG.bump()

        def ln_row(src_row, dst_row):
            VE.wait_ge(SS.h, SS.n)
            VE.bn_stats(stat1[:, 0:6], src_row[0:1, 0:512])
            VE.bn_stats(stat1[:, 6:12], src_row[0:1, 512:1024])
            VE.drain()
            VE.bn_aggr(mv1[:], stat1[:])
            VE.drain()
            VE.tensor_scalar(rst1[:, 0:1], mv1[:, 1:2], EPS, None, ALU.add)
            VE.drain().then_inc(SV.h, 1)
            SV.bump()
            SE.wait_ge(SV.h, SV.n)
            SE.activation(rst1[:, 0:1], rst1[:, 0:1], AF.Ln)
            SE.drain()
            SE.activation(rst1[:, 0:1], rst1[:, 0:1], AF.Exp, scale=-0.5).then_inc(SS.h, 1)
            SS.bump()
            VE.wait_ge(SS.h, SS.n)
            VE.scalar_tensor_tensor(rst1[:, 1:2], mv1[:, 0:1], -1.0, rst1[:, 0:1],
                                    ALU.mult, ALU.mult).then_inc(SV.h, 1)
            SV.bump()
            SE.wait_ge(SV.h, SV.n)
            SE.activation(dst_row[0:1, :], src_row[0:1, :], AF.Identity,
                          bias=rst1[:, 1:2], scale=rst1[:, 0:1]).then_inc(SS.h, 1)
            SS.bump()

        def row_to_fm(src_row, dst_col):
            g_wait(TE, "pb5")
            TE.wait_ge(SS.h, SS.n)
            TE.wait_ge(SG.h, SG.n)
            last = None
            for fc in range(8):
                last = TE.transpose(pb[5][:, fc:fc + 1],
                                    src_row[0:1, fc * 128:(fc + 1) * 128], one1[0:1, 0:1])
            last.then_inc(TT.h, 1)
            TT.bump()
            VE.wait_ge(TT.h, TT.n)
            VE.tensor_copy(dst_col, pb[5][:, 0:8]).then_inc(SV.h, 1)
            SV.bump()
            g_set("pb5", SV)

        VE.wait_ge(SM.h, small_done)
        ln_row(rowa, rowd)
        row_to_fm(rowd, l1fm[:])
        for kk in range(2, 8):
            wrd[kk] = sy_dma(wta_bufs[kk], wta_d[kk], wta_sems[kk])
        for kk in range(8):
            TE.wait_ge(wta_sems[kk].h, wrd[kk])
            if kk == 0:
                TE.wait_ge(SV.h, SV.n)
                g_wait(TE, "pb0", "pb1")
            last = None
            for dh in range(2):
                last = TE.matmul(pb[dh][0:1, 0:512], l1fm[:, kk:kk + 1],
                                 wta_bufs[kk][:, dh * 512:(dh + 1) * 512],
                                 start=(kk == 0), stop=False)
            last.then_inc(TT.h, 1)
            TT.bump()
        last = None
        for dh in range(2):
            last = TE.matmul(pb[dh][0:1, 0:512], one1[0:1, 0:1],
                             rowb[0:1, dh * 512:(dh + 1) * 512], start=False, stop=True)
        last.then_inc(TT.h, 1)
        TT.bump()
        g_set("wsl0", TT)
        g_set("wsl1", TT)
        g_set("st1", TT)
        g_set("st0", TT)
        VE.wait_ge(TT.h, TT.n)
        VE.tensor_copy(rowd[0:1, 0:512], pb[0][0:1, 0:512])
        VE.tensor_copy(rowd[0:1, 512:1024], pb[1][0:1, 0:512])
        VE.drain().then_inc(SV.h, 1)
        SV.bump()
        g_set("pb0", SV)
        g_set("pb1", SV)
        g_wait(TE, "pc")
        TE.wait_ge(SV.h, SV.n)
        last = None
        for dh in range(2):
            last = TE.matmul(pc[:, dh * 512:(dh + 1) * 512], one1[0:1, :],
                             rowd[0:1, dh * 512:(dh + 1) * 512], start=True, stop=True)
        last.then_inc(TT.h, 1)
        TT.bump()
        VE.wait_ge(TT.h, TT.n)
        VE.tensor_copy(y1b[:], pc[:]).then_inc(SV.h, 1)
        SV.bump()
        g_set("pc", SV)
        y1b_ready = SV.n

        VE.tensor_tensor(rowa, rowc, rowd, ALU.add)
        VE.drain().then_inc(SV.h, 1)
        SV.bump()
        g_set("st0", SV)
        ln_row(rowa, rowc)
        row_to_fm(rowc, lncol2[:])
        g_set("AR2", TT)
        g_set("AR0", TT)

        wt_ready = {}
        _stgs = [(st_[0], "st0"), (st_[1], "st1"), (xt[0], "xt0"), (xt[1], "xt1")]
        _store_cnt = []
        _prep_it = [0]
        _prep_done = {}

        def prep_iter(wkey, m):
            wi = WIDX[wkey]
            it = _prep_it[0]
            _prep_it[0] += 1
            slot = it % 2
            g_wait(SY, f"wsl{slot}")
            wr = sy_dma(wslot[slot][:], wt_d[wkey][m], WSEM[slot])
            stg, stgk = _stgs[it % 4]
            g_wait(SE, stgk)
            SE.wait_ge(WSEM[slot].h, wr)
            SE.drain()
            SE.activation(stg[:, 0:1024].bitcast(F32R), wslot[slot][:],
                          AF.Identity).then_inc(SS.h, 1)
            SS.bump()
            VE.wait_ge(SS.h, SS.n)
            VE.tensor_tensor(stg[:, 1024:2048].bitcast(F32R), wslot[slot][:],
                             stg[:, 0:1024], ALU.subtract)
            VE.drain().then_inc(SV.h, 1)
            SV.bump()
            g_set(f"wsl{slot}", SV)
            GE.wait_ge(SV.h, SV.n)
            gsem = GDS[it % 4]
            GE.dma_start(out=wt_hl[wi, m], in_=stg[:].bitcast(F32R)).then_inc(gsem.h, 16)
            gsem.bump(16)
            _store_cnt.append((stgk, gsem, gsem.n))
            if len(_store_cnt) >= 4:
                pk, ps, pn = _store_cnt[-4]
                GE.wait_ge(ps.h, pn)
                GE.sem_inc(SG.h, 1)
                g_set(pk, SG, SG.bump())
            _prep_done[wkey] = _prep_done.get(wkey, 0) + 1

        def prep_fin(wkey):
            for gs in GDS:
                GE.wait_ge(gs.h, gs.n)
            GE.sem_inc(SG.h, 1)
            f = SG.bump()
            for pk, _, _ in _store_cnt[-3:]:
                g_set(pk, SG, f)
            wt_ready[wkey] = f

        def prep_weight(wkey):
            if wkey in wt_ready:
                return
            for m in range(_prep_done.get(wkey, 0), 8):
                prep_iter(wkey, m)
            prep_fin(wkey)

        prep_weight("r")

        ln_lock = [SS.n, SS.n]

        def ln_chunk(xtile, lnt, par):
            so = par * 24
            mo = par * 4
            VE.wait_ge(SS.h, ln_lock[par])
            for b2 in range(2):
                VE.bn_stats(stats[:, so + b2 * 12:so + b2 * 12 + 6],
                            xtile[:, b2 * 1024:b2 * 1024 + 512])
                VE.bn_stats(stats[:, so + b2 * 12 + 6:so + b2 * 12 + 12],
                            xtile[:, b2 * 1024 + 512:(b2 + 1) * 1024])
            VE.drain()
            for b2 in range(2):
                VE.bn_aggr(mv[:, mo + b2 * 2:mo + (b2 + 1) * 2],
                           stats[:, so + b2 * 12:so + (b2 + 1) * 12])
            VE.drain()
            mvv = mv[:, mo:mo + 4].rearrange("p (b t) -> p b t", t=2)
            VE.tensor_scalar(rstd[:, 2 * par:2 * par + 2], mvv[:, :, 1], EPS, None,
                             ALU.add)
            VE.drain().then_inc(SV.h, 1)
            SV.bump()
            SE.wait_ge(SV.h, SV.n)
            SE.activation(rstd[:, 2 * par:2 * par + 2], rstd[:, 2 * par:2 * par + 2],
                          AF.Ln)
            SE.drain()
            SE.activation(rstd[:, 2 * par:2 * par + 2], rstd[:, 2 * par:2 * par + 2],
                          AF.Exp, scale=-0.5).then_inc(SS.h, 1)
            SS.bump()
            VE.wait_ge(SS.h, SS.n)
            VE.scalar_tensor_tensor(nbias[:, 2 * par:2 * par + 2], mvv[:, :, 0], -1.0,
                                    rstd[:, 2 * par:2 * par + 2],
                                    ALU.mult, ALU.mult).then_inc(SV.h, 1)
            SV.bump()
            SE.wait_ge(SV.h, SV.n)
            ns = []
            for b2 in range(2):
                SE.activation(lnt[:, b2 * 1024:(b2 + 1) * 1024],
                              xtile[:, b2 * 1024:(b2 + 1) * 1024], AF.Identity,
                              bias=nbias[:, 2 * par + b2:2 * par + b2 + 1],
                              scale=rstd[:, 2 * par + b2:2 * par + b2 + 1]
                              ).then_inc(SS.h, 1)
                SS.bump()
                ns.append(SS.n)
            ln_lock[par] = SS.n
            return tuple(ns)

        xp_par = [0]

        def xpose_tile(src_tile, src_off, n_tok, dst_view, dst_c0, rsem, rn, stride=None):
            out_n = 0
            for b2 in range(n_tok // 128):
                rb2 = rn[b2] if isinstance(rn, tuple) else rn
                for half in range(2):
                    par = xp_par[0]
                    xp_par[0] ^= 1
                    bank = pb[2 + par]
                    g_wait(TE, f"pb{2 + par}")
                    TE.wait_ge(rsem.h, rb2)
                    last = None
                    for j in range(4):
                        fc = half * 4 + j
                        o = src_off + b2 * 1024 + fc * 128
                        last = TE.transpose(bank[:, j * 128:(j + 1) * 128],
                                            src_tile[:, o:o + 128], ident[:])
                    last.then_inc(TT.h, 1)
                    TT.bump()
                    VE.wait_ge(TT.h, TT.n)
                    pbv = bank[:].rearrange("p (f c) -> p f c", f=4)
                    VE.tensor_copy(dst_view[:, half * 4:half * 4 + 4,
                                            dst_c0 + b2 * 128:dst_c0 + (b2 + 1) * 128],
                                   pbv).then_inc(SV.h, 1)
                    SV.bump()
                    g_set(f"pb{2 + par}", SV)
                    out_n = SV.n
            return out_n

        pair_busy = [0]

        def xpose_split_tile(src_tile, dst_c0, rsem, rn):
            out_n = 0
            for half in range(2):
                par = xp_par[0]
                xp_par[0] ^= 1
                bank = pb[2 + par]
                g_wait(TE, f"pb{2 + par}")
                TE.wait_ge(rsem.h, rn)
                last = None
                for j in range(4):
                    fc = half * 4 + j
                    last = TE.transpose(bank[:, j * 128:(j + 1) * 128],
                                        src_tile[:, fc * 128:(fc + 1) * 128], ident[:])
                last.then_inc(TT.h, 1)
                TT.bump()
                VE.wait_ge(TT.h, TT.n)
                VE.wait_ge(TT.h, pair_busy[0])
                pbv = bank[:].rearrange("p (f c) -> p f c", f=4)
                VE.tensor_copy(sph[:, half * 4:half * 4 + 4, dst_c0:dst_c0 + 128], pbv)
                VE.drain()
                VE.tensor_tensor(spl[:, half * 4:half * 4 + 4, dst_c0:dst_c0 + 128], pbv,
                                 sphf[:, half * 4:half * 4 + 4, dst_c0:dst_c0 + 128],
                                 ALU.subtract)
                VE.drain().then_inc(SV.h, 1)
                SV.bump()
                g_set(f"pb{2 + par}", SV)
                out_n = SV.n
            return out_n

        def xpose_tile_pair(lnt, dst_c0, rsem, rn):
            out_n = 0
            for b2 in range(2):
                rb2 = rn[b2] if isinstance(rn, tuple) else rn
                for half in range(2):
                    par = xp_par[0]
                    xp_par[0] ^= 1
                    bank = pb[2 + par]
                    g_wait(TE, f"pb{2 + par}")
                    TE.wait_ge(rsem.h, rb2)
                    last = None
                    for j in range(4):
                        fc = half * 4 + j
                        last = TE.transpose(bank[:, j * 128:(j + 1) * 128],
                                            lnt[:, b2 * 1024 + fc * 128:
                                                b2 * 1024 + (fc + 1) * 128], ident[:])
                    last.then_inc(TT.h, 1)
                    TT.bump()
                    VE.wait_ge(TT.h, TT.n)
                    VE.wait_ge(TT.h, pair_busy[0])
                    pbv = bank[:].rearrange("p (f c) -> p f c", f=4)
                    d0 = dst_c0 + b2 * 128
                    VE.tensor_copy(sph[:, half * 4:half * 4 + 4, d0:d0 + 128], pbv)
                    VE.drain()
                    VE.tensor_tensor(spl[:, half * 4:half * 4 + 4, d0:d0 + 128], pbv,
                                     sphf[:, half * 4:half * 4 + 4, d0:d0 + 128],
                                     ALU.subtract)
                    VE.drain().then_inc(SV.h, 1)
                    SV.bump()
                    g_set(f"pb{2 + par}", SV)
                    out_n = SV.n
            return out_n

        def split(src_view, nrows, c_start, pre, per_chunk=None):
            out = {}
            first = True
            for c0, ln in ragged(nrows, c_start):
                for cc in range(c0, c0 + ln, CH):
                    if first:
                        SE.wait_ge(TT.h, pair_busy[0])
                    if per_chunk is not None:
                        if cc in per_chunk:
                            s, n = per_chunk[cc]
                            SE.wait_ge(s.h, n)
                    else:
                        for s, n in pre:
                            SE.wait_ge(s.h, n)
                    SE.drain()
                    SE.activation(sph[:, :, cc:cc + CH], src_view[:, :, cc:cc + CH],
                                  AF.Identity).then_inc(SS.h, 1)
                    SS.bump()
                    VE.wait_ge(SS.h, SS.n)
                    if first:
                        VE.wait_ge(TT.h, pair_busy[0])
                        first = False
                    VE.tensor_tensor(spl[:, :, cc:cc + CH], src_view[:, :, cc:cc + CH],
                                     sphf[:, :, cc:cc + CH], ALU.subtract)
                    VE.drain().then_inc(SV.h, 1)
                    SV.bump()
                out[c0] = SV.n
            return out

        def ragged(nrows, c_start):
            out = []
            c0 = c_start * CH
            while c0 < nrows:
                ln = min(512, nrows - c0)
                out.append((c0, ln))
                c0 += ln
            return out

        pre_wrd = {}

        def sweep_preload(wkey):
            if wkey in pre_wrd or wkey not in wt_ready:
                return
            wi = WIDX[wkey]
            d = {}
            for m in (0, 1, 2):
                g_wait(SY, f"ws2_{m}")
                SY.wait_ge(SG.h, wt_ready[wkey])
                d[m] = sy_dma(wslot2[m][:], wt_hl[wi, m], WSEM2[m])
            pre_wrd[wkey] = d

        def sweep3(wkey, nrows, c_start, evac, pre, pf=()):
            prep_weight(wkey)
            if not PF_ON:
                pf = ()
            pending = [(w, m) for w in pf if w not in wt_ready
                       for m in range(_prep_done.get(w, 0), 8)]
            wi = WIDX[wkey]
            chunks = ragged(nrows, c_start)
            if wkey in pre_wrd:
                wrd = pre_wrd.pop(wkey)
            else:
                wrd = {}
                for m in (0, 1, 2):
                    g_wait(SY, f"ws2_{m}")
                    SY.wait_ge(SG.h, wt_ready[wkey])
                    wrd[m] = sy_dma(wslot2[m][:], wt_hl[wi, m], WSEM2[m])
            first = True
            for m in range(8):
                TE.wait_ge(WSEM2[m % 3].h, wrd[m])
                if first:
                    if not isinstance(pre, dict):
                        for s, n in pre:
                            TE.wait_ge(s.h, n)
                    first = False
                for ci, (c0, ln) in enumerate(chunks):
                    par = (m * len(chunks) + ci) % 2
                    g_wait(TE, f"pb{par}")
                    if m == 0 and isinstance(pre, dict):
                        TE.wait_ge(SV.h, pre[c0])
                    last = None
                    nmm = 0
                    for k in range(8):
                        for wo, mvv in ((0, sph), (0, spl), (1024, sph)):
                            last = TE.matmul(pb[par][:, 0:ln],
                                             wslot2[m % 3][:, wo + k * 128:wo + (k + 1) * 128],
                                             mvv[:, k, c0:c0 + ln],
                                             start=(nmm == 0), stop=(nmm == 23))
                            nmm += 1
                    last.then_inc(TT.h, 1)
                    TT.bump()
                    if ci == len(chunks) - 1:
                        g_set(f"ws2_{m % 3}", TT)
                    evac(m, c0, ln, par, TT.n)
                for _ in range(2):
                    if pending:
                        w_, m_ = pending.pop(0)
                        prep_iter(w_, m_)
                        if m_ == 7:
                            prep_fin(w_)
                if m + 3 < 8:
                    g_wait(SY, f"ws2_{m % 3}")
                    wrd[m + 3] = sy_dma(wslot2[m % 3][:], wt_hl[wi, m + 3], WSEM2[m % 3])
            while pending:
                w_, m_ = pending.pop(0)
                prep_iter(w_, m_)
                if m_ == 7:
                    prep_fin(w_)
            pair_busy[0] = TT.n
            return

        def evac_s(dst_view, func, bias_t=None, off=0):
            def f(m, c0, ln, par, ttn):
                SE.wait_ge(TT.h, ttn)
                kw = {}
                if bias_t is not None:
                    kw = dict(bias=bias_t[:, m:m + 1], scale=1.0)
                SE.activation(dst_view[:, m, off + c0:off + c0 + ln],
                              pb[par][:, 0:ln], func, **kw).then_inc(SS.h, 1)
                SS.bump()
                g_set(f"pb{par}", SS)
            return f

        def evac_sigmoid(dst_view):
            def f(m, c0, ln, par, ttn):
                SE.wait_ge(TT.h, ttn)
                SE.activation(dst_view[:, m, c0:c0 + ln], pb[par][:, 0:ln],
                              AF.Tanh, scale=0.5).then_inc(SS.h, 1)
                SS.bump()
                g_set(f"pb{par}", SS)
                VE.wait_ge(SS.h, SS.n)
                VE.tensor_scalar(dst_view[:, m, c0:c0 + ln], dst_view[:, m, c0:c0 + ln],
                                 0.5, 0.5, ALU.mult, ALU.add)
                VE.drain().then_inc(SV.h, 1)
                SV.bump()
            return f

        def evac_relu_sq(dst_view):
            def f(m, c0, ln, par, ttn):
                SE.wait_ge(TT.h, ttn)
                SE.activation(dst_view[:, m, c0:c0 + ln], pb[par][:, 0:ln],
                              AF.Relu).then_inc(SS.h, 1)
                SS.bump()
                g_set(f"pb{par}", SS)
                VE.wait_ge(SS.h, SS.n)
                VE.tensor_tensor(dst_view[:, m, c0:c0 + ln],
                                 dst_view[:, m, c0:c0 + ln],
                                 dst_view[:, m, c0:c0 + ln], ALU.mult)
                VE.drain().then_inc(SV.h, 1)
                SV.bump()
            return f

        def evac_v_prod(dst_view, r_view):
            def f(m, c0, ln, par, ttn):
                VE.wait_ge(TT.h, ttn)
                VE.tensor_tensor(dst_view[:, m, c0:c0 + ln], pb[par][:, 0:ln],
                                 r_view[:, m, c0:c0 + ln], ALU.mult).then_inc(SV.h, 1)
                SV.bump()
                g_set(f"pb{par}", SV)
            return f

        def resid_store(src_raw, prod_sem, prod_n, src_fn, dst, dst_is_y,
                        sbase, chunks, add_y1b=False, resident=False):
            fences = {}
            for c in chunks:
                xti = xt[c % 2]
                if src_fn is not None:
                    g_wait(SY, f"xt{c % 2}")
                    ap, wn = src_fn(c)
                    if wn:
                        SY.wait_ge(SG.h, wn)
                    xrd = sy_dma(tile_v(xti), ap, XSEM[c % 2])
                else:
                    xrd = None
                if add_y1b:
                    if xrd is not None:
                        VE.wait_ge(XSEM[c % 2].h, xrd)
                    VE.wait_ge(SV.h, y1b_ready)
                    VE.tensor_tensor(tile_v(xti), tile_v(xti), bc(y1b[:], 2), ALU.add)
                    VE.drain()
                if dst_is_y:
                    g_wait(VE, f"st{c % 2}")
                    dtile = st_[c % 2]
                else:
                    dtile = xti
                for b2 in range(2):
                    for half in range(2):
                        par = xp_par[0]
                        xp_par[0] ^= 1
                        bank = pb[2 + par]
                        g_wait(TE, f"pb{2 + par}")
                        TE.wait_ge(prod_sem.h, prod_n)
                        last = None
                        for j in range(4):
                            fc = half * 4 + j
                            o = fc * STW + c * CH + b2 * 128
                            last = TE.transpose(bank[:, j * 128:(j + 1) * 128],
                                                src_raw[:, o:o + 128], ident[:])
                        last.then_inc(TT.h, 1)
                        TT.bump()
                        VE.wait_ge(TT.h, TT.n)
                        if xrd is not None:
                            VE.wait_ge(XSEM[c % 2].h, xrd)
                        VE.drain()
                        sl = xti[:, b2 * 1024 + half * 512:b2 * 1024 + (half + 1) * 512]
                        dl = dtile[:, b2 * 1024 + half * 512:b2 * 1024 + (half + 1) * 512]
                        VE.tensor_tensor(dl, sl, bank[:], ALU.add).then_inc(SV.h, 1)
                        SV.bump()
                        g_set(f"pb{2 + par}", SV)
                if dst_is_y:
                    g_set(f"xt{c % 2}", SV)
                    GE.wait_ge(SV.h, SV.n)
                    r0 = sbase + c * CH - HALO
                    ge_dma(y_out[r0:r0 + 256, :].rearrange("(b p) d -> p b d", p=128),
                           tile_v(dtile))
                    fences[c] = ge_fence()
                    g_set(f"st{c % 2}", SG, fences[c])
                elif resident:
                    fences[c] = SV.n
                else:
                    g_set(f"xt{c % 2}", SV)
                    GE.wait_ge(SV.h, SV.n)
                    ge_dma(scr_rows(dst, sbase + c * CH), tile_v(xti))
                    fences[c] = ge_fence()
                    g_set(f"xt{c % 2}", SG, fences[c])
            return fences

        x_written = {"x1": 0, "x2": 0, "x3": 0}
        sl_state = {"tabs": 0}

        def ffn_sublayer(st, sbase, nrows, which):
            is2 = which == "sl2"
            lnc = lncol2 if is2 else lncol4
            c_start = 1 if (st == 0 and not is2) else 0
            chunks = list(range(c_start, nrows // CH))

            sweep_preload("r")
            g_wait(VE, "ARL")
            VE.drain()
            VE.tensor_copy(aL[:, :, c_start * CH:c_start * CH + 1], lnc[:].unsqueeze(2))
            VE.drain().then_inc(SV.h, 1)
            SV.bump()

            lnfm_ready = {}
            mix_ready = {}
            res = nrows == ST_real
            for c in chunks:
                xti = xt[c % 2]
                if is2:
                    g_wait(SY, f"xt{c % 2}")
                    if st == 0 and c == 0:
                        src = x_halo[1:257, :].rearrange("(b p) d -> p b d", p=128)
                    else:
                        t0 = sbase + c * CH - HALO
                        src = x_main[t0:t0 + 256, :].rearrange("(b p) d -> p b d", p=128)
                    xrd = sy_dma(tile_v(xti), src, XSEM[c % 2])
                    VE.wait_ge(XSEM[c % 2].h, xrd)
                    VE.wait_ge(SV.h, y1b_ready)
                    VE.tensor_tensor(tile_v(xti), tile_v(xti), bc(y1b[:], 2), ALU.add)
                    VE.drain().then_inc(SV.h, 1)
                    SV.bump()
                elif res:
                    VE.drain()
                else:
                    g_wait(SY, f"xt{c % 2}")
                    x3f = x_written["x3"]
                    SY.wait_ge(SG.h, x3f[c] if isinstance(x3f, dict) else x3f)
                    xrd = sy_dma(tile_v(xti), scr_rows(x3s, sbase + c * CH), XSEM[c % 2])
                    VE.wait_ge(XSEM[c % 2].h, xrd)
                lnt = st_[c % 2]
                g_wait(SE, f"st{c % 2}")
                ss_apply = ln_chunk(xti, lnt, c % 2)
                g_set(f"xt{c % 2}", SS, ss_apply[1])
                lnfm_ready[c] = xpose_tile(lnt, 0, CH, aL, 1 + c * CH, SS, ss_apply)
                g_set(f"st{c % 2}", TT)
                if st == 0 and is2 and c == 1:
                    VE.tensor_tensor(aL[:, :, CH:CH + 1], aL[:, :, CH:CH + 1],
                                     gate[:].unsqueeze(2), ALU.mult)
                    VE.drain().then_inc(SV.h, 1)
                    SV.bump()
                dt_ = st_[c % 2]
                g_wait(VE, f"st{c % 2}")
                if c == chunks[0]:
                    g_wait(VE, "AR0", "AR2")
                dv = v8(dt_, 256)
                c0 = 1 + c * CH
                VE.drain()
                VE.tensor_tensor(dv, aL[:, :, c0:c0 + CH], aL[:, :, c0 - 1:c0 + CH - 1],
                                 ALU.subtract)
                VE.drain()
                for fc in range(8):
                    VE.scalar_tensor_tensor(a2[:, fc, c * CH:(c + 1) * CH], dv[:, fc, :],
                                            mur[:, fc:fc + 1], aL[:, fc, c0 - 1:c0 + CH - 1],
                                            ALU.mult, ALU.add)
                    VE.scalar_tensor_tensor(a0[:, fc, c * CH:(c + 1) * CH], dv[:, fc, :],
                                            muk[:, fc:fc + 1], aL[:, fc, c0 - 1:c0 + CH - 1],
                                            ALU.mult, ALU.add)
                VE.drain().then_inc(SV.h, 1)
                SV.bump()
                g_set(f"st{c % 2}", SV)
                mix_ready[c * CH] = (SV, SV.n)
            mix_done = SV.n
            VE.tensor_copy(lnc[:].unsqueeze(2), aL[:, :, nrows:nrows + 1])
            VE.drain().then_inc(SV.h, 1)
            SV.bump()
            g_set("ARL", SV)

            sxr = split(a2, nrows, c_start, None, per_chunk=mix_ready)
            g_wait(SE, "AR2")
            sweep3("r", nrows, c_start, evac_sigmoid(a2), sxr, pf=("kf", "vf"))
            sxk = split(a0, nrows, c_start, [(SV, mix_done)])
            g_wait(SE, "AR0")
            sweep3("kf", nrows, c_start, evac_relu_sq(a0), sxk, pf=("q", "k"))
            sq_done = SV.n
            ska = split(a0, nrows, c_start, [(SV, sq_done)])
            g_wait(VE, "AR0")
            sweep3("vf", nrows, c_start, evac_v_prod(a0, a2), ska, pf=("v", "o"))
            prod_done = SV.n
            g_set("AR2", SV, prod_done)

            if res:
                src_fn = None
            elif is2:
                def src_fn(c):
                    if st == 0 and c == 0:
                        return x_halo[1:257, :].rearrange("(b p) d -> p b d", p=128), None
                    t0 = sbase + c * CH - HALO
                    return x_main[t0:t0 + 256, :].rearrange("(b p) d -> p b d", p=128), None
            else:
                x3f = x_written["x3"]

                def src_fn(c):
                    return scr_rows(x3s, sbase + c * CH), (
                        x3f[c] if isinstance(x3f, dict) else x3f)
            cnt = resid_store(AR0, SV, prod_done, src_fn,
                              x2s if is2 else None, not is2, sbase, chunks,
                              add_y1b=(is2 and not res), resident=res)
            g_set("AR0", TT)
            if is2:
                x_written["x2"] = cnt

        def attn_sublayer(st, sbase, nrows):
            nblk = nrows // BLK
            chunks = list(range(nrows // CH))

            sweep_preload("q")

            ln3_ready = {}
            res = nrows == ST_real
            g_wait(VE, "AR0")
            for c in chunks:
                xti = xt[c % 2]
                if res:
                    VE.drain()
                else:
                    g_wait(SY, f"xt{c % 2}")
                    x2f = x_written["x2"]
                    SY.wait_ge(SG.h, x2f[c] if isinstance(x2f, dict) else x2f)
                    xrd = sy_dma(tile_v(xti), scr_rows(x2s, sbase + c * CH), XSEM[c % 2])
                    VE.wait_ge(XSEM[c % 2].h, xrd)
                lnt = st_[c % 2]
                g_wait(SE, f"st{c % 2}")
                ss_apply = ln_chunk(xti, lnt, c % 2)
                g_set(f"xt{c % 2}", SS, ss_apply[1])
                ln3_ready[c * CH] = xpose_tile_pair(lnt, c * CH, SS, ss_apply)
                g_set(f"st{c % 2}", TT)

            def rope(view, coff, ev_done):
                done = 0
                for c in chunks:
                    pt = st_[c % 2]
                    g_wait(GE, f"st{c % 2}")
                    GE.wait_ge(SS.h, ev_done)
                    pv = v8(pt, 256)
                    for dp, sp in [(0, 32), (32, 0), (64, 96), (96, 64)]:
                        ge_dma(pv[dp:dp + 32, :, :],
                               view[sp:sp + 32, :, coff + c * CH:coff + (c + 1) * CH])
                    pf = ge_fence()
                    VE.wait_ge(SG.h, pf)
                    VE.wait_ge(TB.h, sl_state["tabs"])
                    ch_sl = view[:, :, coff + c * CH:coff + (c + 1) * CH]
                    VE.tensor_tensor(ch_sl, ch_sl, bc(cosT[:, c * CH:(c + 1) * CH], 8),
                                     ALU.mult)
                    VE.drain()
                    VE.tensor_tensor(pv, pv, bc(sinT[:, c * CH:(c + 1) * CH], 8), ALU.mult)
                    VE.drain()
                    VE.tensor_tensor(ch_sl, ch_sl, pv, ALU.add)
                    VE.drain().then_inc(SV.h, 1)
                    SV.bump()
                    g_set(f"st{c % 2}", SV)
                    done = SV.n
                return done

            g_wait(SY, "tabs")
            if st == 0:
                g_wait(SY, "wsl0", "wsl1")
            sy_dma(cosT[:, 0:nrows], cos_d[:, sbase:sbase + nrows], TB)
            sl_state["tabs"] = sy_dma(sinT[:, 0:nrows], sin_d[:, sbase:sbase + nrows], TB)
            sl3 = {c0: max(v for k, v in ln3_ready.items() if k < c0 + ln)
                   for c0, ln in ragged(nrows, 0)}
            g_wait(SE, "AR0")
            sweep3("q", nrows, 0, evac_s(a0, AF.Identity, bias_fm["q"]), sl3)
            qrot_done = rope(a0, 0, SS.n)
            g_wait(SE, "ARL")
            sweep3("k", nrows, 0, evac_s(aL, AF.Identity, bias_fm["k"], off=128), sl3)
            krot_done = rope(aL, 128, SS.n)
            g_set("tabs", SV, krot_done)
            VE.wait_ge(SG.h, SG.n)
            VE.tensor_copy(aL[:, :, 0:128], kcv)
            VE.drain()
            VE.tensor_copy(kcv, aL[:, :, nrows:nrows + 128])
            VE.drain().then_inc(SV.h, 1)
            SV.bump()
            kready = SV.n
            g_wait(SE, "AR2")
            sweep3("v", nrows, 0, evac_s(a2, AF.Identity, bias_fm["v"]), sl3)
            v_ev = SS.n
            sweep_preload("o")

            scores_last = TT.n
            exp_prev = SS.n
            recip_prev = SV.n
            for n in range(nblk):
                TE.wait_ge(SS.h, v_ev)
                if n == 0:
                    TE.wait_ge(SV.h, max(qrot_done, krot_done, kready))
                for half in range(2):
                    bank = pb[2 + half]
                    g_wait(TE, f"pb{2 + half}")
                    last = None
                    for j in range(4):
                        fc = half * 4 + j
                        last = TE.transpose(bank[:, j * 128:(j + 1) * 128],
                                            a2[:, fc, n * 128:(n + 1) * 128], ident[:])
                    last.then_inc(TT.h, 1)
                    TT.bump()
                    VE.wait_ge(TT.h, TT.n)
                    VE.tensor_copy(a2[:, half * 4:half * 4 + 4, n * 128:(n + 1) * 128],
                                   bank[:].rearrange("p (f c) -> p f c", f=4))
                    VE.drain().then_inc(SV.h, 1)
                    SV.bump()
                    g_set(f"pb{2 + half}", SV)
                vtm_done = SV.n

                mk = ({0: "first", 1: "band", 2: "blk2"}.get(n, "band")
                      if st == 0 else "band")
                mt = masks[mk]

                pc_started = [False, False]

                def scores_mm(hp):
                    spar = hp % 2
                    sb0, sb1 = (4, 5) if spar == 0 else (0, 1)
                    g_wait(TE, f"pb{sb0}", f"pb{sb1}")
                    last = None
                    for j in range(2):
                        h = 2 * hp + j
                        fc, po = h // 2, (h % 2) * 64
                        q_ap = a0[po:po + 64, fc, n * 128:(n + 1) * 128]
                        k_ap = aL[po:po + 64, fc, n * 128:n * 128 + 256]
                        last = TE.matmul(pb[(sb0, sb1)[j]][:, 0:256], q_ap, k_ap,
                                         start=True, stop=True)
                    last.then_inc(TT.h, 1)
                    TT.bump()
                    return (sb0, sb1, TT.n)

                def mask_part(hp, meta):
                    sb0, sb1, sc_n = meta
                    spar = hp % 2
                    prs = st_[0][:, spar * 1024:spar * 1024 + 512]
                    prsv = prs.rearrange("p (j x) -> p j x", j=2)
                    psx = ps_of[sb0][:].rearrange("p (j x) -> p j x", j=2)[:, :, 0:256]
                    g_wait(VE, f"pr{spar}")
                    VE.wait_ge(TT.h, sc_n)
                    if hp == 0:
                        VE.wait_ge(SS.h, exp_prev)
                        VE.wait_ge(SM.h, small_done)
                    VE.tensor_tensor(prsv, psx, bc(mt[:], 2), ALU.add)
                    VE.drain()
                    VE.tensor_reduce(nmax[:, 2 * hp:2 * hp + 2], prsv,
                                     axis=AX.X, op=ALU.max, negate=True)
                    VE.drain()
                    VE.tensor_scalar(nmax[:, 2 * hp:2 * hp + 2], nmax[:, 2 * hp:2 * hp + 2],
                                     0.125, None, ALU.mult)
                    VE.drain().then_inc(SV.h, 1)
                    SV.bump()
                    g_set(f"pb{sb0}", SV)
                    g_set(f"pb{sb1}", SV)
                    return SV.n

                def pair_rest(hp, m_n):
                    spar = hp % 2
                    prs = st_[0][:, spar * 1024:spar * 1024 + 512]
                    SE.wait_ge(SV.h, m_n)
                    if hp == 0:
                        SE.wait_ge(SV.h, recip_prev)
                    last = None
                    for j in range(2):
                        h = 2 * hp + j
                        last = SE.activation(prs[:, j * 256:(j + 1) * 256],
                                             prs[:, j * 256:(j + 1) * 256], AF.Exp,
                                             bias=nmax[:, h:h + 1], scale=0.125,
                                             accum_out=rsum[:, h:h + 1])
                    last.then_inc(SS.h, 1)
                    SS.bump()
                    exp_n = SS.n
                    tpar = xp_par[0]
                    xp_par[0] ^= 1
                    tbank = pb[2 + tpar]
                    g_wait(TE, f"pb{2 + tpar}")
                    TE.wait_ge(SS.h, exp_n)
                    last = None
                    for j in range(2):
                        for kc in range(2):
                            last = TE.transpose(
                                tbank[:, (2 * j + kc) * 128:(2 * j + kc + 1) * 128],
                                prs[:, j * 256 + kc * 128:j * 256 + (kc + 1) * 128],
                                ident[:])
                    last.then_inc(TT.h, 1)
                    TT.bump()
                    g_set(f"pr{spar}", TT)
                    ats = st_[1][:, spar * 1024:spar * 1024 + 512]
                    g_wait(VE, f"at{spar}")
                    VE.wait_ge(TT.h, TT.n)
                    VE.tensor_copy(ats, tbank[:]).then_inc(SV.h, 1)
                    SV.bump()
                    g_set(f"pb{2 + tpar}", SV)
                    return ats, SV.n

                def pair_ctx(hp, fm):
                    nonlocal scores_last
                    ats, at_n = fm
                    spar = hp % 2
                    if hp == 0:
                        g_wait(TE, "pc")
                    TE.wait_ge(SV.h, max(at_n, vtm_done))
                    last = None
                    for j in range(2):
                        h = 2 * hp + j
                        fc, po = h // 2, (h % 2) * 64
                        bank_i = h // 8
                        for kc in range(2):
                            bv = n - 1 + kc
                            if bv < 0:
                                v_ap = vcv[:, fc, po:po + 64]
                            else:
                                v_ap = a2[:, fc, bv * 128 + po:bv * 128 + po + 64]
                            stt = not pc_started[bank_i]
                            pc_started[bank_i] = True
                            stop = (h % 8 == 7) and kc == 1
                            last = TE.matmul(pc[:, h * 64:(h + 1) * 64],
                                             ats[:, (2 * j + kc) * 128:(2 * j + kc + 1) * 128],
                                             v_ap, start=stt, stop=stop)
                    last.then_inc(TT.h, 1)
                    TT.bump()
                    g_set(f"at{spar}", TT)
                    scores_last = TT.n

                metas = {0: scores_mm(0), 1: scores_mm(1)}
                msv = {0: mask_part(0, metas.pop(0))}
                fms = {}
                for hp in range(8):
                    if hp + 1 < 8:
                        msv[hp + 1] = mask_part(hp + 1, metas.pop(hp + 1))
                    if hp + 2 < 8:
                        metas[hp + 2] = scores_mm(hp + 2)
                    if hp >= 1:
                        pair_ctx(hp - 1, fms.pop(hp - 1))
                    fms[hp] = pair_rest(hp, msv.pop(hp))
                pair_ctx(7, fms.pop(7))
                VE.wait_ge(SS.h, SS.n)
                VE.reciprocal(recip[:, 0:16], rsum[:, 0:16])
                VE.drain()
                VE.wait_ge(TT.h, scores_last)
                pcv = pc[:].rearrange("p (h e) -> p h e", h=16)
                rb = recip[:, 0:16].unsqueeze(2).broadcast_to([128, 16, 64])
                VE.tensor_tensor(ctm[:].rearrange("p (h e) -> p h e", h=16), pcv, rb,
                                 ALU.mult)
                VE.drain().then_inc(SV.h, 1)
                SV.bump()
                g_set("pc", SV)
                recip_prev = SV.n
                exp_prev = SS.n
                ctx_split = xpose_split_tile(ctm, n * 128, SV, SV.n)
            g_set("vc", TT, scores_last)
            g_wait(VE, "vc")
            VE.tensor_copy(vcv, a2[:, :, (nblk - 1) * 128:nblk * 128])
            VE.drain().then_inc(SV.h, 1)
            SV.bump()
            ctx_done = SV.n
            g_set("ARL", TT, scores_last)
            g_set("AR2", SV, ctx_done)
            g_set("AR2", TT, scores_last)
            g_set("AR0", TT, scores_last)

            g_wait(SE, "AR0")
            sweep3("o", nrows, 0, evac_s(a0, AF.Identity, bias_fm["o"]),
                  [(SV, ctx_split)])
            o_done = SS.n

            if res:
                src_fn = None
            else:
                x2f = x_written["x2"]

                def src_fn(c):
                    return scr_rows(x2s, sbase + c * CH), (
                        x2f[c] if isinstance(x2f, dict) else x2f)
            x_written["x3"] = resid_store(AR0, SS, o_done, src_fn,
                                          x3s, False, sbase, chunks, resident=res)
            g_set("AR0", TT)

        _ph = [0]

        def phase_ok():
            _ph[0] += 1
            return trunc is None or _ph[0] <= trunc

        for st in range(n_st):
            nrows = ST_real + (HALO if st == 0 else 0)
            sbase = 0 if st == 0 else HALO + st * ST_real
            if not phase_ok():
                break
            ffn_sublayer(st, sbase, nrows, "sl2")
            if not phase_ok():
                break
            attn_sublayer(st, sbase, nrows)
            if not phase_ok():
                break
            if st == 0:
                x3f = x_written["x3"]
                GE.wait_ge(SG.h, max(x3f.values()) if isinstance(x3f, dict) else x3f)
                ge_dma(rowa, x3s[255:256, :])
                rrd = ge_fence()
                VE.wait_ge(SG.h, rrd)
                g_wait(SE, "AR0")
                ln_row(rowa, rowb)
                row_to_fm(rowb, lncol4[:])
                g_set("AR0", TT)
                VE.drain()
                VE.tensor_tensor(lncol4[:], lncol4[:], gate[:], ALU.mult)
                VE.drain().then_inc(SV.h, 1)
                SV.bump()
            ffn_sublayer(st, sbase, nrows, "sl4")

        GE.wait_ge(GD.h, GD.n)

    return nc



def _prep_w(W):
    a = W.reshape(8, 128, 8, 128).transpose(0, 3, 2, 1)
    return np.ascontiguousarray(a).reshape(8, 128, 1024)


def _prep_wa(W):
    return np.ascontiguousarray(np.ascontiguousarray(W.T).reshape(8, 128, 1024))


def _masks():
    import ml_dtypes
    qi = np.arange(128)[:, None]
    kj = np.arange(256)[None, :]
    band = (kj >= qi) & (kj <= qi + 128)
    first = band & (kj >= 128)
    mb = np.where(band, 0.0, NEG).astype(np.float32)
    mf = np.where(first, 0.0, NEG).astype(np.float32)
    t2 = lambda m: np.ascontiguousarray(m.astype(ml_dtypes.bfloat16))
    return t2(mf), t2(mb)


def _tables(h, T_real):
    n_scr = HALO + T_real
    pos = np.maximum(h * T_real - HALO + np.arange(n_scr), 0).astype(np.float64)
    p = np.arange(128)
    j = p % 32
    inv = 10000.0 ** (-2.0 * j / 64.0)
    ang = pos[None, :] * inv[:, None]
    cos_t = np.cos(ang).astype(np.float32)
    sign = np.where((p % 64) < 32, -1.0, 1.0)
    sin_t = (np.sin(ang) * sign[:, None]).astype(np.float32)
    return cos_t, sin_t


def make_core_inputs(x, weights, b, h, T_real=4096):
    core = {}
    s0 = h * T_real
    core["x_main"] = np.ascontiguousarray(x[b, s0:s0 + T_real])
    if h == 0:
        core["x_halo"] = np.zeros((HALO + 1, D), np.float32)
    else:
        core["x_halo"] = np.ascontiguousarray(x[b, s0 - HALO - 1:s0])
    core["xrow1"] = np.ascontiguousarray(x[b, 1:2])
    core["brow_a"] = np.ascontiguousarray(np.asarray(weights["L0_attn_a_b"],
                                                     np.float32).reshape(1, D))
    core["wta"] = _prep_wa(np.asarray(weights["L0_attn_a_W"], np.float32))
    core["wt_r"] = _prep_w(np.asarray(weights["L0_ffn_W_r"], np.float32))
    core["wt_kf"] = _prep_w(np.asarray(weights["L0_ffn_W_k"], np.float32))
    core["wt_vf"] = _prep_w(np.asarray(weights["L0_ffn_W_v"], np.float32))
    core["wt_q"] = _prep_w(np.asarray(weights["L0_attn_b_Wq"], np.float32))
    core["wt_k"] = _prep_w(np.asarray(weights["L0_attn_b_Wk"], np.float32))
    core["wt_v"] = _prep_w(np.asarray(weights["L0_attn_b_Wv"], np.float32))
    core["wt_o"] = _prep_w(np.asarray(weights["L0_attn_b_Wo"], np.float32))
    rs = lambda v: np.ascontiguousarray(np.asarray(v, np.float32).reshape(8, 128).T)
    core["mufm_r"] = rs(weights["L0_ffn_mu_r"])
    core["mufm_k"] = rs(weights["L0_ffn_mu_k"])
    core["bfm_q"] = rs(weights["L0_attn_b_bq"])
    core["bfm_k"] = rs(weights["L0_attn_b_bk"])
    core["bfm_v"] = rs(weights["L0_attn_b_bv"])
    core["bfm_o"] = rs(weights["L0_attn_b_bo"])
    core["ident"] = np.eye(128, dtype=np.float32)
    mf, mb = _masks()
    core["mask_first"] = mf
    core["mask_band"] = mb
    core["mask_blk2"] = mf if h == 0 else mb
    core["gate"] = np.full((128, 8), 0.0 if h == 0 else 1.0, np.float32)
    cos_t, sin_t = _tables(h, T_real)
    core["cos_t"] = cos_t
    core["sin_t"] = sin_t
    return core


def kernel(**inputs):
    x = np.asarray(inputs["L0_attn_a_x"], np.float32)
    B, S, _ = x.shape
    T_real = S // 2
    nc = build_program(T_real=T_real, ST_real=512)
    in_maps = [make_core_inputs(x, inputs, core // 2, core % 2, T_real)
               for core in range(8)]
    res = run_bass_kernel_spmd(nc, in_maps, core_ids=list(range(8)))
    out = np.empty((B, S, D), np.float32)
    for core in range(8):
        b, h = core // 2, core % 2
        out[b, h * T_real:(h + 1) * T_real] = res.results[core]["y"]
    return out


if __name__ == "__main__":
    import reference
    ins = {k: np.asarray(v) for k, v in reference.setup_inputs().items()}
    out = kernel(**ins)
    ref = np.asarray(reference.reference(**ins))
    print("rel:", np.linalg.norm(out - ref) / np.linalg.norm(ref))


# version 29
# speedup vs baseline: 1.0707x; 1.0294x over previous
import numpy as np
from contextlib import ExitStack

import concourse.bass as bass
import concourse.mybir as mybir
from concourse.bass_utils import run_bass_kernel_spmd

F32 = mybir.dt.float32
F32R = mybir.dt.float32r
BF16 = mybir.dt.bfloat16
ALU = mybir.AluOpType
AF = mybir.ActivationFunctionType
AX = mybir.AxisListType

D = 1024
BLK = 128
CH = 256
EPS = 1e-5
NEG = -1e30
HALO = 256


class Sem:
    def __init__(self, h):
        self.h = h
        self.n = 0

    def bump(self, k=1):
        self.n += k
        return self.n


PF_ON = True


def build_program(T_real=4096, ST_real=512, trunc=None):
    assert T_real % ST_real == 0 and ST_real % CH == 0
    n_st = T_real // ST_real
    n_scr = HALO + T_real
    STW = ST_real + HALO

    nc = bass.Bass("TRN2", target_bir_lowering=False)

    din = lambda name, shape, dt=F32: nc.dram_tensor(name, shape, dt, kind="ExternalInput").ap()
    x_main = din("x_main", [T_real, D])
    x_halo = din("x_halo", [HALO + 1, D])
    xrow1 = din("xrow1", [1, D])
    brow_a = din("brow_a", [1, D])
    wt_d = {k: din("wt_" + k, [8, 128, 1024]) for k in ["r", "kf", "vf", "q", "k", "v", "o"]}
    wta_d = din("wta", [8, 128, 1024])
    mufm_r_d = din("mufm_r", [128, 8])
    mufm_k_d = din("mufm_k", [128, 8])
    bfm_d = {k: din("bfm_" + k, [128, 8]) for k in ["q", "k", "v", "o"]}
    ident_d = din("ident", [128, 128])
    mask_d = {k: din("mask_" + k, [128, 256], BF16) for k in ["first", "band", "blk2"]}
    gate_d = din("gate", [128, 8])
    cos_d = din("cos_t", [128, n_scr])
    sin_d = din("sin_t", [128, n_scr])

    y_out = nc.dram_tensor("y", [T_real, D], F32, kind="ExternalOutput").ap()
    wt_hl = nc.dram_tensor("wt_hl", [7, 8, 128, 2048], F32R, kind="Internal").ap()
    x2s = nc.dram_tensor("x2s", [n_scr, D], F32, kind="Internal").ap()
    x3s = nc.dram_tensor("x3s", [n_scr, D], F32, kind="Internal").ap()

    TE, VE, SE, GE, SY = nc.tensor, nc.vector, nc.scalar, nc.gpsimd, nc.sync

    with ExitStack() as ectx:
        sb = lambda name, shape, dt=F32: ectx.enter_context(nc.sbuf_tensor(name, shape, dt))
        psal = lambda name, shape: ectx.enter_context(nc.psum_tensor(name, shape, F32))

        AR0 = sb("AR0", [128, 8 * STW])
        AR2 = sb("AR2", [128, 8 * STW])
        ARL = sb("ARL", [128, 8 * (STW + 128)])
        SPH = sb("SPH", [128, 8 * STW], F32R)
        SPL = sb("SPL", [128, 8 * STW], F32R)
        wsl = sb("wsl", [128, 2 * 1024])
        wsl2 = sb("wsl2", [128, 3 * 2048], F32R)
        xt = [sb("xt0", [128, 2048]), sb("xt1", [128, 2048])]
        st_ = [sb("st0", [128, 2048]), sb("st1", [128, 2048])]
        ctm = sb("ctm", [128, 1024])
        vcarry = sb("vcarry", [128, 1024])
        kcarry = sb("kcarry", [128, 1024])
        y1b = sb("y1b", [128, 1024])
        masks = {k: sb("m_" + k, [128, 256], BF16) for k in ["first", "band", "blk2"]}
        ident = sb("identt", [128, 128])
        gate = sb("gatet", [128, 8])
        mur = sb("mur", [128, 8])
        muk = sb("muk", [128, 8])
        bias_fm = {k: sb("b_" + k, [128, 8]) for k in ["q", "k", "v", "o"]}
        lncol2 = sb("lncol2", [128, 8])
        lncol4 = sb("lncol4", [128, 8])
        l1fm = sb("l1fm", [128, 8])
        stats = sb("statst", [128, 48])
        mv = sb("mvt", [128, 8])
        rstd = sb("rstdt", [128, 4])
        nbias = sb("nbiast", [128, 4])
        nmax = sb("nmaxt", [128, 16])
        rsum = sb("rsumt", [128, 16])
        recip = sb("recipt", [128, 16])
        rowa = ctm[0:1, 0:1024]
        rowb = AR0[0:1, 0:1024]
        rowc = AR2[0:1, 0:1024]
        rowd = st_[0][0:1, 0:1024]
        one1 = sb("one1", [1, 128])
        cosT = wsl[:, 0:STW]
        sinT = wsl[:, STW:2 * STW]
        stat1 = sb("stat1", [1, 12])
        mv1 = sb("mv1", [1, 2])
        rst1 = sb("rst1", [1, 2])

        psA = psal("psA", [128, 1024])
        psB = psal("psB", [128, 1024])
        psC = psal("psC", [128, 1024])
        pb = [psA[:, 0:512], psA[:, 512:1024], psB[:, 0:512], psB[:, 512:1024],
              psC[:, 0:512], psC[:, 512:1024]]
        pc = psal("pc", [128, 1024])
        ps_of = {0: psA, 1: psA, 2: psB, 3: psB, 4: psC, 5: psC}

        GD = Sem(ectx.enter_context(nc.semaphore("GD")))
        SV = Sem(ectx.enter_context(nc.semaphore("SV")))
        SS = Sem(ectx.enter_context(nc.semaphore("SS")))
        TT = Sem(ectx.enter_context(nc.semaphore("TT")))
        SG = Sem(ectx.enter_context(nc.semaphore("SG")))
        SM = Sem(ectx.enter_context(nc.semaphore("SM")))
        W0 = Sem(ectx.enter_context(nc.semaphore("W0")))
        W1 = Sem(ectx.enter_context(nc.semaphore("W1")))
        X0 = Sem(ectx.enter_context(nc.semaphore("X0")))
        X1 = Sem(ectx.enter_context(nc.semaphore("X1")))
        TB = Sem(ectx.enter_context(nc.semaphore("TB")))
        WSEM = [W0, W1]
        XSEM = [X0, X1]
        GDS = [Sem(ectx.enter_context(nc.semaphore(f"GDS{i}"))) for i in range(4)]
        WSEM2 = [Sem(ectx.enter_context(nc.semaphore(f"WS2{i}"))) for i in range(3)]

        v8 = lambda buf, w: buf[:].rearrange("p (f c) -> p f c", f=8)
        a0, a2 = v8(AR0, STW), v8(AR2, STW)
        aL = v8(ARL, STW + 128)
        sph = v8(SPH, STW)
        spl = v8(SPL, STW)
        sphf = SPH[:].bitcast(F32).rearrange("p (f c) -> p f c", f=8)
        vcv = v8(vcarry, 128)
        kcv = v8(kcarry, 128)
        wslot = [wsl[:, 0:1024], wsl[:, 1024:2048]]
        wslot2 = [wsl2[:, 0:2048], wsl2[:, 2048:4096], wsl2[:, 4096:6144]]
        WIDX = {"r": 0, "kf": 1, "vf": 2, "q": 3, "k": 4, "v": 5, "o": 6}

        guards = {}

        def g_wait(eng, *keys):
            for key in keys:
                for s, n in guards.pop(key, []):
                    eng.wait_ge(s.h, n)

        def g_set(key, sem, n=None):
            guards.setdefault(key, []).append((sem, n if n is not None else sem.n))

        def sy_dma(out, in_, sem):
            SY.dma_start(out=out, in_=in_).then_inc(sem.h, 16)
            return sem.bump(16)

        def ge_dma(out, in_):
            GE.dma_start(out=out, in_=in_).then_inc(GD.h, 16)
            return GD.bump(16)

        def ge_fence():
            GE.wait_ge(GD.h, GD.n)
            GE.sem_inc(SG.h, 1)
            return SG.bump()

        def bc(ap, reps):
            return ap.unsqueeze(1).broadcast_to([ap.shape[0], reps, ap.shape[1]])

        def scr_rows(t, r0):
            return t[r0:r0 + 256, :].rearrange("(b p) d -> p b d", p=128)

        def tile_v(tile):
            return tile[:].rearrange("p (b d) -> p b d", b=2)

        sy_dma(rowa, xrow1, SM)
        wta_bufs = [wslot[0][:], wslot[1][:],
                    st_[1][:, 0:1024], st_[1][:, 1024:2048], st_[0][:, 1024:2048],
                    AR0[:, 1024:2048], AR0[:, 2048:3072], AR2[:, 1024:2048]]
        wta_sems = [W0, W1, X0, X1, TB, WSEM2[0], WSEM2[1], WSEM2[2]]
        wrd = {kk: sy_dma(wta_bufs[kk], wta_d[kk], wta_sems[kk]) for kk in range(2)}
        for dst, src in [(masks["first"][:], mask_d["first"]), (masks["band"][:], mask_d["band"]),
                         (masks["blk2"][:], mask_d["blk2"]), (ident[:], ident_d),
                         (gate[:], gate_d), (mur[:], mufm_r_d), (muk[:], mufm_k_d),
                         (bias_fm["q"][:], bfm_d["q"]), (bias_fm["k"][:], bfm_d["k"]),
                         (bias_fm["v"][:], bfm_d["v"]), (bias_fm["o"][:], bfm_d["o"]),
                         (rowb, brow_a), (rowc, x_halo[0:1, :])]:
            sy_dma(dst, src, SM)
        small_done = SM.n

        GE.memset(kcarry[:], 0.0)
        GE.memset(vcarry[:], 0.0)
        GE.memset(one1[:], 1.0).then_inc(SG.h, 1)
        SG.bump()

        def ln_row(src_row, dst_row):
            VE.wait_ge(SS.h, SS.n)
            VE.bn_stats(stat1[:, 0:6], src_row[0:1, 0:512])
            VE.bn_stats(stat1[:, 6:12], src_row[0:1, 512:1024])
            VE.drain()
            VE.bn_aggr(mv1[:], stat1[:])
            VE.drain()
            VE.tensor_scalar(rst1[:, 0:1], mv1[:, 1:2], EPS, None, ALU.add)
            VE.drain().then_inc(SV.h, 1)
            SV.bump()
            SE.wait_ge(SV.h, SV.n)
            SE.activation(rst1[:, 0:1], rst1[:, 0:1], AF.Ln)
            SE.drain()
            SE.activation(rst1[:, 0:1], rst1[:, 0:1], AF.Exp, scale=-0.5).then_inc(SS.h, 1)
            SS.bump()
            VE.wait_ge(SS.h, SS.n)
            VE.scalar_tensor_tensor(rst1[:, 1:2], mv1[:, 0:1], -1.0, rst1[:, 0:1],
                                    ALU.mult, ALU.mult).then_inc(SV.h, 1)
            SV.bump()
            SE.wait_ge(SV.h, SV.n)
            SE.activation(dst_row[0:1, :], src_row[0:1, :], AF.Identity,
                          bias=rst1[:, 1:2], scale=rst1[:, 0:1]).then_inc(SS.h, 1)
            SS.bump()

        def row_to_fm(src_row, dst_col):
            g_wait(TE, "pb5")
            TE.wait_ge(SS.h, SS.n)
            TE.wait_ge(SG.h, SG.n)
            last = None
            for fc in range(8):
                last = TE.transpose(pb[5][:, fc:fc + 1],
                                    src_row[0:1, fc * 128:(fc + 1) * 128], one1[0:1, 0:1])
            last.then_inc(TT.h, 1)
            TT.bump()
            VE.wait_ge(TT.h, TT.n)
            VE.tensor_copy(dst_col, pb[5][:, 0:8]).then_inc(SV.h, 1)
            SV.bump()
            g_set("pb5", SV)

        VE.wait_ge(SM.h, small_done)
        ln_row(rowa, rowd)
        row_to_fm(rowd, l1fm[:])
        for kk in range(2, 8):
            wrd[kk] = sy_dma(wta_bufs[kk], wta_d[kk], wta_sems[kk])
        for kk in range(8):
            TE.wait_ge(wta_sems[kk].h, wrd[kk])
            if kk == 0:
                TE.wait_ge(SV.h, SV.n)
                g_wait(TE, "pb0", "pb1")
            last = None
            for dh in range(2):
                last = TE.matmul(pb[dh][0:1, 0:512], l1fm[:, kk:kk + 1],
                                 wta_bufs[kk][:, dh * 512:(dh + 1) * 512],
                                 start=(kk == 0), stop=False)
            last.then_inc(TT.h, 1)
            TT.bump()
        last = None
        for dh in range(2):
            last = TE.matmul(pb[dh][0:1, 0:512], one1[0:1, 0:1],
                             rowb[0:1, dh * 512:(dh + 1) * 512], start=False, stop=True)
        last.then_inc(TT.h, 1)
        TT.bump()
        g_set("wsl0", TT)
        g_set("wsl1", TT)
        g_set("st1", TT)
        g_set("st0", TT)
        VE.wait_ge(TT.h, TT.n)
        VE.tensor_copy(rowd[0:1, 0:512], pb[0][0:1, 0:512])
        VE.tensor_copy(rowd[0:1, 512:1024], pb[1][0:1, 0:512])
        VE.drain().then_inc(SV.h, 1)
        SV.bump()
        g_set("pb0", SV)
        g_set("pb1", SV)
        g_wait(TE, "pc")
        TE.wait_ge(SV.h, SV.n)
        last = None
        for dh in range(2):
            last = TE.matmul(pc[:, dh * 512:(dh + 1) * 512], one1[0:1, :],
                             rowd[0:1, dh * 512:(dh + 1) * 512], start=True, stop=True)
        last.then_inc(TT.h, 1)
        TT.bump()
        VE.wait_ge(TT.h, TT.n)
        VE.tensor_copy(y1b[:], pc[:]).then_inc(SV.h, 1)
        SV.bump()
        g_set("pc", SV)
        y1b_ready = SV.n

        VE.tensor_tensor(rowa, rowc, rowd, ALU.add)
        VE.drain().then_inc(SV.h, 1)
        SV.bump()
        g_set("st0", SV)
        ln_row(rowa, rowc)
        row_to_fm(rowc, lncol2[:])
        g_set("AR2", TT)
        g_set("AR0", TT)

        wt_ready = {}
        _stgs = [(st_[0], "st0"), (st_[1], "st1"), (xt[0], "xt0"), (xt[1], "xt1")]
        _store_cnt = []
        _prep_it = [0]
        _prep_done = {}

        def prep_iter(wkey, m):
            wi = WIDX[wkey]
            it = _prep_it[0]
            _prep_it[0] += 1
            slot = it % 2
            g_wait(SY, f"wsl{slot}")
            wr = sy_dma(wslot[slot][:], wt_d[wkey][m], WSEM[slot])
            stg, stgk = _stgs[it % 4]
            g_wait(SE, stgk)
            SE.wait_ge(WSEM[slot].h, wr)
            SE.drain()
            SE.activation(stg[:, 0:1024].bitcast(F32R), wslot[slot][:],
                          AF.Identity).then_inc(SS.h, 1)
            SS.bump()
            VE.wait_ge(SS.h, SS.n)
            VE.tensor_tensor(stg[:, 1024:2048].bitcast(F32R), wslot[slot][:],
                             stg[:, 0:1024], ALU.subtract)
            VE.drain().then_inc(SV.h, 1)
            SV.bump()
            g_set(f"wsl{slot}", SV)
            GE.wait_ge(SV.h, SV.n)
            gsem = GDS[it % 4]
            GE.dma_start(out=wt_hl[wi, m], in_=stg[:].bitcast(F32R)).then_inc(gsem.h, 16)
            gsem.bump(16)
            _store_cnt.append((stgk, gsem, gsem.n))
            if len(_store_cnt) >= 4:
                pk, ps, pn = _store_cnt[-4]
                GE.wait_ge(ps.h, pn)
                GE.sem_inc(SG.h, 1)
                g_set(pk, SG, SG.bump())
            _prep_done[wkey] = _prep_done.get(wkey, 0) + 1

        def prep_fin(wkey):
            for gs in GDS:
                GE.wait_ge(gs.h, gs.n)
            GE.sem_inc(SG.h, 1)
            f = SG.bump()
            for pk, _, _ in _store_cnt[-3:]:
                g_set(pk, SG, f)
            wt_ready[wkey] = f

        def prep_weight(wkey):
            if wkey in wt_ready:
                return
            for m in range(_prep_done.get(wkey, 0), 8):
                prep_iter(wkey, m)
            prep_fin(wkey)

        prep_weight("r")

        ln_lock = [SS.n, SS.n]

        def ln_chunk(xtile, lnt, par):
            so = par * 24
            mo = par * 4
            mvv = mv[:, mo:mo + 4].rearrange("p (b t) -> p b t", t=2)
            VE.wait_ge(SS.h, ln_lock[par])
            ns = []
            for b2 in range(2):
                VE.bn_stats(stats[:, so + b2 * 12:so + b2 * 12 + 6],
                            xtile[:, b2 * 1024:b2 * 1024 + 512])
                VE.bn_stats(stats[:, so + b2 * 12 + 6:so + b2 * 12 + 12],
                            xtile[:, b2 * 1024 + 512:(b2 + 1) * 1024])
                VE.drain()
                VE.bn_aggr(mv[:, mo + b2 * 2:mo + (b2 + 1) * 2],
                           stats[:, so + b2 * 12:so + (b2 + 1) * 12])
                VE.drain()
                rc = rstd[:, 2 * par + b2:2 * par + b2 + 1]
                VE.tensor_scalar(rc, mvv[:, b2, 1:2], EPS, None, ALU.add)
                VE.drain().then_inc(SV.h, 1)
                SV.bump()
                SE.wait_ge(SV.h, SV.n)
                SE.activation(rc, rc, AF.Ln)
                SE.drain()
                SE.activation(rc, rc, AF.Exp, scale=-0.5).then_inc(SS.h, 1)
                SS.bump()
                VE.wait_ge(SS.h, SS.n)
                VE.scalar_tensor_tensor(nbias[:, 2 * par + b2:2 * par + b2 + 1],
                                        mvv[:, b2, 0:1], -1.0, rc,
                                        ALU.mult, ALU.mult).then_inc(SV.h, 1)
                SV.bump()
                SE.wait_ge(SV.h, SV.n)
                SE.activation(lnt[:, b2 * 1024:(b2 + 1) * 1024],
                              xtile[:, b2 * 1024:(b2 + 1) * 1024], AF.Identity,
                              bias=nbias[:, 2 * par + b2:2 * par + b2 + 1],
                              scale=rc).then_inc(SS.h, 1)
                SS.bump()
                ns.append(SS.n)
            ln_lock[par] = SS.n
            return tuple(ns)

        xp_par = [0]

        def xpose_tile(src_tile, src_off, n_tok, dst_view, dst_c0, rsem, rn, stride=None):
            out_n = 0
            for b2 in range(n_tok // 128):
                rb2 = rn[b2] if isinstance(rn, tuple) else rn
                for half in range(2):
                    par = xp_par[0]
                    xp_par[0] ^= 1
                    bank = pb[2 + par]
                    g_wait(TE, f"pb{2 + par}")
                    TE.wait_ge(rsem.h, rb2)
                    last = None
                    for j in range(4):
                        fc = half * 4 + j
                        o = src_off + b2 * 1024 + fc * 128
                        last = TE.transpose(bank[:, j * 128:(j + 1) * 128],
                                            src_tile[:, o:o + 128], ident[:])
                    last.then_inc(TT.h, 1)
                    TT.bump()
                    VE.wait_ge(TT.h, TT.n)
                    pbv = bank[:].rearrange("p (f c) -> p f c", f=4)
                    VE.tensor_copy(dst_view[:, half * 4:half * 4 + 4,
                                            dst_c0 + b2 * 128:dst_c0 + (b2 + 1) * 128],
                                   pbv).then_inc(SV.h, 1)
                    SV.bump()
                    g_set(f"pb{2 + par}", SV)
                    out_n = SV.n
            return out_n

        pair_busy = [0]

        def xpose_split_tile(src_tile, dst_c0, rsem, rn):
            out_n = 0
            for half in range(2):
                par = xp_par[0]
                xp_par[0] ^= 1
                bank = pb[2 + par]
                g_wait(TE, f"pb{2 + par}")
                TE.wait_ge(rsem.h, rn)
                last = None
                for j in range(4):
                    fc = half * 4 + j
                    last = TE.transpose(bank[:, j * 128:(j + 1) * 128],
                                        src_tile[:, fc * 128:(fc + 1) * 128], ident[:])
                last.then_inc(TT.h, 1)
                TT.bump()
                VE.wait_ge(TT.h, TT.n)
                VE.wait_ge(TT.h, pair_busy[0])
                pbv = bank[:].rearrange("p (f c) -> p f c", f=4)
                VE.tensor_copy(sph[:, half * 4:half * 4 + 4, dst_c0:dst_c0 + 128], pbv)
                VE.drain()
                VE.tensor_tensor(spl[:, half * 4:half * 4 + 4, dst_c0:dst_c0 + 128], pbv,
                                 sphf[:, half * 4:half * 4 + 4, dst_c0:dst_c0 + 128],
                                 ALU.subtract)
                VE.drain().then_inc(SV.h, 1)
                SV.bump()
                g_set(f"pb{2 + par}", SV)
                out_n = SV.n
            return out_n

        def xpose_tile_pair(lnt, dst_c0, rsem, rn):
            out_n = 0
            for b2 in range(2):
                rb2 = rn[b2] if isinstance(rn, tuple) else rn
                for half in range(2):
                    par = xp_par[0]
                    xp_par[0] ^= 1
                    bank = pb[2 + par]
                    g_wait(TE, f"pb{2 + par}")
                    TE.wait_ge(rsem.h, rb2)
                    last = None
                    for j in range(4):
                        fc = half * 4 + j
                        last = TE.transpose(bank[:, j * 128:(j + 1) * 128],
                                            lnt[:, b2 * 1024 + fc * 128:
                                                b2 * 1024 + (fc + 1) * 128], ident[:])
                    last.then_inc(TT.h, 1)
                    TT.bump()
                    VE.wait_ge(TT.h, TT.n)
                    VE.wait_ge(TT.h, pair_busy[0])
                    pbv = bank[:].rearrange("p (f c) -> p f c", f=4)
                    d0 = dst_c0 + b2 * 128
                    VE.tensor_copy(sph[:, half * 4:half * 4 + 4, d0:d0 + 128], pbv)
                    VE.drain()
                    VE.tensor_tensor(spl[:, half * 4:half * 4 + 4, d0:d0 + 128], pbv,
                                     sphf[:, half * 4:half * 4 + 4, d0:d0 + 128],
                                     ALU.subtract)
                    VE.drain().then_inc(SV.h, 1)
                    SV.bump()
                    g_set(f"pb{2 + par}", SV)
                    out_n = SV.n
            return out_n

        def split(src_view, nrows, c_start, pre, per_chunk=None):
            out = {}
            first = True
            for c0, ln in ragged(nrows, c_start):
                for cc in range(c0, c0 + ln, CH):
                    if first:
                        SE.wait_ge(TT.h, pair_busy[0])
                    if per_chunk is not None:
                        if cc in per_chunk:
                            s, n = per_chunk[cc]
                            SE.wait_ge(s.h, n)
                    else:
                        for s, n in pre:
                            SE.wait_ge(s.h, n)
                    SE.drain()
                    SE.activation(sph[:, :, cc:cc + CH], src_view[:, :, cc:cc + CH],
                                  AF.Identity).then_inc(SS.h, 1)
                    SS.bump()
                    VE.wait_ge(SS.h, SS.n)
                    if first:
                        VE.wait_ge(TT.h, pair_busy[0])
                        first = False
                    VE.tensor_tensor(spl[:, :, cc:cc + CH], src_view[:, :, cc:cc + CH],
                                     sphf[:, :, cc:cc + CH], ALU.subtract)
                    VE.drain().then_inc(SV.h, 1)
                    SV.bump()
                out[c0] = SV.n
            return out

        def ragged(nrows, c_start):
            out = []
            c0 = c_start * CH
            while c0 < nrows:
                ln = min(512, nrows - c0)
                out.append((c0, ln))
                c0 += ln
            return out

        pre_wrd = {}

        def sweep_preload(wkey):
            if wkey in pre_wrd or wkey not in wt_ready:
                return
            wi = WIDX[wkey]
            d = {}
            for m in (0, 1, 2):
                g_wait(SY, f"ws2_{m}")
                SY.wait_ge(SG.h, wt_ready[wkey])
                d[m] = sy_dma(wslot2[m][:], wt_hl[wi, m], WSEM2[m])
            pre_wrd[wkey] = d

        def sweep3(wkey, nrows, c_start, evac, pre, pf=()):
            prep_weight(wkey)
            if not PF_ON:
                pf = ()
            pending = [(w, m) for w in pf if w not in wt_ready
                       for m in range(_prep_done.get(w, 0), 8)]
            wi = WIDX[wkey]
            chunks = ragged(nrows, c_start)
            if wkey in pre_wrd:
                wrd = pre_wrd.pop(wkey)
            else:
                wrd = {}
                for m in (0, 1, 2):
                    g_wait(SY, f"ws2_{m}")
                    SY.wait_ge(SG.h, wt_ready[wkey])
                    wrd[m] = sy_dma(wslot2[m][:], wt_hl[wi, m], WSEM2[m])
            first = True
            for m in range(8):
                TE.wait_ge(WSEM2[m % 3].h, wrd[m])
                if first:
                    if not isinstance(pre, dict):
                        for s, n in pre:
                            TE.wait_ge(s.h, n)
                    first = False
                for ci, (c0, ln) in enumerate(chunks):
                    par = (m * len(chunks) + ci) % 2
                    g_wait(TE, f"pb{par}")
                    if m == 0 and isinstance(pre, dict):
                        TE.wait_ge(SV.h, pre[c0])
                    last = None
                    nmm = 0
                    for k in range(8):
                        for wo, mvv in ((0, sph), (0, spl), (1024, sph)):
                            last = TE.matmul(pb[par][:, 0:ln],
                                             wslot2[m % 3][:, wo + k * 128:wo + (k + 1) * 128],
                                             mvv[:, k, c0:c0 + ln],
                                             start=(nmm == 0), stop=(nmm == 23))
                            nmm += 1
                    last.then_inc(TT.h, 1)
                    TT.bump()
                    if ci == len(chunks) - 1:
                        g_set(f"ws2_{m % 3}", TT)
                    evac(m, c0, ln, par, TT.n)
                for _ in range(2):
                    if pending:
                        w_, m_ = pending.pop(0)
                        prep_iter(w_, m_)
                        if m_ == 7:
                            prep_fin(w_)
                if m + 3 < 8:
                    g_wait(SY, f"ws2_{m % 3}")
                    wrd[m + 3] = sy_dma(wslot2[m % 3][:], wt_hl[wi, m + 3], WSEM2[m % 3])
            while pending:
                w_, m_ = pending.pop(0)
                prep_iter(w_, m_)
                if m_ == 7:
                    prep_fin(w_)
            pair_busy[0] = TT.n
            return

        def evac_s(dst_view, func, bias_t=None, off=0):
            def f(m, c0, ln, par, ttn):
                SE.wait_ge(TT.h, ttn)
                kw = {}
                if bias_t is not None:
                    kw = dict(bias=bias_t[:, m:m + 1], scale=1.0)
                SE.activation(dst_view[:, m, off + c0:off + c0 + ln],
                              pb[par][:, 0:ln], func, **kw).then_inc(SS.h, 1)
                SS.bump()
                g_set(f"pb{par}", SS)
            return f

        def evac_sigmoid(dst_view):
            def f(m, c0, ln, par, ttn):
                SE.wait_ge(TT.h, ttn)
                SE.activation(dst_view[:, m, c0:c0 + ln], pb[par][:, 0:ln],
                              AF.Tanh, scale=0.5).then_inc(SS.h, 1)
                SS.bump()
                g_set(f"pb{par}", SS)
                VE.wait_ge(SS.h, SS.n)
                VE.tensor_scalar(dst_view[:, m, c0:c0 + ln], dst_view[:, m, c0:c0 + ln],
                                 0.5, 0.5, ALU.mult, ALU.add)
                VE.drain().then_inc(SV.h, 1)
                SV.bump()
            return f

        def evac_relu_sq(dst_view):
            def f(m, c0, ln, par, ttn):
                SE.wait_ge(TT.h, ttn)
                SE.activation(dst_view[:, m, c0:c0 + ln], pb[par][:, 0:ln],
                              AF.Relu).then_inc(SS.h, 1)
                SS.bump()
                g_set(f"pb{par}", SS)
                VE.wait_ge(SS.h, SS.n)
                VE.tensor_tensor(dst_view[:, m, c0:c0 + ln],
                                 dst_view[:, m, c0:c0 + ln],
                                 dst_view[:, m, c0:c0 + ln], ALU.mult)
                VE.drain().then_inc(SV.h, 1)
                SV.bump()
            return f

        def evac_v_prod(dst_view, r_view):
            def f(m, c0, ln, par, ttn):
                VE.wait_ge(TT.h, ttn)
                VE.tensor_tensor(dst_view[:, m, c0:c0 + ln], pb[par][:, 0:ln],
                                 r_view[:, m, c0:c0 + ln], ALU.mult).then_inc(SV.h, 1)
                SV.bump()
                g_set(f"pb{par}", SV)
            return f

        def resid_store(src_raw, prod_sem, prod_n, src_fn, dst, dst_is_y,
                        sbase, chunks, add_y1b=False, resident=False):
            fences = {}
            for c in chunks:
                xti = xt[c % 2]
                if src_fn is not None:
                    g_wait(SY, f"xt{c % 2}")
                    ap, wn = src_fn(c)
                    if wn:
                        SY.wait_ge(SG.h, wn)
                    xrd = sy_dma(tile_v(xti), ap, XSEM[c % 2])
                else:
                    xrd = None
                if add_y1b:
                    if xrd is not None:
                        VE.wait_ge(XSEM[c % 2].h, xrd)
                    VE.wait_ge(SV.h, y1b_ready)
                    VE.tensor_tensor(tile_v(xti), tile_v(xti), bc(y1b[:], 2), ALU.add)
                    VE.drain()
                if dst_is_y:
                    g_wait(VE, f"st{c % 2}")
                    dtile = st_[c % 2]
                else:
                    dtile = xti
                for b2 in range(2):
                    for half in range(2):
                        par = xp_par[0]
                        xp_par[0] ^= 1
                        bank = pb[2 + par]
                        g_wait(TE, f"pb{2 + par}")
                        TE.wait_ge(prod_sem.h, prod_n)
                        last = None
                        for j in range(4):
                            fc = half * 4 + j
                            o = fc * STW + c * CH + b2 * 128
                            last = TE.transpose(bank[:, j * 128:(j + 1) * 128],
                                                src_raw[:, o:o + 128], ident[:])
                        last.then_inc(TT.h, 1)
                        TT.bump()
                        VE.wait_ge(TT.h, TT.n)
                        if xrd is not None:
                            VE.wait_ge(XSEM[c % 2].h, xrd)
                        VE.drain()
                        sl = xti[:, b2 * 1024 + half * 512:b2 * 1024 + (half + 1) * 512]
                        dl = dtile[:, b2 * 1024 + half * 512:b2 * 1024 + (half + 1) * 512]
                        VE.tensor_tensor(dl, sl, bank[:], ALU.add).then_inc(SV.h, 1)
                        SV.bump()
                        g_set(f"pb{2 + par}", SV)
                if dst_is_y:
                    g_set(f"xt{c % 2}", SV)
                    GE.wait_ge(SV.h, SV.n)
                    r0 = sbase + c * CH - HALO
                    ge_dma(y_out[r0:r0 + 256, :].rearrange("(b p) d -> p b d", p=128),
                           tile_v(dtile))
                    fences[c] = ge_fence()
                    g_set(f"st{c % 2}", SG, fences[c])
                elif resident:
                    fences[c] = SV.n
                else:
                    g_set(f"xt{c % 2}", SV)
                    GE.wait_ge(SV.h, SV.n)
                    ge_dma(scr_rows(dst, sbase + c * CH), tile_v(xti))
                    fences[c] = ge_fence()
                    g_set(f"xt{c % 2}", SG, fences[c])
            return fences

        x_written = {"x1": 0, "x2": 0, "x3": 0}
        sl_state = {"tabs": 0}

        def ffn_sublayer(st, sbase, nrows, which):
            is2 = which == "sl2"
            lnc = lncol2 if is2 else lncol4
            c_start = 1 if (st == 0 and not is2) else 0
            chunks = list(range(c_start, nrows // CH))

            sweep_preload("r")
            g_wait(VE, "ARL")
            VE.drain()
            VE.tensor_copy(aL[:, :, c_start * CH:c_start * CH + 1], lnc[:].unsqueeze(2))
            VE.drain().then_inc(SV.h, 1)
            SV.bump()

            lnfm_ready = {}
            mix_ready = {}
            res = nrows == ST_real
            for c in chunks:
                xti = xt[c % 2]
                if is2:
                    g_wait(SY, f"xt{c % 2}")
                    if st == 0 and c == 0:
                        src = x_halo[1:257, :].rearrange("(b p) d -> p b d", p=128)
                    else:
                        t0 = sbase + c * CH - HALO
                        src = x_main[t0:t0 + 256, :].rearrange("(b p) d -> p b d", p=128)
                    xrd = sy_dma(tile_v(xti), src, XSEM[c % 2])
                    VE.wait_ge(XSEM[c % 2].h, xrd)
                    VE.wait_ge(SV.h, y1b_ready)
                    VE.tensor_tensor(tile_v(xti), tile_v(xti), bc(y1b[:], 2), ALU.add)
                    VE.drain().then_inc(SV.h, 1)
                    SV.bump()
                elif res:
                    VE.drain()
                else:
                    g_wait(SY, f"xt{c % 2}")
                    x3f = x_written["x3"]
                    SY.wait_ge(SG.h, x3f[c] if isinstance(x3f, dict) else x3f)
                    xrd = sy_dma(tile_v(xti), scr_rows(x3s, sbase + c * CH), XSEM[c % 2])
                    VE.wait_ge(XSEM[c % 2].h, xrd)
                lnt = st_[c % 2]
                g_wait(SE, f"st{c % 2}")
                ss_apply = ln_chunk(xti, lnt, c % 2)
                g_set(f"xt{c % 2}", SS, ss_apply[1])
                lnfm_ready[c] = xpose_tile(lnt, 0, CH, aL, 1 + c * CH, SS, ss_apply)
                g_set(f"st{c % 2}", TT)
                if st == 0 and is2 and c == 1:
                    VE.tensor_tensor(aL[:, :, CH:CH + 1], aL[:, :, CH:CH + 1],
                                     gate[:].unsqueeze(2), ALU.mult)
                    VE.drain().then_inc(SV.h, 1)
                    SV.bump()
                dt_ = st_[c % 2]
                g_wait(VE, f"st{c % 2}")
                if c == chunks[0]:
                    g_wait(VE, "AR0", "AR2")
                dv = v8(dt_, 256)
                c0 = 1 + c * CH
                VE.drain()
                VE.tensor_tensor(dv, aL[:, :, c0:c0 + CH], aL[:, :, c0 - 1:c0 + CH - 1],
                                 ALU.subtract)
                VE.drain()
                for fc in range(8):
                    VE.scalar_tensor_tensor(a2[:, fc, c * CH:(c + 1) * CH], dv[:, fc, :],
                                            mur[:, fc:fc + 1], aL[:, fc, c0 - 1:c0 + CH - 1],
                                            ALU.mult, ALU.add)
                    VE.scalar_tensor_tensor(a0[:, fc, c * CH:(c + 1) * CH], dv[:, fc, :],
                                            muk[:, fc:fc + 1], aL[:, fc, c0 - 1:c0 + CH - 1],
                                            ALU.mult, ALU.add)
                VE.drain().then_inc(SV.h, 1)
                SV.bump()
                g_set(f"st{c % 2}", SV)
                mix_ready[c * CH] = (SV, SV.n)
            mix_done = SV.n
            VE.tensor_copy(lnc[:].unsqueeze(2), aL[:, :, nrows:nrows + 1])
            VE.drain().then_inc(SV.h, 1)
            SV.bump()
            g_set("ARL", SV)

            sxr = split(a2, nrows, c_start, None, per_chunk=mix_ready)
            g_wait(SE, "AR2")
            sweep3("r", nrows, c_start, evac_sigmoid(a2), sxr, pf=("kf", "vf"))
            sxk = split(a0, nrows, c_start, [(SV, mix_done)])
            g_wait(SE, "AR0")
            sweep3("kf", nrows, c_start, evac_relu_sq(a0), sxk, pf=("q", "k"))
            sq_done = SV.n
            ska = split(a0, nrows, c_start, [(SV, sq_done)])
            g_wait(VE, "AR0")
            sweep3("vf", nrows, c_start, evac_v_prod(a0, a2), ska, pf=("v", "o"))
            prod_done = SV.n
            g_set("AR2", SV, prod_done)

            if res:
                src_fn = None
            elif is2:
                def src_fn(c):
                    if st == 0 and c == 0:
                        return x_halo[1:257, :].rearrange("(b p) d -> p b d", p=128), None
                    t0 = sbase + c * CH - HALO
                    return x_main[t0:t0 + 256, :].rearrange("(b p) d -> p b d", p=128), None
            else:
                x3f = x_written["x3"]

                def src_fn(c):
                    return scr_rows(x3s, sbase + c * CH), (
                        x3f[c] if isinstance(x3f, dict) else x3f)
            cnt = resid_store(AR0, SV, prod_done, src_fn,
                              x2s if is2 else None, not is2, sbase, chunks,
                              add_y1b=(is2 and not res), resident=res)
            g_set("AR0", TT)
            if is2:
                x_written["x2"] = cnt

        def attn_sublayer(st, sbase, nrows):
            nblk = nrows // BLK
            chunks = list(range(nrows // CH))

            sweep_preload("q")

            ln3_ready = {}
            res = nrows == ST_real
            g_wait(VE, "AR0")
            for c in chunks:
                xti = xt[c % 2]
                if res:
                    VE.drain()
                else:
                    g_wait(SY, f"xt{c % 2}")
                    x2f = x_written["x2"]
                    SY.wait_ge(SG.h, x2f[c] if isinstance(x2f, dict) else x2f)
                    xrd = sy_dma(tile_v(xti), scr_rows(x2s, sbase + c * CH), XSEM[c % 2])
                    VE.wait_ge(XSEM[c % 2].h, xrd)
                lnt = st_[c % 2]
                g_wait(SE, f"st{c % 2}")
                ss_apply = ln_chunk(xti, lnt, c % 2)
                g_set(f"xt{c % 2}", SS, ss_apply[1])
                ln3_ready[c * CH] = xpose_tile_pair(lnt, c * CH, SS, ss_apply)
                g_set(f"st{c % 2}", TT)

            def rope(view, coff, ev_done):
                done = 0
                for c in chunks:
                    pt = st_[c % 2]
                    g_wait(GE, f"st{c % 2}")
                    GE.wait_ge(SS.h, ev_done)
                    pv = v8(pt, 256)
                    for dp, sp in [(0, 32), (32, 0), (64, 96), (96, 64)]:
                        ge_dma(pv[dp:dp + 32, :, :],
                               view[sp:sp + 32, :, coff + c * CH:coff + (c + 1) * CH])
                    pf = ge_fence()
                    VE.wait_ge(SG.h, pf)
                    VE.wait_ge(TB.h, sl_state["tabs"])
                    ch_sl = view[:, :, coff + c * CH:coff + (c + 1) * CH]
                    VE.tensor_tensor(ch_sl, ch_sl, bc(cosT[:, c * CH:(c + 1) * CH], 8),
                                     ALU.mult)
                    VE.drain()
                    VE.tensor_tensor(pv, pv, bc(sinT[:, c * CH:(c + 1) * CH], 8), ALU.mult)
                    VE.drain()
                    VE.tensor_tensor(ch_sl, ch_sl, pv, ALU.add)
                    VE.drain().then_inc(SV.h, 1)
                    SV.bump()
                    g_set(f"st{c % 2}", SV)
                    done = SV.n
                return done

            g_wait(SY, "tabs")
            if st == 0:
                g_wait(SY, "wsl0", "wsl1")
            sy_dma(cosT[:, 0:nrows], cos_d[:, sbase:sbase + nrows], TB)
            sl_state["tabs"] = sy_dma(sinT[:, 0:nrows], sin_d[:, sbase:sbase + nrows], TB)
            sl3 = {c0: max(v for k, v in ln3_ready.items() if k < c0 + ln)
                   for c0, ln in ragged(nrows, 0)}
            g_wait(SE, "AR0")
            sweep3("q", nrows, 0, evac_s(a0, AF.Identity, bias_fm["q"]), sl3)
            qrot_done = rope(a0, 0, SS.n)
            g_wait(SE, "ARL")
            sweep3("k", nrows, 0, evac_s(aL, AF.Identity, bias_fm["k"], off=128), sl3)
            krot_done = rope(aL, 128, SS.n)
            g_set("tabs", SV, krot_done)
            VE.wait_ge(SG.h, SG.n)
            VE.tensor_copy(aL[:, :, 0:128], kcv)
            VE.drain()
            VE.tensor_copy(kcv, aL[:, :, nrows:nrows + 128])
            VE.drain().then_inc(SV.h, 1)
            SV.bump()
            kready = SV.n
            g_wait(SE, "AR2")
            sweep3("v", nrows, 0, evac_s(a2, AF.Identity, bias_fm["v"]), sl3)
            v_ev = SS.n
            sweep_preload("o")

            scores_last = TT.n
            exp_prev = SS.n
            recip_prev = SV.n
            for n in range(nblk):
                TE.wait_ge(SS.h, v_ev)
                if n == 0:
                    TE.wait_ge(SV.h, max(qrot_done, krot_done, kready))
                for half in range(2):
                    bank = pb[2 + half]
                    g_wait(TE, f"pb{2 + half}")
                    last = None
                    for j in range(4):
                        fc = half * 4 + j
                        last = TE.transpose(bank[:, j * 128:(j + 1) * 128],
                                            a2[:, fc, n * 128:(n + 1) * 128], ident[:])
                    last.then_inc(TT.h, 1)
                    TT.bump()
                    VE.wait_ge(TT.h, TT.n)
                    VE.tensor_copy(a2[:, half * 4:half * 4 + 4, n * 128:(n + 1) * 128],
                                   bank[:].rearrange("p (f c) -> p f c", f=4))
                    VE.drain().then_inc(SV.h, 1)
                    SV.bump()
                    g_set(f"pb{2 + half}", SV)
                vtm_done = SV.n

                mk = ({0: "first", 1: "band", 2: "blk2"}.get(n, "band")
                      if st == 0 else "band")
                mt = masks[mk]

                pc_started = [False, False]

                def scores_mm(hp):
                    spar = hp % 2
                    sb0, sb1 = (4, 5) if spar == 0 else (0, 1)
                    g_wait(TE, f"pb{sb0}", f"pb{sb1}")
                    last = None
                    for j in range(2):
                        h = 2 * hp + j
                        fc, po = h // 2, (h % 2) * 64
                        q_ap = a0[po:po + 64, fc, n * 128:(n + 1) * 128]
                        k_ap = aL[po:po + 64, fc, n * 128:n * 128 + 256]
                        last = TE.matmul(pb[(sb0, sb1)[j]][:, 0:256], q_ap, k_ap,
                                         start=True, stop=True)
                    last.then_inc(TT.h, 1)
                    TT.bump()
                    return (sb0, sb1, TT.n)

                def mask_part(hp, meta):
                    sb0, sb1, sc_n = meta
                    spar = hp % 2
                    prs = st_[0][:, spar * 1024:spar * 1024 + 512]
                    prsv = prs.rearrange("p (j x) -> p j x", j=2)
                    psx = ps_of[sb0][:].rearrange("p (j x) -> p j x", j=2)[:, :, 0:256]
                    g_wait(VE, f"pr{spar}")
                    VE.wait_ge(TT.h, sc_n)
                    if hp == 0:
                        VE.wait_ge(SS.h, exp_prev)
                        VE.wait_ge(SM.h, small_done)
                    VE.tensor_tensor(prsv, psx, bc(mt[:], 2), ALU.add)
                    VE.drain()
                    VE.tensor_reduce(nmax[:, 2 * hp:2 * hp + 2], prsv,
                                     axis=AX.X, op=ALU.max, negate=True)
                    VE.drain()
                    VE.tensor_scalar(nmax[:, 2 * hp:2 * hp + 2], nmax[:, 2 * hp:2 * hp + 2],
                                     0.125, None, ALU.mult)
                    VE.drain().then_inc(SV.h, 1)
                    SV.bump()
                    g_set(f"pb{sb0}", SV)
                    g_set(f"pb{sb1}", SV)
                    return SV.n

                def pair_rest(hp, m_n):
                    spar = hp % 2
                    prs = st_[0][:, spar * 1024:spar * 1024 + 512]
                    SE.wait_ge(SV.h, m_n)
                    if hp == 0:
                        SE.wait_ge(SV.h, recip_prev)
                    last = None
                    for j in range(2):
                        h = 2 * hp + j
                        last = SE.activation(prs[:, j * 256:(j + 1) * 256],
                                             prs[:, j * 256:(j + 1) * 256], AF.Exp,
                                             bias=nmax[:, h:h + 1], scale=0.125,
                                             accum_out=rsum[:, h:h + 1])
                    last.then_inc(SS.h, 1)
                    SS.bump()
                    exp_n = SS.n
                    tpar = xp_par[0]
                    xp_par[0] ^= 1
                    tbank = pb[2 + tpar]
                    g_wait(TE, f"pb{2 + tpar}")
                    TE.wait_ge(SS.h, exp_n)
                    last = None
                    for j in range(2):
                        for kc in range(2):
                            last = TE.transpose(
                                tbank[:, (2 * j + kc) * 128:(2 * j + kc + 1) * 128],
                                prs[:, j * 256 + kc * 128:j * 256 + (kc + 1) * 128],
                                ident[:])
                    last.then_inc(TT.h, 1)
                    TT.bump()
                    g_set(f"pr{spar}", TT)
                    ats = st_[1][:, spar * 1024:spar * 1024 + 512]
                    g_wait(VE, f"at{spar}")
                    VE.wait_ge(TT.h, TT.n)
                    VE.tensor_copy(ats, tbank[:]).then_inc(SV.h, 1)
                    SV.bump()
                    g_set(f"pb{2 + tpar}", SV)
                    return ats, SV.n

                def pair_ctx(hp, fm):
                    nonlocal scores_last
                    ats, at_n = fm
                    spar = hp % 2
                    if hp == 0:
                        g_wait(TE, "pc")
                    TE.wait_ge(SV.h, max(at_n, vtm_done))
                    last = None
                    for j in range(2):
                        h = 2 * hp + j
                        fc, po = h // 2, (h % 2) * 64
                        bank_i = h // 8
                        for kc in range(2):
                            bv = n - 1 + kc
                            if bv < 0:
                                v_ap = vcv[:, fc, po:po + 64]
                            else:
                                v_ap = a2[:, fc, bv * 128 + po:bv * 128 + po + 64]
                            stt = not pc_started[bank_i]
                            pc_started[bank_i] = True
                            stop = (h % 8 == 7) and kc == 1
                            last = TE.matmul(pc[:, h * 64:(h + 1) * 64],
                                             ats[:, (2 * j + kc) * 128:(2 * j + kc + 1) * 128],
                                             v_ap, start=stt, stop=stop)
                    last.then_inc(TT.h, 1)
                    TT.bump()
                    g_set(f"at{spar}", TT)
                    scores_last = TT.n

                metas = {0: scores_mm(0), 1: scores_mm(1)}
                msv = {0: mask_part(0, metas.pop(0))}
                fms = {}
                for hp in range(8):
                    if hp + 1 < 8:
                        msv[hp + 1] = mask_part(hp + 1, metas.pop(hp + 1))
                    if hp + 2 < 8:
                        metas[hp + 2] = scores_mm(hp + 2)
                    if hp >= 1:
                        pair_ctx(hp - 1, fms.pop(hp - 1))
                    fms[hp] = pair_rest(hp, msv.pop(hp))
                pair_ctx(7, fms.pop(7))
                VE.wait_ge(SS.h, SS.n)
                VE.reciprocal(recip[:, 0:16], rsum[:, 0:16])
                VE.drain()
                VE.wait_ge(TT.h, scores_last)
                pcv = pc[:].rearrange("p (h e) -> p h e", h=16)
                rb = recip[:, 0:16].unsqueeze(2).broadcast_to([128, 16, 64])
                VE.tensor_tensor(ctm[:].rearrange("p (h e) -> p h e", h=16), pcv, rb,
                                 ALU.mult)
                VE.drain().then_inc(SV.h, 1)
                SV.bump()
                g_set("pc", SV)
                recip_prev = SV.n
                exp_prev = SS.n
                ctx_split = xpose_split_tile(ctm, n * 128, SV, SV.n)
            g_set("vc", TT, scores_last)
            g_wait(VE, "vc")
            VE.tensor_copy(vcv, a2[:, :, (nblk - 1) * 128:nblk * 128])
            VE.drain().then_inc(SV.h, 1)
            SV.bump()
            ctx_done = SV.n
            g_set("ARL", TT, scores_last)
            g_set("AR2", SV, ctx_done)
            g_set("AR2", TT, scores_last)
            g_set("AR0", TT, scores_last)

            g_wait(SE, "AR0")
            sweep3("o", nrows, 0, evac_s(a0, AF.Identity, bias_fm["o"]),
                  [(SV, ctx_split)])
            o_done = SS.n

            if res:
                src_fn = None
            else:
                x2f = x_written["x2"]

                def src_fn(c):
                    return scr_rows(x2s, sbase + c * CH), (
                        x2f[c] if isinstance(x2f, dict) else x2f)
            x_written["x3"] = resid_store(AR0, SS, o_done, src_fn,
                                          x3s, False, sbase, chunks, resident=res)
            g_set("AR0", TT)

        _ph = [0]

        def phase_ok():
            _ph[0] += 1
            return trunc is None or _ph[0] <= trunc

        for st in range(n_st):
            nrows = ST_real + (HALO if st == 0 else 0)
            sbase = 0 if st == 0 else HALO + st * ST_real
            if not phase_ok():
                break
            ffn_sublayer(st, sbase, nrows, "sl2")
            if not phase_ok():
                break
            attn_sublayer(st, sbase, nrows)
            if not phase_ok():
                break
            if st == 0:
                x3f = x_written["x3"]
                GE.wait_ge(SG.h, max(x3f.values()) if isinstance(x3f, dict) else x3f)
                ge_dma(rowa, x3s[255:256, :])
                rrd = ge_fence()
                VE.wait_ge(SG.h, rrd)
                g_wait(SE, "AR0")
                ln_row(rowa, rowb)
                row_to_fm(rowb, lncol4[:])
                g_set("AR0", TT)
                VE.drain()
                VE.tensor_tensor(lncol4[:], lncol4[:], gate[:], ALU.mult)
                VE.drain().then_inc(SV.h, 1)
                SV.bump()
            ffn_sublayer(st, sbase, nrows, "sl4")

        GE.wait_ge(GD.h, GD.n)

    return nc



def _prep_w(W):
    a = W.reshape(8, 128, 8, 128).transpose(0, 3, 2, 1)
    return np.ascontiguousarray(a).reshape(8, 128, 1024)


def _prep_wa(W):
    return np.ascontiguousarray(np.ascontiguousarray(W.T).reshape(8, 128, 1024))


def _masks():
    import ml_dtypes
    qi = np.arange(128)[:, None]
    kj = np.arange(256)[None, :]
    band = (kj >= qi) & (kj <= qi + 128)
    first = band & (kj >= 128)
    mb = np.where(band, 0.0, NEG).astype(np.float32)
    mf = np.where(first, 0.0, NEG).astype(np.float32)
    t2 = lambda m: np.ascontiguousarray(m.astype(ml_dtypes.bfloat16))
    return t2(mf), t2(mb)


def _tables(h, T_real):
    n_scr = HALO + T_real
    pos = np.maximum(h * T_real - HALO + np.arange(n_scr), 0).astype(np.float64)
    p = np.arange(128)
    j = p % 32
    inv = 10000.0 ** (-2.0 * j / 64.0)
    ang = pos[None, :] * inv[:, None]
    cos_t = np.cos(ang).astype(np.float32)
    sign = np.where((p % 64) < 32, -1.0, 1.0)
    sin_t = (np.sin(ang) * sign[:, None]).astype(np.float32)
    return cos_t, sin_t


def make_core_inputs(x, weights, b, h, T_real=4096):
    core = {}
    s0 = h * T_real
    core["x_main"] = np.ascontiguousarray(x[b, s0:s0 + T_real])
    if h == 0:
        core["x_halo"] = np.zeros((HALO + 1, D), np.float32)
    else:
        core["x_halo"] = np.ascontiguousarray(x[b, s0 - HALO - 1:s0])
    core["xrow1"] = np.ascontiguousarray(x[b, 1:2])
    core["brow_a"] = np.ascontiguousarray(np.asarray(weights["L0_attn_a_b"],
                                                     np.float32).reshape(1, D))
    core["wta"] = _prep_wa(np.asarray(weights["L0_attn_a_W"], np.float32))
    core["wt_r"] = _prep_w(np.asarray(weights["L0_ffn_W_r"], np.float32))
    core["wt_kf"] = _prep_w(np.asarray(weights["L0_ffn_W_k"], np.float32))
    core["wt_vf"] = _prep_w(np.asarray(weights["L0_ffn_W_v"], np.float32))
    core["wt_q"] = _prep_w(np.asarray(weights["L0_attn_b_Wq"], np.float32))
    core["wt_k"] = _prep_w(np.asarray(weights["L0_attn_b_Wk"], np.float32))
    core["wt_v"] = _prep_w(np.asarray(weights["L0_attn_b_Wv"], np.float32))
    core["wt_o"] = _prep_w(np.asarray(weights["L0_attn_b_Wo"], np.float32))
    rs = lambda v: np.ascontiguousarray(np.asarray(v, np.float32).reshape(8, 128).T)
    core["mufm_r"] = rs(weights["L0_ffn_mu_r"])
    core["mufm_k"] = rs(weights["L0_ffn_mu_k"])
    core["bfm_q"] = rs(weights["L0_attn_b_bq"])
    core["bfm_k"] = rs(weights["L0_attn_b_bk"])
    core["bfm_v"] = rs(weights["L0_attn_b_bv"])
    core["bfm_o"] = rs(weights["L0_attn_b_bo"])
    core["ident"] = np.eye(128, dtype=np.float32)
    mf, mb = _masks()
    core["mask_first"] = mf
    core["mask_band"] = mb
    core["mask_blk2"] = mf if h == 0 else mb
    core["gate"] = np.full((128, 8), 0.0 if h == 0 else 1.0, np.float32)
    cos_t, sin_t = _tables(h, T_real)
    core["cos_t"] = cos_t
    core["sin_t"] = sin_t
    return core


def kernel(**inputs):
    x = np.asarray(inputs["L0_attn_a_x"], np.float32)
    B, S, _ = x.shape
    T_real = S // 2
    nc = build_program(T_real=T_real, ST_real=512)
    in_maps = [make_core_inputs(x, inputs, core // 2, core % 2, T_real)
               for core in range(8)]
    res = run_bass_kernel_spmd(nc, in_maps, core_ids=list(range(8)))
    out = np.empty((B, S, D), np.float32)
    for core in range(8):
        b, h = core // 2, core % 2
        out[b, h * T_real:(h + 1) * T_real] = res.results[core]["y"]
    return out


if __name__ == "__main__":
    import reference
    ins = {k: np.asarray(v) for k, v in reference.setup_inputs().items()}
    out = kernel(**ins)
    ref = np.asarray(reference.reference(**ins))
    print("rel:", np.linalg.norm(out - ref) / np.linalg.norm(ref))
